# version 28
# baseline (speedup 1.0000x reference)
import numpy as np
import ml_dtypes
import concourse.bass as bass
import concourse.mybir as mybir
import concourse.bass_utils as _bass_utils
from concourse.bass_utils import run_bass_kernel_spmd

if not getattr(_bass_utils, "_ldw_opt_patched", False):
    _orig_walrus_args = _bass_utils.get_walrus_args

    def _walrus_args_ldw(arch, tmpdir, **kw):
        return [a.replace("--enable-ldw-opt=false", "--enable-ldw-opt=true")
                for a in _orig_walrus_args(arch, tmpdir, **kw)]

    _bass_utils.get_walrus_args = _walrus_args_ldw
    _bass_utils._ldw_opt_patched = True

F32 = mybir.dt.float32
BF16 = mybir.dt.bfloat16
AF = mybir.ActivationFunctionType
AL = mybir.AluOpType
AX = mybir.AxisListType

D = 1024
NCH = 8
NH = 16
DH = 64
GT = 512
CT = 128
SWAP_MASK = [i ^ 1 for i in range(32)]


def _dtsz(dt):
    return 2 if dt == BF16 else 4


class Buf:
    __slots__ = ("name", "w", "readers", "excl")

    def __init__(self, name, excl=False):
        self.name = name
        self.w = None
        self.readers = []
        self.excl = excl


class Sched:
    ENGS = ("t", "v", "sc", "g", "sy")

    def __init__(self, nc):
        self.nc = nc
        self.q = {e: [] for e in self.ENGS}
        self.sems = {}
        for k in self.ENGS:
            self.sems[k] = nc.alloc_semaphore("S_" + k)
        self.cnt = {k: 0 for k in self.sems}
        self.waited = {e: {} for e in self.ENGS}
        self.ninstr = 0
        self.dma_bumps = 0

    def _need(self, eng, tok):
        key, val = tok
        if key.startswith("d:"):
            cur = self.cnt[key]
            if cur > val:
                self.dma_bumps += 1
                val = cur
        w = self.waited[eng]
        if val <= w.get(key, 0):
            return
        w[key] = val
        sem = self.sems[key]
        self.q[eng].append(lambda E, s=sem, v=val: E.wait_ge(s, v))

    def _deps(self, eng, reads, writes):
        toks = set()
        for b in reads:
            if b.w is not None:
                toks.add(b.w)
            if b.excl and False:
                for tok in b.readers:
                    if tok[0] != eng:
                        toks.add(tok)
        for b in writes:
            if b.w is not None:
                toks.add(b.w)
            toks.update(b.readers)
        for tok in sorted(toks):
            self._need(eng, tok)

    def _finish(self, tok, reads, writes):
        for b in writes:
            b.w = tok
            b.readers = []
        for b in reads:
            if b.w is not tok:
                b.readers.append(tok)

    def emit(self, eng, fns, reads=(), writes=()):
        if not isinstance(fns, (list, tuple)):
            fns = [fns]
        self._deps(eng, reads, writes)
        self.cnt[eng] += 1
        tok = (eng, self.cnt[eng])
        sem = self.sems[eng]
        n = len(fns)
        self.ninstr += n
        for i, f in enumerate(fns):
            if i == n - 1:
                self.q[eng].append(lambda E, f=f, s=sem: f(E).then_inc(s, 1))
            else:
                self.q[eng].append(lambda E, f=f: f(E))
        self._finish(tok, reads, writes)
        return tok

    def emit_chains(self, eng, chains, rbufs):
        wbufs = [c[0] for c in chains]
        self._deps(eng, rbufs, wbufs)
        sem = self.sems[eng]
        nsteps = max(len(c[1]) for c in chains)
        toks = []
        for i in range(nsteps):
            for (buf, quads) in chains:
                if i >= len(quads):
                    continue
                o, l, r = quads[i]
                st = (i == 0)
                sp = (i == len(quads) - 1)
                self.ninstr += 1
                if sp:
                    self.cnt[eng] += 1
                    tok = (eng, self.cnt[eng])
                    toks.append((buf, tok))
                    self.q[eng].append(
                        lambda E, o=o, l=l, r=r, a=st, s=sem:
                        E.matmul(o, l, r, start=a, stop=True,
                                 skip_group_check=True).then_inc(s, 1))
                else:
                    self.q[eng].append(
                        lambda E, o=o, l=l, r=r, a=st:
                        E.matmul(o, l, r, start=a, stop=False,
                                 skip_group_check=True))
        last = toks[-1][1]
        for buf, tok in toks:
            buf.w = tok
            buf.readers = []
        for b in rbufs:
            b.readers.append(last)

    def dma(self, qn, out_ap, in_ap, reads=(), writes=(), st="m"):
        key = f"d:{qn}:{st}"
        if key not in self.sems:
            self.sems[key] = self.nc.alloc_semaphore("S" + key)
            self.cnt[key] = 0
        self._deps(qn, reads, writes)
        self.cnt[key] += 16
        tok = (key, self.cnt[key])
        sem = self.sems[key]
        self.ninstr += 1
        self.q[qn].append(
            lambda E, o=out_ap, i=in_ap, s=sem: E.dma_start(out=o, in_=i).then_inc(s, 16))
        self._finish(tok, reads, writes)
        return tok

    def barrier(self):
        snap = dict(self.cnt)
        for e in self.ENGS:
            for key, val in snap.items():
                if val > 0:
                    self._need(e, (key, val))

    def finalize(self):
        for key, val in self.cnt.items():
            if val > 0:
                self._need("sy", (key, val))

    def replay(self):
        with self.nc.Block() as blk:
            blk.tensor(lambda E: [f(E) for f in self.q["t"]])
            blk.vector(lambda E: [f(E) for f in self.q["v"]])
            blk.scalar(lambda E: [f(E) for f in self.q["sc"]])
            blk.gpsimd(lambda E: [f(E) for f in self.q["g"]])
            blk.sync(lambda E: [f(E) for f in self.q["sy"]])


class SB:
    def __init__(self, nc):
        self.nc = nc
        self.off = nc.sbuf_base + 4096
        self.cap = nc.sbuf_top

    def alloc(self, name, shape, dtype):
        nbytes = int(np.prod(shape[1:])) * _dtsz(dtype)
        self.off = (self.off + 63) & ~63
        assert self.off + nbytes <= self.cap, \
            f"SBUF overflow at {name}: {self.off}+{nbytes} > {self.cap}"
        t = self.nc.alloc_sbuf_tensor_at(name, list(shape), dtype, offset=self.off)
        self.off += nbytes
        return t, Buf(name)

    def mark(self):
        return self.off

    def reset(self, mark):
        self.off = mark


class KB:
    def __init__(self, T, taps=()):
        assert T % 1024 == 0
        self.T = T
        self.taps = tuple(taps)
        self.nc = bass.Bass("TRN2", target_bir_lowering=False)
        self.S = Sched(self.nc)
        self.sb = SB(self.nc)
        self.dram = {}
        self.NG = T // GT

    def din(self, name, shape, dtype=F32):
        ap = self.nc.dram_tensor(name, list(shape), dtype, kind="ExternalInput").ap()
        self.dram[name] = (ap, Buf(name))
        return ap

    def dout(self, name, shape, dtype=F32):
        ap = self.nc.dram_tensor(name, list(shape), dtype, kind="ExternalOutput").ap()
        self.dram[name] = (ap, Buf(name))
        return ap

    def dscr(self, name, shape, dtype=BF16):
        ap = self.nc.dram_tensor(name, list(shape), dtype, kind="Internal").ap()
        self.dram[name] = (ap, Buf(name))
        return ap

    def db(self, name):
        return self.dram[name][1]

    def da(self, name):
        return self.dram[name][0]

    def mm(self, psum_ap, pairs, psum_buf, rbufs, start=True, stop=True, skip=False):
        fns = []
        n = len(pairs)
        for i, (lt, rh) in enumerate(pairs):
            st = start and (i == 0)
            sp = stop and (i == n - 1)
            fns.append(lambda E, o=psum_ap, l=lt, r=rh, a=st, b=sp, k=skip:
                       E.matmul(o, l, r, start=a, stop=b, skip_group_check=k))
        return self.S.emit("t", fns, reads=rbufs, writes=[psum_buf])

    def mmx(self, quads, rbufs, wbufs):
        fns = [lambda E, o=o, l=l, r=r, a=a, b=b:
               E.matmul(o, l, r, start=a, stop=b, skip_group_check=True)
               for (o, l, r, a, b) in quads]
        return self.S.emit("t", fns, reads=rbufs, writes=wbufs)

    def acp(self, out_ap, in_ap, rbufs, wbufs):
        return self.act(out_ap, in_ap, AF.Copy, rbufs, wbufs)

    def tr(self, psum_ap, in_ap, ident_ap, psum_buf, rbufs):
        return self.S.emit(
            "t", lambda E, o=psum_ap, i=in_ap, d=ident_ap: E.transpose(o, i, d),
            reads=rbufs, writes=[psum_buf])

    def cp(self, eng, out_ap, in_ap, rbufs, wbufs):
        return self.S.emit(eng, lambda E, o=out_ap, i=in_ap: E.tensor_copy(o, i),
                           reads=rbufs, writes=wbufs)

    def tt(self, eng, op, out_ap, a_ap, b_ap, rbufs, wbufs):
        return self.S.emit(
            eng, lambda E, o=out_ap, a=a_ap, b=b_ap, p=op:
            E.tensor_tensor(o, a, b, op=p), reads=rbufs, writes=wbufs)

    def ts(self, eng, out_ap, in_ap, s1, s2, op0, op1, rbufs, wbufs):
        return self.S.emit(
            eng, lambda E, o=out_ap, i=in_ap, a=s1, b=s2, p=op0, q=op1:
            E.tensor_scalar(o, i, a, b, p, q), reads=rbufs, writes=wbufs)

    def stt(self, eng, out_ap, in0_ap, scalar, in1_ap, op0, op1, rbufs, wbufs,
            accum=None):
        return self.S.emit(
            eng, lambda E, o=out_ap, i=in0_ap, s=scalar, j=in1_ap, p=op0, q=op1,
            a=accum: E.scalar_tensor_tensor(o, i, s, j, p, q, accum_out=a),
            reads=rbufs, writes=wbufs)

    def act(self, out_ap, in_ap, func, rbufs, wbufs, bias=0.0, scale=1.0, accum=None):
        return self.S.emit(
            "sc", lambda E, o=out_ap, i=in_ap, f=func, b=bias, s=scale, a=accum:
            E.activation(out=o, in_=i, func=f, bias=b, scale=s, accum_out=a),
            reads=rbufs, writes=wbufs)

    def red(self, out_ap, in_ap, op, rbufs, wbufs):
        return self.S.emit(
            "v", lambda E, o=out_ap, i=in_ap, p=op: E.tensor_reduce(o, i, AX.X, p),
            reads=rbufs, writes=wbufs)

    def recip(self, out_ap, in_ap, rbufs, wbufs):
        return self.S.emit(
            "v", lambda E, o=out_ap, i=in_ap: E.reciprocal(o, i),
            reads=rbufs, writes=wbufs)

    def shuf(self, out_ap, in_ap, rbufs, wbufs):
        return self.S.emit(
            "v", lambda E, o=out_ap, i=in_ap: E.stream_shuffle(o, i, SWAP_MASK),
            reads=rbufs, writes=wbufs)

    def memset(self, eng, ap, val, wbufs):
        return self.S.emit(eng, lambda E, a=ap, v=val: E.memset(a, v), writes=wbufs)

    def ttr(self, out_ap, in0_ap, in1_ap, accum_ap, rbufs, wbufs):
        return self.S.emit(
            "v", lambda E, o=out_ap, a=in0_ap, b=in1_ap, c=accum_ap:
            E.tensor_tensor_reduce(o, a, b, 1.0, 0.0, AL.mult, AL.add, c),
            reads=rbufs, writes=wbufs)

    def build(self):
        nc, S, sb, T = self.nc, self.S, self.sb, self.T
        NG = self.NG
        T2 = T // 2
        NT2 = T2 // GT

        x_in = self.din("x", [T, D])
        wdwp_d = self.din("wdwp", [128, 24])
        bvecs_d = self.din("bvecs", [128, 32])
        b1p_d = self.din("b1p", [128, 128])
        bd1p_d = self.din("bd1p", [128, 16])
        wpwT_d = self.din("wpwT", [D, D])
        wqkvT_d = self.din("wqkvT", [D, 3 * D])
        woT_d = self.din("woT", [D, D])
        wgT_d = self.din("wgT", [D, D])
        wmgT_d = self.din("wmgT", [D, 4])
        w1T_d = self.din("w1T", [4, D, 4 * D])
        w2T_d = self.din("w2T", [4, 4 * D, D])
        b2_d = self.din("b2", [4, D])
        Wbase_d = self.din("Wbase", [D, D])
        Wd1_d = self.din("Wd1", [8, D, 256])
        Wd2_d = self.din("Wd2", [8, 256, D])
        bd2_d = self.din("bd2", [8, D])
        WadF_d = self.din("WadF", [8, D * 64])
        bad_d = self.din("bad", [8, 64])
        WauF_d = self.din("WauF", [8, 64 * D])
        bau_d = self.din("bau", [8, D])
        identb_d = self.din("identb", [128, 128], BF16)
        identf_d = self.din("identf", [128, 128], F32)
        utmask_d = self.din("utmask", [128, 512], BF16)
        ropeC_d = self.din("ropeC", [128, T], BF16)
        ropeS_d = self.din("ropeS", [128, T], BF16)
        ones_d = self.din("onesb", [128, 512], BF16)
        out_d = self.dout("out", [T, D])

        xn_sc = self.dscr("xn_sc", [D, T])
        xp_sc = self.dscr("xp_sc", [D, T])
        xpp_sc = self.dscr("xpp_sc", [D, T])
        h_sc = self.dscr("h_sc", [D, T])
        s_sc = self.dscr("s_sc", [D, T])
        gw_sc = self.dscr("gw_sc", [4, T])
        wad_sc = self.dscr("wad_sc", [D * 64])
        wau_sc = self.dscr("wau_sc", [64 * D])
        lrow_sc = self.dscr("lrow_sc", [8], F32)
        gb_sc = self.dscr("gb_sc", [D], BF16)
        oh_sc = self.dscr("oh_sc", [8], BF16)
        for nm in self.taps:
            self.dout("tap_" + nm, [D, T], BF16)

        P = {}
        for i in range(4):
            P[f"B{i}"] = (nc.alloc_psum_tensor(f"PB{i}", [128, 512], F32),
                          Buf(f"PB{i}", excl=True))
        P["S0"] = (nc.alloc_psum_tensor("PS0", [128, 512], F32), Buf("PS0", excl=True))
        P["S1"] = (nc.alloc_psum_tensor("PS1", [128, 512], F32), Buf("PS1", excl=True))
        P["T0"] = (nc.alloc_psum_tensor("PT0", [128, 1024], BF16), Buf("PT0", excl=True))
        P["T1"] = (nc.alloc_psum_tensor("PT1", [128, 1024], BF16), Buf("PT1", excl=True))
        self.P = P

        idb, idb_b = sb.alloc("idb", [128, 128], BF16)
        idf, idf_b = sb.alloc("idf", [128, 128], F32)
        utm, utm_b = sb.alloc("utm", [128, 512], BF16)
        onb, onb_b = sb.alloc("onb", [128, 512], BF16)
        wdwc, wdwc_b = sb.alloc("wdwc", [128, 3 * NCH], F32)
        bg_n, bg_n_b = sb.alloc("bg_n", [128, NCH], F32)
        lngc, lngc_b = sb.alloc("lngc", [128, NCH], F32)
        lnbc, lnbc_b = sb.alloc("lnbc", [128, NCH], F32)
        b1c, b1c_b = sb.alloc("b1c", [128, 4 * 32], F32)
        bd1c, bd1c_b = sb.alloc("bd1c", [128, 16], F32)
        bd2r, bd2r_b = sb.alloc("bd2r", [1, 8 * D], BF16)
        b2r, b2r_b = sb.alloc("b2r", [4, D], BF16)
        bbc, bbc_b = sb.alloc("bbc", [128, NCH], F32)
        badr, badr_b = sb.alloc("badr", [8, 64], BF16)
        baur, baur_b = sb.alloc("baur", [8, D], BF16)
        gbias, gbias_b = sb.alloc("gbias", [128, NCH], F32)
        selv, selv_b = sb.alloc("selv", [128, 2], F32)
        bauc, bauc_b = sb.alloc("bauc", [128, NCH], F32)
        S_sb, S_sb_b = sb.alloc("S_sb", [128, 8 * 65], BF16)
        ohc, ohc_b = sb.alloc("ohc", [8, 2], BF16)
        kon, kon_b = sb.alloc("kon", [128, 4], F32)
        qkv_mark = sb.mark()
        wqv, wqv_b = sb.alloc("wq", [128, NCH * D], BF16)
        wkv, wkv_b = sb.alloc("wk", [128, NCH * D], BF16)
        wvv, wvv_b = sb.alloc("wv", [128, NCH * D], BF16)
        perm_mark = sb.mark()
        self.memset("v", kon[:, 0:1], 1e-5, [kon_b])
        self.memset("v", kon[:, 1:2], -0.5, [kon_b])
        self.memset("v", kon[:, 2:3], -1.0, [kon_b])
        self.memset("v", kon[:, 3:4], 1e-6, [kon_b])

        S.dma("sy", idb[:], identb_d, writes=[idb_b], st="pre")
        S.dma("sy", idf[:], identf_d, writes=[idf_b], st="pre")
        S.dma("sy", utm[:], utmask_d, writes=[utm_b], st="pre")
        S.dma("sy", onb[:], ones_d, writes=[onb_b], st="pre")
        S.dma("g", b2r[:], b2_d, writes=[b2r_b], st="pre")
        S.dma("g", bd2r[:], bd2_d.rearrange("e d -> (e d)").unsqueeze(0),
              writes=[bd2r_b], st="pre")
        S.dma("g", badr[:], bad_d, writes=[badr_b], st="pre")
        S.dma("g", baur[:], bau_d, writes=[baur_b], st="pre")
        S.dma("sy", wdwc[:], wdwp_d, writes=[wdwc_b], st="pre")
        S.dma("sy", bg_n[:], bvecs_d[:, 0:8], writes=[bg_n_b], st="pre")
        S.dma("sy", lngc[:], bvecs_d[:, 8:16], writes=[lngc_b], st="pre")
        S.dma("sy", lnbc[:], bvecs_d[:, 16:24], writes=[lnbc_b], st="pre")
        S.dma("sy", bbc[:], bvecs_d[:, 24:32], writes=[bbc_b], st="pre")
        S.dma("sy", b1c[:], b1p_d, writes=[b1c_b], st="pre")
        S.dma("sy", bd1c[:], bd1p_d, writes=[bd1c_b], st="pre")


        self.c = dict(idb=(idb, idb_b), idf=(idf, idf_b), utm=(utm, utm_b),
                      onb=(onb, onb_b), wdwc=(wdwc, wdwc_b), bg_n=(bg_n, bg_n_b),
                      lngc=(lngc, lngc_b), lnbc=(lnbc, lnbc_b), b1c=(b1c, b1c_b),
                      bd1c=(bd1c, bd1c_b), bd2r=(bd2r, bd2r_b), b2r=(b2r, b2r_b),
                      bbc=(bbc, bbc_b), badr=(badr, badr_b), baur=(baur, baur_b),
                      gbias=(gbias, gbias_b), selv=(selv, selv_b),
                      bauc=(bauc, bauc_b), S_sb=(S_sb, S_sb_b), ohc=(ohc, ohc_b),
                      kon=(kon, kon_b), wq=(wqv, wqv_b), wk=(wkv, wkv_b),
                      wv=(wvv, wvv_b))
        self.d = dict(x=x_in, wpwT=wpwT_d, wqkvT=wqkvT_d, woT=woT_d, wgT=wgT_d,
                      wmgT=wmgT_d, w1T=w1T_d, w2T=w2T_d, Wbase=Wbase_d,
                      Wd1=Wd1_d, Wd2=Wd2_d, WadF=WadF_d, WauF=WauF_d,
                      ropeC=ropeC_d, ropeS=ropeS_d, out=out_d,
                      xn_sc=xn_sc, xp_sc=xp_sc, xpp_sc=xpp_sc, h_sc=h_sc,
                      s_sc=s_sc, gw_sc=gw_sc, wad_sc=wad_sc, wau_sc=wau_sc,
                      lrow_sc=lrow_sc, oh_sc=oh_sc)

        self.phase_p0m1(perm_mark)
        S.barrier()
        self.phase_m2(perm_mark)
        S.barrier()
        self.phase_m3(qkv_mark)
        S.barrier()
        self.phase_ffn(qkv_mark)
        for nm in self.taps:
            src = self.d[nm + "_sc"]
            S.dma("sy", self.da("tap_" + nm), src, reads=[self.db(nm + "_sc")],
                  writes=[self.db("tap_" + nm)], st="tap")
        S.finalize()
        S.replay()
        return nc

    def phase_p0m1(self, mark):
        S, sb, P, T = self.S, self.sb, self.P, self.T
        NG = self.NG
        idb, idb_b = self.c["idb"]
        wdwc, wdwc_b = self.c["wdwc"]
        sb.reset(mark)
        xtm = [sb.alloc(f"xtm{i}", [128, 4 * D], F32) for i in range(2)]
        xcv2 = [[sb.alloc(f"xcv{p}_{c}", [128, 514], BF16) for c in range(NCH)]
                for p in range(2)]
        xout, xout_b = sb.alloc("xout", [128, 2 * GT], BF16)
        wpw, wpw_b = sb.alloc("wpw", [128, NCH * D], BF16)
        wpw3 = [sb.alloc(f"wpw3_{k}", [128, NCH * D], BF16) for k in range(3)]
        stats = [sb.alloc(f"stat{i}", [128, 16], F32) for i in range(4)]
        xnt = [sb.alloc(f"xnt{i}", [128, D], BF16) for i in range(4)]

        for ci in range(NCH):
            S.dma("g", wpw[:, ci * D:(ci + 1) * D],
                  self.d["wpwT"][ci * 128:(ci + 1) * 128, :], writes=[wpw_b],
                  st="wpw")
        for k in range(3):
            for ci in range(NCH):
                self.ts("v", wpw3[k][0][:, ci * D:(ci + 1) * D],
                        wpw[:, ci * D:(ci + 1) * D],
                        wdwc[:, k * NCH + ci: k * NCH + ci + 1], None,
                        AL.mult, AL.bypass, [wpw_b, wdwc_b], [wpw3[k][1]])
        for c in range(NCH):
            self.memset("v", xcv2[0][c][0][:, 0:2], 0.0, [xcv2[0][c][1]])

        for g in range(NG):
            t0 = g * GT
            xt, xt_b = xtm[g % 2]
            S.dma("g", xt[:].rearrange("p (s d) -> p s d", d=D),
                  self.d["x"][t0:t0 + GT, :].rearrange("(s p) d -> p s d", p=128),
                  writes=[xt_b], st="x%d" % (g % 2))
            for wi2, nm in enumerate(("wq", "wk", "wv")):
                wt_, wt_b = self.c[nm]
                S.dma("g", wt_[:, g * D:(g + 1) * D],
                      self.d["wqkvT"][g * 128:(g + 1) * 128,
                                      wi2 * D:(wi2 + 1) * D],
                      writes=[wt_b], st="wqkv")
            xcv = xcv2[g % 2]
            if g > 0:
                prev = xcv2[(g - 1) % 2]
                for c in range(NCH):
                    self.cp("v", xcv[c][0][:, 0:2], prev[c][0][:, 512:514],
                            [prev[c][1]], [xcv[c][1]])
            kon, kon_b = self.c["kon"]
            for s4 in range(4):
                xv = xt[:, s4 * D:(s4 + 1) * D]
                stat, stat_b = stats[s4]
                self.S.emit("v", lambda E, o=stat[:, 0:6], i=xv[:, 0:512]:
                            E.bn_stats(o, i), reads=[xt_b], writes=[stat_b])
                self.S.emit("v", lambda E, o=stat[:, 6:12], i=xv[:, 512:1024]:
                            E.bn_stats(o, i), reads=[xt_b], writes=[stat_b])
                self.S.emit("v", lambda E, o=stat[:, 12:14], i=stat[:, 0:12]:
                            E.bn_aggr(o, i), reads=[stat_b], writes=[stat_b])
                self.act(stat[:, 14:15], stat[:, 13:14], AF.Ln, [stat_b, kon_b],
                         [stat_b], bias=kon[:, 0:1])
                self.act(stat[:, 14:15], stat[:, 14:15], AF.Exp, [stat_b, kon_b],
                         [stat_b], scale=kon[:, 1:2])
                xn, xn_b = xnt[s4]
                self.ts("v", xn[:], xv, stat[:, 12:13], stat[:, 14:15],
                        AL.subtract, AL.mult, [xt_b, stat_b], [xn_b])
            for s4 in range(4):
                xn, xn_b = xnt[s4]
                for c in range(NCH):
                    pt, ptb = P["T%d" % ((s4 * NCH + c) % 2)]
                    self.tr(pt[:, 0:128], xn[:, c * 128:(c + 1) * 128], idb[:],
                            ptb, [xn_b, idb_b])
                    dstv = xcv[c][0][:, 2 + s4 * 128: 2 + (s4 + 1) * 128]
                    if c % 2 == 0:
                        self.acp(dstv, pt[:, 0:128], [ptb], [xcv[c][1]])
                    else:
                        self.cp("v", dstv, pt[:, 0:128], [ptb], [xcv[c][1]])
            for c in range(NCH):
                S.dma("sy", self.d["xn_sc"][c * 128:(c + 1) * 128, t0:t0 + GT],
                      xcv[c][0][:, 2:514], reads=[xcv[c][1]],
                      writes=[self.db("xn_sc")], st="xnw")
            for co in range(NCH):
                pb, pbb = P["B%d" % (co % 2)]
                pairs = []
                for k, sh in ((2, 2), (1, 1), (0, 0)):
                    wv, wv_b = wpw3[k]
                    for ci in range(NCH):
                        pairs.append(
                            (wv[:, ci * D + co * 128: ci * D + (co + 1) * 128],
                             xcv[ci][0][:, sh: sh + 512]))
                self.mm(pb[:], pairs, pbb,
                        [wpw3[0][1], wpw3[1][1], wpw3[2][1]] +
                        [xcv[ci][1] for ci in range(NCH)])
                xo = xout[:, (co % 2) * GT:(co % 2 + 1) * GT]
                self.tt("v", AL.add, xo, pb[:], xcv[co][0][:, 2:514],
                        [pbb, xcv[co][1]], [xout_b])
                S.dma("sy", self.d["xp_sc"][co * 128:(co + 1) * 128, t0:t0 + GT],
                      xo, reads=[xout_b], writes=[self.db("xp_sc")], st="xpw")

    def phase_m2(self, mark):
        S, sb, P, T = self.S, self.sb, self.P, self.T
        NG = self.NG
        idb, idb_b = self.c["idb"]
        utm, utm_b = self.c["utm"]
        onb, onb_b = self.c["onb"]
        bg_n, bg_n_b = self.c["bg_n"]
        lngc, lngc_b = self.c["lngc"]
        lnbc, lnbc_b = self.c["lnbc"]
        S_sb, S_sb_b = self.c["S_sb"]
        sb.reset(mark)
        wq, wq_b = self.c["wq"]
        wk, wk_b = self.c["wk"]
        wv, wv_b = self.c["wv"]
        wo, wo_b = sb.alloc("wo", [128, NCH * D], BF16)
        wg, wg_b = sb.alloc("wg", [128, NCH * D], BF16)
        wmg, wmg_b = sb.alloc("wmg", [128, NCH * 4], BF16)
        rcg = [sb.alloc(f"rcg{i}", [128, GT], BF16) for i in range(2)]
        rsg = [sb.alloc(f"rsg{i}", [128, GT], BF16) for i in range(2)]
        xg = [sb.alloc(f"xg{i}", [128, NCH * 520], BF16) for i in range(2)]
        qp, qp_b = sb.alloc("qp", [128, NCH * GT], BF16)
        kp, kp_b = sb.alloc("kp", [128, NCH * GT], BF16)
        vaug, vaug_b = sb.alloc("vaug", [128, 4 * 1048], BF16)
        ktm, ktm_b = sb.alloc("ktm", [128, 1024], BF16)
        atb = [sb.alloc(f"atb{i}", [128, 512], BF16) for i in range(4)]
        ndsb = [sb.alloc(f"ndsb{i}", [128, 512], BF16) for i in range(4)]
        att, att_b = sb.alloc("attg", [128, NCH * GT], BF16)
        xpt, xpt_b = sb.alloc("xppg", [128, NCH * GT], BF16)
        hgt, hgt_b = att, att_b
        rop = [sb.alloc(f"rop{i}", [128, GT], F32) for i in range(6)]
        gat = [sb.alloc(f"gat{i}", [128, GT], BF16) for i in range(2)]
        stt2, stt2_b = sb.alloc("stt2", [128, 1024], F32)
        stbf, stbf_b = sb.alloc("stbf", [128, 1024], BF16)
        sqbs = [sb.alloc(f"sqb{i}", [128, GT], BF16) for i in range(2)]
        gwt, gwt_b = sb.alloc("gwt", [128, 8], BF16)
        gtmp, gtmp_b = sb.alloc("gtmp", [4, 128], BF16)

        for ci in range(NCH):
            r = slice(ci * 128, (ci + 1) * 128)
            cd = slice(ci * D, (ci + 1) * D)
            S.dma("g", wo[:, cd], self.d["woT"][r, :], writes=[wo_b], st="wm2")
            S.dma("g", wg[:, cd], self.d["wgT"][r, :], writes=[wg_b], st="wm2")
            S.dma("g", wmg[:, ci * 4:(ci + 1) * 4], self.d["wmgT"][r, :],
                  writes=[wmg_b], st="wm2")

        for k4 in range(4):
            for hh in range(NH):
                o = k4 * 1048 + hh * 65 + 64
                self.memset("v", vaug[:, o:o + 1], 1.0, [vaug_b])
        self.memset("v", S_sb[:], 0.0, [S_sb_b])

        pend = []
        for g in range(NG):
            t0 = g * GT
            xgt, xgt_b = xg[g % 2]
            rc, rc_b = rcg[g % 2]
            rs, rs_b = rsg[g % 2]
            S.dma("g", rc[:], self.d["ropeC"][:, t0:t0 + GT], writes=[rc_b],
                  st="rope%d" % (g % 2))
            S.dma("g", rs[:], self.d["ropeS"][:, t0:t0 + GT], writes=[rs_b],
                  st="rope%d" % (g % 2))
            for c in range(NCH):
                base = c * 520
                r = slice(c * 128, (c + 1) * 128)
                if g == 0:
                    S.dma("g", xgt[:, base + 1: base + 513],
                          self.d["xp_sc"][r, 0:GT], reads=[self.db("xp_sc")],
                          writes=[xgt_b], st="xg0")
                    self.cp("v", xgt[:, base:base + 1],
                            xgt[:, base + 1:base + 2], [xgt_b], [xgt_b])
                else:
                    S.dma("g", xgt[:, base: base + 513],
                          self.d["xp_sc"][r, t0 - 1:t0 + GT],
                          reads=[self.db("xp_sc")], writes=[xgt_b],
                          st="xg%d" % (g % 2))

            for wi, (wb_, wbuf, dst, dstb) in enumerate(
                    ((wq, wq_b, qp, qp_b), (wk, wk_b, kp, kp_b))):
                for co in range(NCH):
                    t6 = (wi * NCH + co) % 2
                    c0, c0b = rop[3 * t6]
                    r0, r0b = rop[3 * t6 + 1]
                    r1, r1b = rop[3 * t6 + 2]
                    pb, pbb = P["B%d" % (co % 4)]
                    pairs = [(wb_[:, ci * D + co * 128: ci * D + (co + 1) * 128],
                              xgt[:, ci * 520 + 1: ci * 520 + 513])
                             for ci in range(NCH)]
                    self.mm(pb[:], pairs, pbb, [wbuf, xgt_b])
                    self.cp("v", c0[:], pb[:], [pbb], [c0b])
                    Cs = rc[:]
                    Ss = rs[:]
                    self.shuf(r0[:], c0[:], [c0b], [r0b])
                    self.tt("g", AL.mult, r1[:], c0[:], Cs, [c0b, rc_b], [r1b])
                    self.tt("g", AL.mult, r0[:], r0[:], Ss, [r0b, rs_b], [r0b])
                    self.tt("v", AL.add, r1[:], r1[:], r0[:], [r0b, r1b], [r1b])
                    self.act(r0[:], r1[:], AF.Exp, [r1b], [r0b])
                    self.S.emit("v", lambda E, o=r0[:], i=r0[:]:
                                E.tensor_scalar_min(o, i, 1.0),
                                reads=[r0b], writes=[r0b])
                    self.stt("v", dst[:, co * GT:(co + 1) * GT], r1[:], 1.0,
                             r0[:], AL.add, AL.max, [r0b, r1b], [dstb])

            for fn in pend:
                fn()
            pend.clear()
            for k4 in range(4):
                for hf in range(2):
                    pb, pbb = P["B%d" % (3 - hf)]
                    pairs = [(xgt[:, ci * 520 + 1 + k4 * 128:
                                  ci * 520 + 1 + (k4 + 1) * 128],
                              wv[:, ci * D + hf * 512: ci * D + (hf + 1) * 512])
                             for ci in range(NCH)]
                    self.mm(pb[:], pairs, pbb, [wv_b, xgt_b])
                    ov = vaug[:, k4 * 1048 + hf * 8 * 65: k4 * 1048 + hf * 8 * 65 + 8 * 65] \
                        .rearrange("p (h y) -> p h y", y=65)[:, :, 0:64]
                    iv = pb[:].rearrange("p (h y) -> p h y", y=64)
                    self.acp(ov, iv, [pbb], [vaug_b])

            for k4 in range(4):
                kk = g * 4 + k4
                vb = k4 * 1048
                ptr, ptrb = P["T%d" % (kk % 2)]
                for c in range(NCH):
                    self.tr(ptr[:, c * 128:(c + 1) * 128],
                            kp[:, c * GT + k4 * 128: c * GT + (k4 + 1) * 128],
                            idb[:], ptrb, [kp_b, idb_b])
                self.acp(ktm[:], ptr[:], [ptrb], [ktm_b])
                waves = []
                for w in range(4):
                    ws = [w * 4 + hh for hh in range(4)]
                    tsls = [slice((h // 2) * GT + k4 * 128,
                                  (h // 2) * GT + (k4 + 1) * 128) for h in ws]
                    p0s = [(h % 2) * 64 for h in ws]
                    waves.append((ws, tsls, p0s))
                for w, (ws, tsls, p0s) in enumerate(waves):
                    atp, atpb = P["B%d" % w]
                    for hh in range(4):
                        self.mm(atp[:, hh * 128:(hh + 1) * 128],
                                [(kp[p0s[hh]:p0s[hh] + 64, tsls[hh]],
                                  qp[p0s[hh]:p0s[hh] + 64, tsls[hh]])],
                                atpb, [qp_b, kp_b], skip=True)
                    ab4, ab4b = atb[w]
                    self.tt("v", AL.mult, ab4[:], atp[:], utm[:],
                            [atpb, utm_b], [ab4b])
                for w, (ws, tsls, p0s) in enumerate(waves):
                    sp, spb = P["S%d" % (w // 2)]
                    for hh in range(4):
                        h = ws[hh]
                        self.mm(sp[p0s[hh]:p0s[hh] + 64,
                                   ((h // 2) % 4) * 65:((h // 2) % 4) * 65 + 65],
                                [(ktm[:, h * 64:(h + 1) * 64],
                                  vaug[:, vb + h * 65: vb + h * 65 + 65])],
                                spb, [ktm_b, vaug_b], start=(kk == 0),
                                stop=True, skip=True)
                kon, kon_b = self.c["kon"]
                for w, (ws, tsls, p0s) in enumerate(waves):
                    nd, ndb = P["B%d" % w]
                    ab4, ab4b = atb[w]
                    for hh in range(4):
                        h = ws[hh]
                        col = hh * 128
                        self.mm(nd[0:65, col:col + 128],
                                [(vaug[:, vb + h * 65: vb + h * 65 + 65],
                                  ab4[:, col:col + 128])],
                                ndb, [vaug_b, ab4b], start=True, stop=False,
                                skip=True)
                        self.mm(nd[0:65, col:col + 128],
                                [(S_sb[p0s[hh]:p0s[hh] + 64,
                                       (h // 2) * 65:(h // 2) * 65 + 65],
                                  qp[p0s[hh]:p0s[hh] + 64, tsls[hh]])],
                                ndb, [S_sb_b, qp_b], start=False, stop=True,
                                skip=True)
                    nsb, nsbb = ndsb[w]
                    self.acp(nsb[0:64, :], nd[0:64, :], [ndb], [nsbb])
                    self.act(nsb[64:65, :], nd[64:65, :], AF.Ln,
                             [ndb, kon_b], [nsbb], bias=kon[64:65, 3:4])
                    self.act(nsb[64:65, :], nsb[64:65, :], AF.Exp,
                             [nsbb, kon_b], [nsbb], scale=kon[64:65, 2:3])
                for w, (ws, tsls, p0s) in enumerate(waves):
                    nd, ndb = P["B%d" % w]
                    nsb, nsbb = ndsb[w]
                    self.mm(nd[0:64, :], [(onb[64:65, 0:64], nsb[64:65, :])],
                            ndb, [onb_b, nsbb])
                    for hh in range(4):
                        h = ws[hh]
                        col = hh * 128
                        self.tt("v", AL.mult,
                                att[p0s[hh]:p0s[hh] + 64, tsls[hh]],
                                nsb[0:64, col:col + 128], nd[0:64, col:col + 128],
                                [nsbb, ndb], [att_b])
                for si in range(2):
                    sp, spb = P["S%d" % si]
                    self.acp(S_sb[:, si * 260:(si + 1) * 260], sp[:, 0:260],
                             [spb], [S_sb_b])
            delg, delg_b = qp, qp_b
            for c in range(NCH):
                self.tt("g" if c % 2 else "v", AL.subtract,
                        delg[:, c * GT:(c + 1) * GT],
                        xgt[:, c * 520 + 1: c * 520 + 513],
                        xgt[:, c * 520: c * 520 + 512], [xgt_b], [delg_b])
            for co in range(NCH):
                po, pob = P["B%d" % (co % 2)]
                pairs = [(wo[:, ci * D + co * 128: ci * D + (co + 1) * 128],
                          att[:, ci * GT:(ci + 1) * GT]) for ci in range(NCH)]
                self.mm(po[:], pairs, pob, [wo_b, att_b])
                pg, pgb = P["B%d" % (2 + co % 2)]
                pairs = [(wg[:, ci * D + co * 128: ci * D + (co + 1) * 128],
                          delg[:, ci * GT:(ci + 1) * GT]) for ci in range(NCH)]
                self.mm(pg[:], pairs, pgb, [wg_b, delg_b])
                ga, gab = gat[co % 2]
                self.act(ga[:], pg[:], AF.Sigmoid, [pgb, bg_n_b], [gab],
                         bias=bg_n[:, co:co + 1])
                self.tt("v", AL.mult, ga[:], ga[:], po[:], [gab, pob], [gab])
                self.tt("g", AL.add, xpt[:, co * GT:(co + 1) * GT], ga[:],
                        xgt[:, co * 520 + 1: co * 520 + 513], [gab, xgt_b],
                        [xpt_b])
                S.dma("sy", self.d["xpp_sc"][co * 128:(co + 1) * 128, t0:t0 + GT],
                      xpt[:, co * GT:(co + 1) * GT], reads=[xpt_b],
                      writes=[self.db("xpp_sc")], st="xppw")
            pA, pAb = P["B0"]
            pB, pBb = P["B1"]
            for c in range(NCH):
                sqb, sqb_b = sqbs[c % 2]
                self.act(sqb[:], xpt[:, c * GT:(c + 1) * GT], AF.Square,
                         [xpt_b], [sqb_b])
                self.mm(pB[0:1, :], [(onb[:, 0:1], sqb[:])], pBb,
                        [onb_b, sqb_b], start=(c == 0), stop=(c == NCH - 1),
                        skip=True)
            pairs = [(onb[:, 0:1], xpt[:, c * GT:(c + 1) * GT])
                     for c in range(NCH)]
            self.mm(pA[0:1, :], pairs, pAb, [onb_b, xpt_b])
            self.ts("v", stt2[0:1, 0:512], pA[0:1, :], 1.0 / D, None, AL.mult,
                    AL.bypass, [pAb], [stt2_b])
            self.ts("v", stt2[0:1, 512:1024], pB[0:1, :], 1.0 / D, None,
                    AL.mult, AL.bypass, [pBb], [stt2_b])
            self.tt("v", AL.mult, sqbs[0][0][0:1, 0:512], stt2[0:1, 0:512],
                    stt2[0:1, 0:512], [stt2_b], [sqbs[0][1]])
            self.tt("v", AL.subtract, stt2[0:1, 512:1024], stt2[0:1, 512:1024],
                    sqbs[0][0][0:1, 0:512], [stt2_b, sqbs[0][1]], [stt2_b])
            kon, kon_b = self.c["kon"]
            self.act(stt2[0:1, 512:1024], stt2[0:1, 512:1024], AF.Ln,
                     [stt2_b, kon_b], [stt2_b], bias=kon[0:1, 0:1])
            self.act(stt2[0:1, 512:1024], stt2[0:1, 512:1024], AF.Exp,
                     [stt2_b, kon_b], [stt2_b], scale=kon[0:1, 1:2])
            self.cp("v", stbf[0:1, 0:512], stt2[0:1, 0:512], [stt2_b], [stbf_b])
            self.cp("v", stbf[0:1, 512:1024], stt2[0:1, 512:1024], [stt2_b],
                    [stbf_b])
            TA, TAb = P["T0"][0][:].bitcast(F32), P["T0"][1]
            TB, TBb = P["T1"][0][:].bitcast(F32), P["T1"][1]
            self.mm(TA[:, 0:512], [(onb[0:1, 0:128], stbf[0:1, 0:512])], TAb,
                    [onb_b, stbf_b])
            self.mm(TB[:, 0:512], [(onb[0:1, 0:128], stbf[0:1, 512:1024])], TBb,
                    [onb_b, stbf_b])
            for c in range(NCH):
                r0, r0b = rop[c % 2]
                self.tt("v", AL.subtract, r0[:], xpt[:, c * GT:(c + 1) * GT],
                        TA[:, 0:512], [xpt_b, TAb], [r0b])
                self.tt("v", AL.mult, r0[:], r0[:], TB[:, 0:512], [r0b, TBb], [r0b])
                self.ts("v", hgt[:, c * GT:(c + 1) * GT], r0[:],
                        lngc[:, c:c + 1], lnbc[:, c:c + 1], AL.mult, AL.add,
                        [r0b, lngc_b, lnbc_b], [hgt_b])
                S.dma("sy", self.d["h_sc"][c * 128:(c + 1) * 128, t0:t0 + GT],
                      hgt[:, c * GT:(c + 1) * GT], reads=[hgt_b],
                      writes=[self.db("h_sc")], st="hw")
            pend.append(lambda t0=t0: self._moe_gate(
                t0, hgt, hgt_b, wmg, wmg_b, stt2, stt2_b, gwt, gwt_b,
                gtmp, gtmp_b, idb, idb_b))
        for fn in pend:
            fn()
        pend.clear()

    def _moe_gate(self, t0, hgt, hgt_b, wmg, wmg_b, stt2, stt2_b,
                  gwt, gwt_b, gtmp, gtmp_b, idb, idb_b):
        S, P = self.S, self.P
        for ts4 in range(4):
            o = 16 * ts4
            pb, pbb = P["B%d" % (2 + ts4 % 2)]
            pairs = [(hgt[:, ci * GT + ts4 * 128: ci * GT + (ts4 + 1) * 128],
                      wmg[:, ci * 4:(ci + 1) * 4]) for ci in range(NCH)]
            self.mm(pb[:, 0:4], pairs, pbb, [hgt_b, wmg_b])
            self.red(stt2[:, o:o + 1], pb[:, 0:4], AL.max, [pbb], [stt2_b])
            self.ts("v", stt2[:, o + 4:o + 8], pb[:, 0:4], stt2[:, o:o + 1],
                    None, AL.subtract, AL.bypass, [pbb, stt2_b], [stt2_b])
            self.act(stt2[:, o + 8:o + 12], stt2[:, o + 4:o + 8], AF.Exp,
                     [stt2_b], [stt2_b])
            self.red(stt2[:, o + 1:o + 2], stt2[:, o + 8:o + 12], AL.add,
                     [stt2_b], [stt2_b])
            self.recip(stt2[:, o + 2:o + 3], stt2[:, o + 1:o + 2], [stt2_b],
                       [stt2_b])
            self.ts("v", gwt[:, 4 * (ts4 % 2): 4 * (ts4 % 2) + 4],
                    stt2[:, o + 8:o + 12], stt2[:, o + 2:o + 3], None,
                    AL.mult, AL.bypass, [stt2_b], [gwt_b])
            pt, ptb = P["T%d" % (ts4 % 2)]
            self.tr(pt[0:4, 0:128], gwt[:, 4 * (ts4 % 2): 4 * (ts4 % 2) + 4],
                    idb[:], ptb, [gwt_b, idb_b])
            self.cp("v", gtmp[:], pt[0:4, 0:128], [ptb], [gtmp_b])
            S.dma("sy", self.d["gw_sc"][0:4, t0 + ts4 * 128: t0 + (ts4 + 1) * 128],
                  gtmp[:], reads=[gtmp_b], writes=[self.db("gw_sc")], st="gww")

    def phase_m3(self, mark):
        S, sb, P, T = self.S, self.sb, self.P, self.T
        T2 = T // 2
        NT2 = T2 // GT
        onb, onb_b = self.c["onb"]
        b1c, b1c_b = self.c["b1c"]
        b2r, b2r_b = self.c["b2r"]
        sb.reset(mark)
        hh_, hh_b = sb.alloc("hhalf", [128, NCH * T2], BF16)
        acc, acc_b = sb.alloc("m3acc", [128, NCH * T2], F32)
        hidG, hidG_b = sb.alloc("hidG", [128, 8 * T2], BF16)
        w1b_, w1b_bb = sb.alloc("w1b", [128, 8 * D], BF16)
        w2b, w2b_b = sb.alloc("w2b", [128, 8 * D], BF16)
        gwb, gwb_b = sb.alloc("gwb", [128, T2], BF16)
        gw4, gw4_b = sb.alloc("gw4", [4, T2], BF16)
        gwr, gwr_b = sb.alloc("gwr", [1, 512], BF16)
        sil = [sb.alloc(f"sil{i}", [128, 512], F32) for i in range(2)]
        xts, xts_b = sb.alloc("xts", [128, 512], BF16)
        sts, sts_b = sb.alloc("sts", [128, 512], BF16)

        for hf in range(2):
            toff = hf * T2
            for c in range(NCH):
                S.dma("g", hh_[:, c * T2:(c + 1) * T2],
                      self.d["h_sc"][c * 128:(c + 1) * 128, toff:toff + T2],
                      reads=[self.db("h_sc")], writes=[hh_b], st="hr")
            S.dma("sy", gw4[:], self.d["gw_sc"][0:4, toff:toff + T2],
                  reads=[self.db("gw_sc")], writes=[gw4_b], st="gwr")
            for e in range(4):
                for tt in range(NT2):
                    S.dma("sy", gwr[:], self.d["gw_sc"][e, toff + tt * GT:
                                                        toff + (tt + 1) * GT]
                          .unsqueeze(0), reads=[self.db("gw_sc")], writes=[gwr_b], st="gwrow")
                    pb, pbb = P["B3"]
                    self.mm(pb[:], [(onb[0:1, 0:128], gwr[:])], pbb,
                            [onb_b, gwr_b])
                    self.cp("v", gwb[:, tt * GT:(tt + 1) * GT], pb[:], [pbb],
                            [gwb_b])
                T0f = P["T0"][0][:].bitcast(F32)
                T1f = P["T1"][0][:].bitcast(F32)
                psW = [(P["S0"][0][:], P["S0"][1]), (P["S1"][0][:], P["S1"][1]),
                       (T0f, P["T0"][1]), (T1f, P["T1"][1])]
                for jg in range(4):
                    wb_, wb_b = w1b_, w1b_bb
                    f0 = jg * 8
                    for ci in range(NCH):
                        S.dma("g", wb_[:, ci * D:(ci + 1) * D],
                              self.d["w1T"][e, ci * 128:(ci + 1) * 128,
                                            f0 * 128:(f0 + 8) * 128],
                              writes=[wb_b], st="w1")
                    for j in range(8):
                        for tt in range(NT2):
                            pb, pbb = P["B%d" % tt]
                            pairs = [(wb_[:, ci * D + j * 128: ci * D + (j + 1) * 128],
                                      hh_[:, ci * T2 + tt * GT: ci * T2 + (tt + 1) * GT])
                                     for ci in range(NCH)]
                            self.mm(pb[:], pairs, pbb, [wb_b, hh_b])
                            sl, sl_b = sil[tt % 2]
                            self.act(sl[:], pb[:], AF.Silu, [pbb, b1c_b],
                                     [sl_b], bias=b1c[:, e * 32 + f0 + j:
                                                      e * 32 + f0 + j + 1])
                            self.tt("v", AL.mult,
                                    hidG[:, j * T2 + tt * GT: j * T2 + (tt + 1) * GT],
                                    sl[:], gwb[:, tt * GT:(tt + 1) * GT],
                                    [sl_b, gwb_b], [hidG_b])
                    for j in range(8):
                        S.dma("g", w2b[:, j * D:(j + 1) * D],
                              self.d["w2T"][e, (f0 + j) * 128:(f0 + j + 1) * 128, :],
                              writes=[w2b_b], st="w2")
                    for co in range(NCH):
                        bias_here = (e == 0 and jg == 0)
                        for tt in range(NT2):
                            pairs = [(w2b[:, j * D + co * 128: j * D + (co + 1) * 128],
                                      hidG[:, j * T2 + tt * GT: j * T2 + (tt + 1) * GT])
                                     for j in range(8)]
                            rb = [w2b_b, hidG_b]
                            if bias_here:
                                pairs.append((b2r[0:4, co * 128:(co + 1) * 128],
                                              gw4[0:4, tt * GT:(tt + 1) * GT]))
                                rb += [b2r_b, gw4_b]
                            self.mm(psW[tt][0], pairs, psW[tt][1], rb)
                            asl = acc[:, co * T2 + tt * GT: co * T2 + (tt + 1) * GT]
                            if bias_here:
                                self.cp("v", asl, psW[tt][0], [psW[tt][1]],
                                        [acc_b])
                            else:
                                self.tt("v", AL.add, asl, asl, psW[tt][0],
                                        [acc_b, psW[tt][1]], [acc_b])
                            if e == 3 and jg == 3:
                                r = slice(co * 128, (co + 1) * 128)
                                tsl = slice(toff + tt * GT, toff + (tt + 1) * GT)
                                S.dma("g", xts[:], self.d["xpp_sc"][r, tsl],
                                      reads=[self.db("xpp_sc")], writes=[xts_b],
                                      st="xppr")
                                self.tt("v", AL.add, sts[:], asl, xts[:],
                                        [acc_b, xts_b], [sts_b])
                                S.dma("sy", self.d["s_sc"][r, tsl], sts[:],
                                      reads=[sts_b], writes=[self.db("s_sc")],
                                      st="sw")


    def phase_ffn(self, mark):
        S, sb, P, T = self.S, self.sb, self.P, self.T
        NT = T // GT
        onb, onb_b = self.c["onb"]
        idb, idb_b = self.c["idb"]
        idf, idf_b = self.c["idf"]
        bd1c, bd1c_b = self.c["bd1c"]
        bd2r, bd2r_b = self.c["bd2r"]
        bbc, bbc_b = self.c["bbc"]
        badr, badr_b = self.c["badr"]
        baur, baur_b = self.c["baur"]
        gbias, gbias_b = self.c["gbias"]
        selv, selv_b = self.c["selv"]
        bauc, bauc_b = self.c["bauc"]
        sb.reset(mark)
        xnb, xnb_b = sb.alloc("xnb", [128, NCH * T], BF16)
        wbase, wbase_b = sb.alloc("wbase", [128, NCH * D], BF16)
        wd1b = [sb.alloc(f"wd1b{i}", [128, NCH * 256], BF16) for i in range(2)]
        wd2b = [sb.alloc(f"wd2b{i}", [128, 2 * D], BF16) for i in range(2)]
        hid2, hid2_b = sb.alloc("hid2", [128, 2 * GT], BF16)
        jkf, jkf_b = sb.alloc("jkf", [128, GT], F32)
        wadeb, wadeb_b = sb.alloc("wadeb", [128, NCH * 64], BF16)
        waueb, waueb_b = sb.alloc("waueb", [64, D], BF16)
        af, af_b = sb.alloc("af", [64, T], BF16)
        gbrow, gbrow_b = sb.alloc("gbrow", [1, D], BF16)
        ohc, ohc_b = self.c["ohc"]
        sel_mark = sb.mark()
        lrx, lrx_b = sb.alloc("lrx", [128, NT * NCH], F32)
        lrr, lrr_b = sb.alloc("lrr", [128, NT * NCH], F32)
        LRX, LRX_b = sb.alloc("LRX", [128, 8], F32)
        LRR, LRR_b = sb.alloc("LRR", [128, 8], F32)
        LL, LL_b = sb.alloc("LL", [128, 8], F32)
        llt, llt_b = sb.alloc("llt", [8, 128], F32)
        lc8, lc8_b = sb.alloc("lc8", [8, 4], F32)
        lrw, lrw_b = sb.alloc("lrw", [1, 16], F32)
        ohr, ohr_b = sb.alloc("ohr", [1, 8], BF16)
        selb = [sb.alloc(f"selb{i}", [8, 4096], BF16) for i in range(2)]
        web, web_b = sb.alloc("web", [1, 4096], BF16)

        for c in range(NCH):
            S.dma("g", xnb[:, c * T:(c + 1) * T],
                  self.d["xn_sc"][c * 128:(c + 1) * 128, :],
                  reads=[self.db("xn_sc")], writes=[xnb_b], st="xnr")
            S.dma("g", wbase[:, c * D:(c + 1) * D],
                  self.d["Wbase"][c * 128:(c + 1) * 128, :], writes=[wbase_b],
                  st="wb")

        for e in range(8):
            wd1, wd1_b = wd1b[e % 2]
            wd2, wd2_b = wd2b[e % 2]
            for ci in range(NCH):
                S.dma("g", wd1[:, ci * 256:(ci + 1) * 256],
                      self.d["Wd1"][e, ci * 128:(ci + 1) * 128, :],
                      writes=[wd1_b], st="wd%d" % (e % 2))
            for f in range(2):
                S.dma("g", wd2[:, f * D:(f + 1) * D],
                      self.d["Wd2"][e, f * 128:(f + 1) * 128, :], writes=[wd2_b],
                      st="wd%d" % (e % 2))
            for tt in range(NT):
                tsl = slice(tt * GT, (tt + 1) * GT)
                for f in range(2):
                    pb, pbb = P["B%d" % f]
                    pairs = [(wd1[:, ci * 256 + f * 128: ci * 256 + (f + 1) * 128],
                              xnb[:, ci * T + tt * GT: ci * T + (tt + 1) * GT])
                             for ci in range(NCH)]
                    self.mm(pb[:], pairs, pbb, [wd1_b, xnb_b])
                    self.act(hid2[:, f * GT:(f + 1) * GT], pb[:], AF.Relu,
                             [pbb, bd1c_b], [hid2_b],
                             bias=bd1c[:, e * 2 + f: e * 2 + f + 1])
                for co in range(NCH):
                    pb, pbb = P["B%d" % (2 + co % 2)]
                    pairs = [(wd2[:, f * D + co * 128: f * D + (co + 1) * 128],
                              hid2[:, f * GT:(f + 1) * GT]) for f in range(2)]
                    pairs.append((bd2r[0:1, e * D + co * 128: e * D + (co + 1) * 128],
                                  onb[0:1, :]))
                    self.mm(pb[:], pairs, pbb, [wd2_b, hid2_b, bd2r_b, onb_b])
                    idx = tt * NCH + co
                    self.stt("v", jkf[:], pb[:], 1.0,
                             xnb[:, co * T + tt * GT: co * T + (tt + 1) * GT],
                             AL.mult, AL.mult, [pbb, xnb_b], [jkf_b, lrx_b],
                             accum=lrx[:, idx:idx + 1])
                    self.act(jkf[:], pb[:], AF.Square, [pbb], [jkf_b, lrr_b],
                             accum=lrr[:, idx:idx + 1])
            self.red(LRX[:, e:e + 1], lrx[:], AL.add, [lrx_b], [LRX_b])
            self.red(LRR[:, e:e + 1], lrr[:], AL.add, [lrr_b], [LRR_b])
        self.stt("v", LL[:], LRX[:], -2.0, LRR[:], AL.mult, AL.add,
                 [LRX_b, LRR_b], [LL_b])
        pf, pfb = P["B2"]
        self.tr(pf[0:8, 0:128], LL[:], idf[:], pfb, [LL_b, idf_b])
        self.cp("v", llt[:], pf[0:8, 0:128], [pfb], [llt_b])
        self.red(lc8[:, 0:1], llt[:], AL.add, [llt_b], [lc8_b])
        S.dma("sy", self.d["lrow_sc"], lc8[0:8, 0:1], reads=[lc8_b],
              writes=[self.db("lrow_sc")], st="selm")
        S.dma("sy", lrw[0:1, 0:8], self.d["lrow_sc"].unsqueeze(0),
              reads=[self.db("lrow_sc")], writes=[lrw_b], st="selm")
        self.red(lrw[0:1, 8:9], lrw[0:1, 0:8], AL.min, [lrw_b], [lrw_b])
        self.ts("v", ohr[:], lrw[0:1, 0:8], lrw[0:1, 8:9], None, AL.is_le,
                AL.bypass, [lrw_b], [ohr_b])
        S.dma("sy", self.d["oh_sc"], ohr[0:1, 0:8], reads=[ohr_b],
              writes=[self.db("oh_sc")], st="selm")
        S.dma("sy", ohc[0:8, 0:1], self.d["oh_sc"].unsqueeze(1),
              reads=[self.db("oh_sc")], writes=[ohc_b], st="selm")
        for (src, dst, dstb) in ((self.d["WadF"], self.d["wad_sc"],
                                  self.db("wad_sc")),
                                 (self.d["WauF"], self.d["wau_sc"],
                                  self.db("wau_sc"))):
            for ch in range(16):
                slb, slb_b = selb[ch % 2]
                S.dma("g", slb[:], src[:, ch * 4096:(ch + 1) * 4096],
                      writes=[slb_b], st="sel%d" % (ch % 2))
                for i in range(8):
                    pb, pbb = P["B%d" % (i % 4)]
                    self.mm(pb[0:1, :], [(ohc[0:8, 0:1],
                                          slb[0:8, i * 512:(i + 1) * 512])],
                            pbb, [ohc_b, slb_b])
                    self.cp("v", web[0:1, i * 512:(i + 1) * 512], pb[0:1, :],
                            [pbb], [web_b])
                S.dma("sy", dst[ch * 4096:(ch + 1) * 4096].unsqueeze(0),
                      web[:], reads=[web_b], writes=[dstb], st="selw")
        pb, pbb = P["B2"]
        self.mm(pb[0:64, 0:1], [(badr[0:8, :], ohc[0:8, 0:1])], pbb,
                [badr_b, ohc_b])
        self.cp("v", selv[0:64, 0:1], pb[0:64, 0:1], [pbb], [selv_b])
        for c in range(NCH):
            pb2, pb2b = P["B%d" % (3 - c % 2)]
            self.mm(pb2[:, 0:1], [(baur[0:8, c * 128:(c + 1) * 128],
                                   ohc[0:8, 0:1])], pb2b, [baur_b, ohc_b])
            self.cp("v", bauc[:, c:c + 1], pb2[:, 0:1], [pb2b], [bauc_b])
        self.tt("v", AL.add, gbias[:], bbc[:], bauc[:], [bbc_b, bauc_b],
                [gbias_b])
        pgb, pgbb = P["B3"]
        self.tr(pgb[0:8, 0:128], gbias[:, 0:8], idf[:], pgbb, [gbias_b, idf_b])
        gb8, gb8_b = sb.alloc("gb8", [8, 128], BF16)
        self.cp("v", gb8[:], pgb[0:8, 0:128], [pgbb], [gb8_b])
        S.dma("sy", self.da("gb_sc").rearrange("(c p) -> c p", p=128), gb8[:],
              reads=[gb8_b], writes=[self.db("gb_sc")], st="selw")
        S.dma("sy", gbrow[:], self.da("gb_sc").unsqueeze(0),
              reads=[self.db("gb_sc")], writes=[gbrow_b], st="selr")
        for c in range(NCH):
            S.dma("sy", wadeb[:, c * 64:(c + 1) * 64],
                  self.d["wad_sc"][c * 8192:(c + 1) * 8192]
                  .rearrange("(p r) -> p r", r=64),
                  reads=[self.db("wad_sc")], writes=[wadeb_b], st="selr")
        S.dma("sy", waueb[:], self.d["wau_sc"].rearrange("(p o) -> p o", o=D),
              reads=[self.db("wau_sc")], writes=[waueb_b], st="selr")
        S.barrier()
        sb.reset(sel_mark)
        sgs = [sb.alloc(f"sg{i}", [128, NCH * GT], BF16) for i in range(2)]
        stms = [sb.alloc(f"stm{i}", [128, 1024], BF16) for i in range(2)]
        gtms = [sb.alloc(f"gtm{i}", [128, 1024], BF16) for i in range(2)]
        prds = [sb.alloc(f"prd{i}", [128, 1024], F32) for i in range(2)]
        xtm2 = [sb.alloc(f"xtm2{i}", [128, D], F32) for i in range(4)]
        otm = [sb.alloc(f"otm{i}", [128, D], F32) for i in range(2)]
        for tt in range(NT):
            pb, pbb = P["B%d" % (tt % 2)]
            pairs = [(wadeb[:, ci * 64:(ci + 1) * 64],
                      xnb[:, ci * T + tt * GT: ci * T + (tt + 1) * GT])
                     for ci in range(NCH)]
            self.mm(pb[0:64, :], pairs, pbb, [wadeb_b, xnb_b])
            self.act(af[0:64, tt * GT:(tt + 1) * GT], pb[0:64, :], AF.Relu,
                     [pbb, selv_b], [af_b], bias=selv[0:64, 0:1])
        for tt in range(NT):
            t0 = tt * GT
            sg, sg_b = sgs[tt % 2]
            for c in range(NCH):
                S.dma("g", sg[:, c * GT:(c + 1) * GT],
                      self.d["s_sc"][c * 128:(c + 1) * 128, t0:t0 + GT],
                      reads=[self.db("s_sc")], writes=[sg_b], st="sr")
            for s4 in range(4):
                xt, xt_b = xtm2[s4]
                S.dma("g" if s4 % 2 == 0 else "sc", xt[:],
                      self.d["x"][t0 + s4 * 128: t0 + (s4 + 1) * 128, :],
                      writes=[xt_b], st="x2%d" % (s4 % 2))
            for s4 in range(4):
                ts0 = t0 + s4 * 128
                gtm, gtm_b = gtms[s4 % 2]
                for hf in range(2):
                    pb, pbb = P["B%d" % ((s4 * 2 + hf) % 4)]
                    pairs = [(xnb[:, ci * T + ts0: ci * T + ts0 + 128],
                              wbase[:, ci * D + hf * 512: ci * D + hf * 512 + 512])
                             for ci in range(NCH)]
                    pairs.append((af[0:64, ts0:ts0 + 128],
                                  waueb[0:64, hf * 512: hf * 512 + 512]))
                    pairs.append((onb[0:1, 0:128],
                                  gbrow[0:1, hf * 512: hf * 512 + 512]))
                    self.mm(pb[:], pairs, pbb,
                            [wbase_b, xnb_b, waueb_b, af_b, onb_b, gbrow_b])
                    self.act(gtm[:, hf * 512: hf * 512 + 512], pb[:], AF.Silu,
                             [pbb], [gtm_b])
                pt, ptb = P["T%d" % (s4 % 2)]
                stm, stm_b = stms[s4 % 2]
                prd, prd_b = prds[s4 % 2]
                for c in range(NCH):
                    self.tr(pt[:, c * 128:(c + 1) * 128],
                            sg[:, c * GT + s4 * 128: c * GT + (s4 + 1) * 128],
                            idb[:], ptb, [sg_b, idb_b])
                self.acp(stm[:], pt[:], [ptb], [stm_b])
                xt, xt_b = xtm2[s4]
                ot, ot_b = otm[s4 % 2]
                self.tt("v", AL.mult, prd[:], stm[:], gtm[:],
                        [stm_b, gtm_b], [prd_b])
                self.tt("v", AL.add, ot[:], xt[:], prd[:], [xt_b, prd_b],
                        [ot_b])
                S.dma("sy", self.d["out"][t0 + s4 * 128: t0 + (s4 + 1) * 128, :],
                      ot[:], reads=[ot_b], writes=[self.db("out")], st="out")


def host_prep(inputs, T=None):
    gi = {k: np.asarray(v) for k, v in inputs.items()}
    x = gi["L0_attn_x"]
    B = x.shape[0]
    Tfull = x.shape[1]
    T = T or Tfull
    f32 = np.float32
    bf = ml_dtypes.bfloat16

    com = {}
    com["wdwp"] = np.ascontiguousarray(
        gi["L0_attn_w_dw"].astype(f32).reshape(8, 128, 3).transpose(1, 2, 0)
        .reshape(128, 24))
    com["wpwT"] = np.ascontiguousarray(gi["L0_attn_w_pw"].T.astype(f32))
    com["wqkvT"] = np.ascontiguousarray(gi["L0_attn_w_qkv"].T.astype(f32))
    com["woT"] = np.ascontiguousarray(gi["L0_attn_w_o"].T.astype(f32))
    com["wgT"] = np.ascontiguousarray(gi["L0_attn_w_gate"].T.astype(f32))
    com["wmgT"] = np.ascontiguousarray(gi["L0_attn_w_moe_gate"].T.astype(f32))
    com["w1T"] = np.ascontiguousarray(gi["L0_attn_w1"].transpose(0, 2, 1).astype(f32))
    com["w2T"] = np.ascontiguousarray(gi["L0_attn_w2"].transpose(0, 2, 1).astype(f32))
    com["b1p"] = np.ascontiguousarray(
        gi["L0_attn_b1"].astype(f32).reshape(4, 32, 128).transpose(2, 0, 1)
        .reshape(128, 128))
    com["b2"] = gi["L0_attn_b2"].astype(f32)
    com["bd1p"] = np.ascontiguousarray(
        gi["L0_ffn_bd1"].astype(f32).reshape(8, 2, 128).transpose(2, 0, 1)
        .reshape(128, 16))
    com["bvecs"] = np.ascontiguousarray(np.concatenate(
        [np.asarray(gi[k]).astype(f32).reshape(8, 128).T for k in
         ("L0_attn_b_gate", "L0_attn_ln_g", "L0_attn_ln_b", "L0_ffn_b_base")],
        axis=1))
    com["Wbase"] = np.ascontiguousarray(gi["L0_ffn_W_base"].astype(f32))
    com["Wd1"] = np.ascontiguousarray(gi["L0_ffn_Wd1"].astype(f32))
    com["Wd2"] = np.ascontiguousarray(gi["L0_ffn_Wd2"].astype(f32))
    com["bd2"] = gi["L0_ffn_bd2"].astype(f32)
    com["WadF"] = np.ascontiguousarray(gi["L0_ffn_Wa_down"].reshape(8, -1).astype(f32))
    com["bad"] = gi["L0_ffn_ba_down"].astype(f32)
    com["WauF"] = np.ascontiguousarray(gi["L0_ffn_Wa_up"].reshape(8, -1).astype(f32))
    com["bau"] = gi["L0_ffn_ba_up"].astype(f32)

    com["identb"] = np.eye(128, dtype=bf)
    com["identf"] = np.eye(128, dtype=f32)
    com["utmask"] = np.ascontiguousarray(
        np.tile(np.triu(np.ones((128, 128), np.float32)), (1, 4))).astype(bf)
    com["onesb"] = np.ones((128, 512), bf)
    inv = (1.0 / 10000.0 ** (np.arange(0, DH, 2, dtype=np.float64) / DH))
    pos = np.arange(T, dtype=np.float64)
    invf = inv.astype(np.float32).astype(np.float64)
    posf = pos.astype(np.float32).astype(np.float64)
    th = (posf[None, :] * invf[:, None]).astype(np.float32).astype(np.float64)
    sn = np.sin(th).astype(np.float32)
    cs = np.cos(th).astype(np.float32)
    C64 = np.repeat(cs, 2, axis=0)
    S64 = np.empty((64, T), np.float32)
    S64[0::2] = -sn
    S64[1::2] = sn
    com["ropeC"] = np.concatenate([C64, C64], 0).astype(bf)
    com["ropeS"] = np.concatenate([S64, S64], 0).astype(bf)

    in_maps = []
    for b in range(B):
        m = dict(com)
        m["x"] = np.ascontiguousarray(x[b, :T].astype(f32))
        in_maps.append(m)
    return in_maps


_CACHED = {}


def kernel(**inputs):
    T = int(np.asarray(inputs["L0_attn_x"]).shape[1])
    in_maps = host_prep(inputs, T)
    key = ("k", T)
    if key not in _CACHED:
        kb = KB(T)
        _CACHED[key] = kb.build()
    nc = _CACHED[key]
    res = run_bass_kernel_spmd(nc, in_maps, core_ids=list(range(8)))
    out = np.stack([np.asarray(r["out"]) for r in res.results], axis=0)
    return out.astype(np.float32)


# version 33
# speedup vs baseline: 1.0021x; 1.0021x over previous
import numpy as np
import ml_dtypes
import concourse.bass as bass
import concourse.mybir as mybir
import concourse.bass_utils as _bass_utils
from concourse.bass_utils import run_bass_kernel_spmd

if not getattr(_bass_utils, "_ldw_opt_patched", False):
    _orig_walrus_args = _bass_utils.get_walrus_args

    def _walrus_args_ldw(arch, tmpdir, **kw):
        return [a.replace("--enable-ldw-opt=false", "--enable-ldw-opt=true")
                for a in _orig_walrus_args(arch, tmpdir, **kw)]

    _bass_utils.get_walrus_args = _walrus_args_ldw
    _bass_utils._ldw_opt_patched = True

F32 = mybir.dt.float32
BF16 = mybir.dt.bfloat16
AF = mybir.ActivationFunctionType
AL = mybir.AluOpType
AX = mybir.AxisListType

D = 1024
NCH = 8
NH = 16
DH = 64
GT = 512
CT = 128
SWAP_MASK = [i ^ 1 for i in range(32)]


def _dtsz(dt):
    return 2 if dt == BF16 else 4


class Buf:
    __slots__ = ("name", "w", "readers", "excl")

    def __init__(self, name, excl=False):
        self.name = name
        self.w = None
        self.readers = []
        self.excl = excl


class Sched:
    ENGS = ("t", "v", "sc", "g", "sy")

    def __init__(self, nc):
        self.nc = nc
        self.q = {e: [] for e in self.ENGS}
        self.sems = {}
        for k in self.ENGS:
            self.sems[k] = nc.alloc_semaphore("S_" + k)
        self.cnt = {k: 0 for k in self.sems}
        self.waited = {e: {} for e in self.ENGS}
        self.ninstr = 0
        self.dma_bumps = 0

    def _need(self, eng, tok):
        key, val = tok
        if key.startswith("d:"):
            cur = self.cnt[key]
            if cur > val:
                self.dma_bumps += 1
                val = cur
        w = self.waited[eng]
        if val <= w.get(key, 0):
            return
        w[key] = val
        sem = self.sems[key]
        self.q[eng].append(lambda E, s=sem, v=val: E.wait_ge(s, v))

    def _deps(self, eng, reads, writes):
        toks = set()
        for b in reads:
            if b.w is not None:
                toks.add(b.w)
            if b.excl and False:
                for tok in b.readers:
                    if tok[0] != eng:
                        toks.add(tok)
        for b in writes:
            if b.w is not None:
                toks.add(b.w)
            toks.update(b.readers)
        for tok in sorted(toks):
            self._need(eng, tok)

    def _finish(self, tok, reads, writes):
        for b in writes:
            b.w = tok
            b.readers = []
        for b in reads:
            if b.w is not tok:
                b.readers.append(tok)

    def emit(self, eng, fns, reads=(), writes=()):
        if not isinstance(fns, (list, tuple)):
            fns = [fns]
        self._deps(eng, reads, writes)
        self.cnt[eng] += 1
        tok = (eng, self.cnt[eng])
        sem = self.sems[eng]
        n = len(fns)
        self.ninstr += n
        for i, f in enumerate(fns):
            if i == n - 1:
                self.q[eng].append(lambda E, f=f, s=sem: f(E).then_inc(s, 1))
            else:
                self.q[eng].append(lambda E, f=f: f(E))
        self._finish(tok, reads, writes)
        return tok

    def emit_chains(self, eng, chains, rbufs):
        wbufs = [c[0] for c in chains]
        self._deps(eng, rbufs, wbufs)
        sem = self.sems[eng]
        nsteps = max(len(c[1]) for c in chains)
        toks = []
        for i in range(nsteps):
            for (buf, quads) in chains:
                if i >= len(quads):
                    continue
                o, l, r = quads[i]
                st = (i == 0)
                sp = (i == len(quads) - 1)
                self.ninstr += 1
                if sp:
                    self.cnt[eng] += 1
                    tok = (eng, self.cnt[eng])
                    toks.append((buf, tok))
                    self.q[eng].append(
                        lambda E, o=o, l=l, r=r, a=st, s=sem:
                        E.matmul(o, l, r, start=a, stop=True,
                                 skip_group_check=True).then_inc(s, 1))
                else:
                    self.q[eng].append(
                        lambda E, o=o, l=l, r=r, a=st:
                        E.matmul(o, l, r, start=a, stop=False,
                                 skip_group_check=True))
        last = toks[-1][1]
        for buf, tok in toks:
            buf.w = tok
            buf.readers = []
        for b in rbufs:
            b.readers.append(last)

    def dma(self, qn, out_ap, in_ap, reads=(), writes=(), st="m"):
        key = f"d:{qn}:{st}"
        if key not in self.sems:
            self.sems[key] = self.nc.alloc_semaphore("S" + key)
            self.cnt[key] = 0
        self._deps(qn, reads, writes)
        self.cnt[key] += 16
        tok = (key, self.cnt[key])
        sem = self.sems[key]
        self.ninstr += 1
        self.q[qn].append(
            lambda E, o=out_ap, i=in_ap, s=sem: E.dma_start(out=o, in_=i).then_inc(s, 16))
        self._finish(tok, reads, writes)
        return tok

    def barrier(self):
        snap = dict(self.cnt)
        for e in self.ENGS:
            for key, val in snap.items():
                if val > 0:
                    self._need(e, (key, val))

    def finalize(self):
        for key, val in self.cnt.items():
            if val > 0:
                self._need("sy", (key, val))

    def replay(self):
        with self.nc.Block() as blk:
            blk.tensor(lambda E: [f(E) for f in self.q["t"]])
            blk.vector(lambda E: [f(E) for f in self.q["v"]])
            blk.scalar(lambda E: [f(E) for f in self.q["sc"]])
            blk.gpsimd(lambda E: [f(E) for f in self.q["g"]])
            blk.sync(lambda E: [f(E) for f in self.q["sy"]])


class SB:
    def __init__(self, nc):
        self.nc = nc
        self.off = nc.sbuf_base + 4096
        self.cap = nc.sbuf_top

    def alloc(self, name, shape, dtype):
        nbytes = int(np.prod(shape[1:])) * _dtsz(dtype)
        self.off = (self.off + 63) & ~63
        assert self.off + nbytes <= self.cap, \
            f"SBUF overflow at {name}: {self.off}+{nbytes} > {self.cap}"
        t = self.nc.alloc_sbuf_tensor_at(name, list(shape), dtype, offset=self.off)
        self.off += nbytes
        return t, Buf(name)

    def mark(self):
        return self.off

    def reset(self, mark):
        self.off = mark


class KB:
    def __init__(self, T, taps=()):
        assert T % 1024 == 0
        self.T = T
        self.taps = tuple(taps)
        self.nc = bass.Bass("TRN2", target_bir_lowering=False)
        self.S = Sched(self.nc)
        self.sb = SB(self.nc)
        self.dram = {}
        self.NG = T // GT

    def din(self, name, shape, dtype=F32):
        ap = self.nc.dram_tensor(name, list(shape), dtype, kind="ExternalInput").ap()
        self.dram[name] = (ap, Buf(name))
        return ap

    def dout(self, name, shape, dtype=F32):
        ap = self.nc.dram_tensor(name, list(shape), dtype, kind="ExternalOutput").ap()
        self.dram[name] = (ap, Buf(name))
        return ap

    def dscr(self, name, shape, dtype=BF16):
        ap = self.nc.dram_tensor(name, list(shape), dtype, kind="Internal").ap()
        self.dram[name] = (ap, Buf(name))
        return ap

    def db(self, name):
        return self.dram[name][1]

    def da(self, name):
        return self.dram[name][0]

    def mm(self, psum_ap, pairs, psum_buf, rbufs, start=True, stop=True, skip=False):
        fns = []
        n = len(pairs)
        for i, (lt, rh) in enumerate(pairs):
            st = start and (i == 0)
            sp = stop and (i == n - 1)
            fns.append(lambda E, o=psum_ap, l=lt, r=rh, a=st, b=sp, k=skip:
                       E.matmul(o, l, r, start=a, stop=b, skip_group_check=k))
        return self.S.emit("t", fns, reads=rbufs, writes=[psum_buf])

    def mmx(self, quads, rbufs, wbufs):
        fns = [lambda E, o=o, l=l, r=r, a=a, b=b:
               E.matmul(o, l, r, start=a, stop=b, skip_group_check=True)
               for (o, l, r, a, b) in quads]
        return self.S.emit("t", fns, reads=rbufs, writes=wbufs)

    def acp(self, out_ap, in_ap, rbufs, wbufs):
        return self.act(out_ap, in_ap, AF.Copy, rbufs, wbufs)

    def tr(self, psum_ap, in_ap, ident_ap, psum_buf, rbufs):
        return self.S.emit(
            "t", lambda E, o=psum_ap, i=in_ap, d=ident_ap: E.transpose(o, i, d),
            reads=rbufs, writes=[psum_buf])

    def cp(self, eng, out_ap, in_ap, rbufs, wbufs):
        return self.S.emit(eng, lambda E, o=out_ap, i=in_ap: E.tensor_copy(o, i),
                           reads=rbufs, writes=wbufs)

    def tt(self, eng, op, out_ap, a_ap, b_ap, rbufs, wbufs):
        return self.S.emit(
            eng, lambda E, o=out_ap, a=a_ap, b=b_ap, p=op:
            E.tensor_tensor(o, a, b, op=p), reads=rbufs, writes=wbufs)

    def ts(self, eng, out_ap, in_ap, s1, s2, op0, op1, rbufs, wbufs):
        return self.S.emit(
            eng, lambda E, o=out_ap, i=in_ap, a=s1, b=s2, p=op0, q=op1:
            E.tensor_scalar(o, i, a, b, p, q), reads=rbufs, writes=wbufs)

    def stt(self, eng, out_ap, in0_ap, scalar, in1_ap, op0, op1, rbufs, wbufs,
            accum=None):
        return self.S.emit(
            eng, lambda E, o=out_ap, i=in0_ap, s=scalar, j=in1_ap, p=op0, q=op1,
            a=accum: E.scalar_tensor_tensor(o, i, s, j, p, q, accum_out=a),
            reads=rbufs, writes=wbufs)

    def act(self, out_ap, in_ap, func, rbufs, wbufs, bias=0.0, scale=1.0, accum=None):
        return self.S.emit(
            "sc", lambda E, o=out_ap, i=in_ap, f=func, b=bias, s=scale, a=accum:
            E.activation(out=o, in_=i, func=f, bias=b, scale=s, accum_out=a),
            reads=rbufs, writes=wbufs)

    def red(self, out_ap, in_ap, op, rbufs, wbufs):
        return self.S.emit(
            "v", lambda E, o=out_ap, i=in_ap, p=op: E.tensor_reduce(o, i, AX.X, p),
            reads=rbufs, writes=wbufs)

    def recip(self, out_ap, in_ap, rbufs, wbufs):
        return self.S.emit(
            "v", lambda E, o=out_ap, i=in_ap: E.reciprocal(o, i),
            reads=rbufs, writes=wbufs)

    def shuf(self, out_ap, in_ap, rbufs, wbufs):
        return self.S.emit(
            "v", lambda E, o=out_ap, i=in_ap: E.stream_shuffle(o, i, SWAP_MASK),
            reads=rbufs, writes=wbufs)

    def memset(self, eng, ap, val, wbufs):
        return self.S.emit(eng, lambda E, a=ap, v=val: E.memset(a, v), writes=wbufs)

    def ttr(self, out_ap, in0_ap, in1_ap, accum_ap, rbufs, wbufs):
        return self.S.emit(
            "v", lambda E, o=out_ap, a=in0_ap, b=in1_ap, c=accum_ap:
            E.tensor_tensor_reduce(o, a, b, 1.0, 0.0, AL.mult, AL.add, c),
            reads=rbufs, writes=wbufs)

    def build(self):
        nc, S, sb, T = self.nc, self.S, self.sb, self.T
        NG = self.NG
        T2 = T // 2
        NT2 = T2 // GT

        x_in = self.din("x", [T, D])
        wdwp_d = self.din("wdwp", [128, 24])
        bvecs_d = self.din("bvecs", [128, 32])
        b1p_d = self.din("b1p", [128, 128])
        bd1p_d = self.din("bd1p", [128, 16])
        wpwT_d = self.din("wpwT", [D, D])
        wqkvT_d = self.din("wqkvT", [D, 3 * D])
        woT_d = self.din("woT", [D, D])
        wgT_d = self.din("wgT", [D, D])
        wmgT_d = self.din("wmgT", [D, 4])
        w1T_d = self.din("w1T", [4, D, 4 * D])
        w2T_d = self.din("w2T", [4, 4 * D, D])
        b2_d = self.din("b2", [4, D])
        Wbase_d = self.din("Wbase", [D, D])
        Wd1_d = self.din("Wd1", [8, D, 256])
        Wd2_d = self.din("Wd2", [8, 256, D])
        bd2_d = self.din("bd2", [8, D])
        WadF_d = self.din("WadF", [8, D * 64])
        bad_d = self.din("bad", [8, 64])
        WauF_d = self.din("WauF", [8, 64 * D])
        bau_d = self.din("bau", [8, D])
        identb_d = self.din("identb", [128, 128], BF16)
        identf_d = self.din("identf", [128, 128], F32)
        utmask_d = self.din("utmask", [128, 512], BF16)
        ropeC_d = self.din("ropeC", [128, T], BF16)
        ropeS_d = self.din("ropeS", [128, T], BF16)
        ones_d = self.din("onesb", [128, 512], BF16)
        out_d = self.dout("out", [T, D])

        xn_sc = self.dscr("xn_sc", [D, T])
        xp_sc = self.dscr("xp_sc", [D, T])
        xpp_sc = self.dscr("xpp_sc", [D, T])
        h_sc = self.dscr("h_sc", [D, T])
        s_sc = self.dscr("s_sc", [D, T])
        gw_sc = self.dscr("gw_sc", [4, T])
        wad_sc = self.dscr("wad_sc", [D * 64])
        wau_sc = self.dscr("wau_sc", [64 * D])
        lrow_sc = self.dscr("lrow_sc", [8], F32)
        gb_sc = self.dscr("gb_sc", [D], BF16)
        oh_sc = self.dscr("oh_sc", [8], BF16)
        for nm in self.taps:
            self.dout("tap_" + nm, [D, T], BF16)

        P = {}
        for i in range(4):
            P[f"B{i}"] = (nc.alloc_psum_tensor(f"PB{i}", [128, 512], F32),
                          Buf(f"PB{i}", excl=True))
        P["S0"] = (nc.alloc_psum_tensor("PS0", [128, 512], F32), Buf("PS0", excl=True))
        P["S1"] = (nc.alloc_psum_tensor("PS1", [128, 512], F32), Buf("PS1", excl=True))
        P["T0"] = (nc.alloc_psum_tensor("PT0", [128, 1024], BF16), Buf("PT0", excl=True))
        P["T1"] = (nc.alloc_psum_tensor("PT1", [128, 1024], BF16), Buf("PT1", excl=True))
        self.P = P

        idb, idb_b = sb.alloc("idb", [128, 128], BF16)
        idf, idf_b = sb.alloc("idf", [128, 128], F32)
        utm, utm_b = sb.alloc("utm", [128, 512], BF16)
        onb, onb_b = sb.alloc("onb", [128, 512], BF16)
        wdwc, wdwc_b = sb.alloc("wdwc", [128, 3 * NCH], F32)
        bg_n, bg_n_b = sb.alloc("bg_n", [128, NCH], F32)
        lngc, lngc_b = sb.alloc("lngc", [128, NCH], F32)
        lnbc, lnbc_b = sb.alloc("lnbc", [128, NCH], F32)
        b1c, b1c_b = sb.alloc("b1c", [128, 4 * 32], F32)
        bd1c, bd1c_b = sb.alloc("bd1c", [128, 16], F32)
        bd2r, bd2r_b = sb.alloc("bd2r", [1, 8 * D], BF16)
        b2r, b2r_b = sb.alloc("b2r", [4, D], BF16)
        bbc, bbc_b = sb.alloc("bbc", [128, NCH], F32)
        badr, badr_b = sb.alloc("badr", [8, 64], BF16)
        baur, baur_b = sb.alloc("baur", [8, D], BF16)
        gbias, gbias_b = sb.alloc("gbias", [128, NCH], F32)
        selv, selv_b = sb.alloc("selv", [128, 2], F32)
        bauc, bauc_b = sb.alloc("bauc", [128, NCH], F32)
        S_sb, S_sb_b = sb.alloc("S_sb", [128, 8 * 65], BF16)
        ohc, ohc_b = sb.alloc("ohc", [8, 2], BF16)
        kon, kon_b = sb.alloc("kon", [128, 4], F32)
        qkv_mark = sb.mark()
        wqv, wqv_b = sb.alloc("wq", [128, NCH * D], BF16)
        wkv, wkv_b = sb.alloc("wk", [128, NCH * D], BF16)
        wvv, wvv_b = sb.alloc("wv", [128, NCH * D], BF16)
        perm_mark = sb.mark()
        self.memset("v", kon[:, 0:1], 1e-5, [kon_b])
        self.memset("v", kon[:, 1:2], -0.5, [kon_b])
        self.memset("v", kon[:, 2:3], -1.0, [kon_b])
        self.memset("v", kon[:, 3:4], 1e-6, [kon_b])

        S.dma("sy", idb[:], identb_d, writes=[idb_b], st="pre")
        S.dma("sy", idf[:], identf_d, writes=[idf_b], st="pre")
        S.dma("sy", utm[:], utmask_d, writes=[utm_b], st="pre")
        S.dma("sy", onb[:], ones_d, writes=[onb_b], st="pre")
        S.dma("g", b2r[:], b2_d, writes=[b2r_b], st="pre")
        S.dma("g", bd2r[:], bd2_d.rearrange("e d -> (e d)").unsqueeze(0),
              writes=[bd2r_b], st="pre")
        S.dma("g", badr[:], bad_d, writes=[badr_b], st="pre")
        S.dma("g", baur[:], bau_d, writes=[baur_b], st="pre")
        S.dma("sy", wdwc[:], wdwp_d, writes=[wdwc_b], st="pre")
        S.dma("sy", bg_n[:], bvecs_d[:, 0:8], writes=[bg_n_b], st="pre")
        S.dma("sy", lngc[:], bvecs_d[:, 8:16], writes=[lngc_b], st="pre")
        S.dma("sy", lnbc[:], bvecs_d[:, 16:24], writes=[lnbc_b], st="pre")
        S.dma("sy", bbc[:], bvecs_d[:, 24:32], writes=[bbc_b], st="pre")
        S.dma("sy", b1c[:], b1p_d, writes=[b1c_b], st="pre")
        S.dma("sy", bd1c[:], bd1p_d, writes=[bd1c_b], st="pre")


        self.c = dict(idb=(idb, idb_b), idf=(idf, idf_b), utm=(utm, utm_b),
                      onb=(onb, onb_b), wdwc=(wdwc, wdwc_b), bg_n=(bg_n, bg_n_b),
                      lngc=(lngc, lngc_b), lnbc=(lnbc, lnbc_b), b1c=(b1c, b1c_b),
                      bd1c=(bd1c, bd1c_b), bd2r=(bd2r, bd2r_b), b2r=(b2r, b2r_b),
                      bbc=(bbc, bbc_b), badr=(badr, badr_b), baur=(baur, baur_b),
                      gbias=(gbias, gbias_b), selv=(selv, selv_b),
                      bauc=(bauc, bauc_b), S_sb=(S_sb, S_sb_b), ohc=(ohc, ohc_b),
                      kon=(kon, kon_b), wq=(wqv, wqv_b), wk=(wkv, wkv_b),
                      wv=(wvv, wvv_b))
        self.d = dict(x=x_in, wpwT=wpwT_d, wqkvT=wqkvT_d, woT=woT_d, wgT=wgT_d,
                      wmgT=wmgT_d, w1T=w1T_d, w2T=w2T_d, Wbase=Wbase_d,
                      Wd1=Wd1_d, Wd2=Wd2_d, WadF=WadF_d, WauF=WauF_d,
                      ropeC=ropeC_d, ropeS=ropeS_d, out=out_d,
                      xn_sc=xn_sc, xp_sc=xp_sc, xpp_sc=xpp_sc, h_sc=h_sc,
                      s_sc=s_sc, gw_sc=gw_sc, wad_sc=wad_sc, wau_sc=wau_sc,
                      lrow_sc=lrow_sc, oh_sc=oh_sc)

        self.phase_p0m1(perm_mark)
        S.barrier()
        self.phase_m2(perm_mark)
        S.barrier()
        self.phase_m3(qkv_mark)
        S.barrier()
        self.phase_ffn(qkv_mark)
        for nm in self.taps:
            src = self.d[nm + "_sc"]
            S.dma("sy", self.da("tap_" + nm), src, reads=[self.db(nm + "_sc")],
                  writes=[self.db("tap_" + nm)], st="tap")
        S.finalize()
        S.replay()
        return nc

    def phase_p0m1(self, mark):
        S, sb, P, T = self.S, self.sb, self.P, self.T
        NG = self.NG
        idb, idb_b = self.c["idb"]
        wdwc, wdwc_b = self.c["wdwc"]
        sb.reset(mark)
        xtm = [sb.alloc(f"xtm{i}", [128, 4 * D], F32) for i in range(2)]
        xcv2 = [[sb.alloc(f"xcv{p}_{c}", [128, 514], BF16) for c in range(NCH)]
                for p in range(2)]
        xout, xout_b = sb.alloc("xout", [128, 2 * GT], BF16)
        wpw, wpw_b = sb.alloc("wpw", [128, NCH * D], BF16)
        wpw3 = [sb.alloc(f"wpw3_{k}", [128, NCH * D], BF16) for k in range(3)]
        stats = [sb.alloc(f"stat{i}", [128, 16], F32) for i in range(4)]
        xnt = [sb.alloc(f"xnt{i}", [128, D], BF16) for i in range(4)]

        for ci in range(NCH):
            S.dma("g", wpw[:, ci * D:(ci + 1) * D],
                  self.d["wpwT"][ci * 128:(ci + 1) * 128, :], writes=[wpw_b],
                  st="wpw")
        for k in range(3):
            for ci in range(NCH):
                self.ts("v", wpw3[k][0][:, ci * D:(ci + 1) * D],
                        wpw[:, ci * D:(ci + 1) * D],
                        wdwc[:, k * NCH + ci: k * NCH + ci + 1], None,
                        AL.mult, AL.bypass, [wpw_b, wdwc_b], [wpw3[k][1]])
        for c in range(NCH):
            self.memset("v", xcv2[0][c][0][:, 0:2], 0.0, [xcv2[0][c][1]])

        for g in range(NG):
            t0 = g * GT
            xt, xt_b = xtm[g % 2]
            S.dma("g", xt[:].rearrange("p (s d) -> p s d", d=D),
                  self.d["x"][t0:t0 + GT, :].rearrange("(s p) d -> p s d", p=128),
                  writes=[xt_b], st="x%d" % (g % 2))
            for wi2, nm in enumerate(("wq", "wk", "wv")):
                wt_, wt_b = self.c[nm]
                S.dma("g", wt_[:, g * D:(g + 1) * D],
                      self.d["wqkvT"][g * 128:(g + 1) * 128,
                                      wi2 * D:(wi2 + 1) * D],
                      writes=[wt_b], st="wqkv")
            xcv = xcv2[g % 2]
            if g > 0:
                prev = xcv2[(g - 1) % 2]
                for c in range(NCH):
                    self.cp("v", xcv[c][0][:, 0:2], prev[c][0][:, 512:514],
                            [prev[c][1]], [xcv[c][1]])
            kon, kon_b = self.c["kon"]
            for s4 in range(4):
                xv = xt[:, s4 * D:(s4 + 1) * D]
                stat, stat_b = stats[s4]
                self.S.emit("v", lambda E, o=stat[:, 0:6], i=xv[:, 0:512]:
                            E.bn_stats(o, i), reads=[xt_b], writes=[stat_b])
                self.S.emit("v", lambda E, o=stat[:, 6:12], i=xv[:, 512:1024]:
                            E.bn_stats(o, i), reads=[xt_b], writes=[stat_b])
                self.S.emit("v", lambda E, o=stat[:, 12:14], i=stat[:, 0:12]:
                            E.bn_aggr(o, i), reads=[stat_b], writes=[stat_b])
                self.act(stat[:, 14:15], stat[:, 13:14], AF.Ln, [stat_b, kon_b],
                         [stat_b], bias=kon[:, 0:1])
                self.act(stat[:, 14:15], stat[:, 14:15], AF.Exp, [stat_b, kon_b],
                         [stat_b], scale=kon[:, 1:2])
                xn, xn_b = xnt[s4]
                self.ts("v", xn[:], xv, stat[:, 12:13], stat[:, 14:15],
                        AL.subtract, AL.mult, [xt_b, stat_b], [xn_b])
            for s4 in range(4):
                xn, xn_b = xnt[s4]
                for c in range(NCH):
                    pt, ptb = P["T%d" % ((s4 * NCH + c) % 2)]
                    self.tr(pt[:, 0:128], xn[:, c * 128:(c + 1) * 128], idb[:],
                            ptb, [xn_b, idb_b])
                    dstv = xcv[c][0][:, 2 + s4 * 128: 2 + (s4 + 1) * 128]
                    if c % 2 == 0:
                        self.acp(dstv, pt[:, 0:128], [ptb], [xcv[c][1]])
                    else:
                        self.cp("v", dstv, pt[:, 0:128], [ptb], [xcv[c][1]])
            for c in range(NCH):
                S.dma("sy", self.d["xn_sc"][c * 128:(c + 1) * 128, t0:t0 + GT],
                      xcv[c][0][:, 2:514], reads=[xcv[c][1]],
                      writes=[self.db("xn_sc")], st="xnw")
            for co in range(NCH):
                pb, pbb = P["B%d" % (co % 2)]
                pairs = []
                for k, sh in ((2, 2), (1, 1), (0, 0)):
                    wv, wv_b = wpw3[k]
                    for ci in range(NCH):
                        pairs.append(
                            (wv[:, ci * D + co * 128: ci * D + (co + 1) * 128],
                             xcv[ci][0][:, sh: sh + 512]))
                self.mm(pb[:], pairs, pbb,
                        [wpw3[0][1], wpw3[1][1], wpw3[2][1]] +
                        [xcv[ci][1] for ci in range(NCH)])
                xo = xout[:, (co % 2) * GT:(co % 2 + 1) * GT]
                self.tt("v", AL.add, xo, pb[:], xcv[co][0][:, 2:514],
                        [pbb, xcv[co][1]], [xout_b])
                S.dma("sy", self.d["xp_sc"][co * 128:(co + 1) * 128, t0:t0 + GT],
                      xo, reads=[xout_b], writes=[self.db("xp_sc")], st="xpw")

    def phase_m2(self, mark):
        S, sb, P, T = self.S, self.sb, self.P, self.T
        NG = self.NG
        idb, idb_b = self.c["idb"]
        utm, utm_b = self.c["utm"]
        onb, onb_b = self.c["onb"]
        bg_n, bg_n_b = self.c["bg_n"]
        lngc, lngc_b = self.c["lngc"]
        lnbc, lnbc_b = self.c["lnbc"]
        S_sb, S_sb_b = self.c["S_sb"]
        sb.reset(mark)
        wq, wq_b = self.c["wq"]
        wk, wk_b = self.c["wk"]
        wv, wv_b = self.c["wv"]
        wo, wo_b = sb.alloc("wo", [128, NCH * D], BF16)
        wg, wg_b = sb.alloc("wg", [128, NCH * D], BF16)
        wmg, wmg_b = sb.alloc("wmg", [128, NCH * 4], BF16)
        rcg = [sb.alloc(f"rcg{i}", [128, GT], BF16) for i in range(2)]
        rsg = [sb.alloc(f"rsg{i}", [128, GT], BF16) for i in range(2)]
        xg = [sb.alloc(f"xg{i}", [128, NCH * 520], BF16) for i in range(2)]
        qp, qp_b = sb.alloc("qp", [128, NCH * GT], BF16)
        kp, kp_b = sb.alloc("kp", [128, NCH * GT], BF16)
        vaug, vaug_b = sb.alloc("vaug", [128, 4 * 1048], BF16)
        ktm, ktm_b = sb.alloc("ktm", [128, 1024], BF16)
        atb = [sb.alloc(f"atb{i}", [128, 512], BF16) for i in range(4)]
        ndsb = [sb.alloc(f"ndsb{i}", [128, 512], BF16) for i in range(4)]
        att, att_b = sb.alloc("attg", [128, NCH * GT], BF16)
        xpt, xpt_b = sb.alloc("xppg", [128, NCH * GT], BF16)
        hgt, hgt_b = att, att_b
        rop = [sb.alloc(f"rop{i}", [128, GT], F32) for i in range(6)]
        gat = [sb.alloc(f"gat{i}", [128, GT], BF16) for i in range(2)]
        stt2, stt2_b = sb.alloc("stt2", [128, 1024], F32)
        stbf, stbf_b = sb.alloc("stbf", [128, 1024], BF16)
        sqbs = [sb.alloc(f"sqb{i}", [128, GT], BF16) for i in range(2)]
        gwt, gwt_b = sb.alloc("gwt", [128, 8], BF16)
        gtmp, gtmp_b = sb.alloc("gtmp", [4, 128], BF16)

        for ci in range(NCH):
            S.dma("g", wmg[:, ci * 4:(ci + 1) * 4],
                  self.d["wmgT"][ci * 128:(ci + 1) * 128, :],
                  writes=[wmg_b], st="wm2")

        for k4 in range(4):
            for hh in range(NH):
                o = k4 * 1048 + hh * 65 + 64
                self.memset("v", vaug[:, o:o + 1], 1.0, [vaug_b])
        self.memset("v", S_sb[:], 0.0, [S_sb_b])

        pend = []
        for g in range(NG):
            t0 = g * GT
            xgt, xgt_b = xg[g % 2]
            rc, rc_b = rcg[g % 2]
            rs, rs_b = rsg[g % 2]
            S.dma("g", rc[:], self.d["ropeC"][:, t0:t0 + GT], writes=[rc_b],
                  st="rope%d" % (g % 2))
            S.dma("g", rs[:], self.d["ropeS"][:, t0:t0 + GT], writes=[rs_b],
                  st="rope%d" % (g % 2))
            for c in range(NCH):
                base = c * 520
                r = slice(c * 128, (c + 1) * 128)
                if g == 0:
                    S.dma("g", xgt[:, base + 1: base + 513],
                          self.d["xp_sc"][r, 0:GT], reads=[self.db("xp_sc")],
                          writes=[xgt_b], st="xg0")
                    self.cp("v", xgt[:, base:base + 1],
                            xgt[:, base + 1:base + 2], [xgt_b], [xgt_b])
                else:
                    S.dma("g", xgt[:, base: base + 513],
                          self.d["xp_sc"][r, t0 - 1:t0 + GT],
                          reads=[self.db("xp_sc")], writes=[xgt_b],
                          st="xg%d" % (g % 2))
            if g == 0:
                for ci in range(NCH):
                    cd = slice(ci * D, (ci + 1) * D)
                    r = slice(ci * 128, (ci + 1) * 128)
                    S.dma("g", wo[:, cd], self.d["woT"][r, :], writes=[wo_b],
                          st="wm2")
                    S.dma("g", wg[:, cd], self.d["wgT"][r, :], writes=[wg_b],
                          st="wm2")

            for wi, (wb_, wbuf, dst, dstb) in enumerate(
                    ((wq, wq_b, qp, qp_b), (wk, wk_b, kp, kp_b))):
                for co in range(NCH):
                    t6 = (wi * NCH + co) % 2
                    c0, c0b = rop[3 * t6]
                    r0, r0b = rop[3 * t6 + 1]
                    r1, r1b = rop[3 * t6 + 2]
                    pb, pbb = P["B%d" % (co % 4)]
                    pairs = [(wb_[:, ci * D + co * 128: ci * D + (co + 1) * 128],
                              xgt[:, ci * 520 + 1: ci * 520 + 513])
                             for ci in range(NCH)]
                    self.mm(pb[:], pairs, pbb, [wbuf, xgt_b])
                    self.cp("v", c0[:], pb[:], [pbb], [c0b])
                    Cs = rc[:]
                    Ss = rs[:]
                    self.shuf(r0[:], c0[:], [c0b], [r0b])
                    self.tt("g", AL.mult, r1[:], c0[:], Cs, [c0b, rc_b], [r1b])
                    self.tt("g", AL.mult, r0[:], r0[:], Ss, [r0b, rs_b], [r0b])
                    self.tt("v", AL.add, r1[:], r1[:], r0[:], [r0b, r1b], [r1b])
                    self.act(r0[:], r1[:], AF.Exp, [r1b], [r0b])
                    self.S.emit("v", lambda E, o=r0[:], i=r0[:]:
                                E.tensor_scalar_min(o, i, 1.0),
                                reads=[r0b], writes=[r0b])
                    self.stt("v", dst[:, co * GT:(co + 1) * GT], r1[:], 1.0,
                             r0[:], AL.add, AL.max, [r0b, r1b], [dstb])

            for fn in pend:
                fn()
            pend.clear()
            for k4 in range(4):
                for hf in range(2):
                    pb, pbb = P["B%d" % (3 - hf)]
                    pairs = [(xgt[:, ci * 520 + 1 + k4 * 128:
                                  ci * 520 + 1 + (k4 + 1) * 128],
                              wv[:, ci * D + hf * 512: ci * D + (hf + 1) * 512])
                             for ci in range(NCH)]
                    self.mm(pb[:], pairs, pbb, [wv_b, xgt_b])
                    ov = vaug[:, k4 * 1048 + hf * 8 * 65: k4 * 1048 + hf * 8 * 65 + 8 * 65] \
                        .rearrange("p (h y) -> p h y", y=65)[:, :, 0:64]
                    iv = pb[:].rearrange("p (h y) -> p h y", y=64)
                    self.acp(ov, iv, [pbb], [vaug_b])

            for k4 in range(4):
                kk = g * 4 + k4
                vb = k4 * 1048
                ptr, ptrb = P["T%d" % (kk % 2)]
                for c in range(NCH):
                    self.tr(ptr[:, c * 128:(c + 1) * 128],
                            kp[:, c * GT + k4 * 128: c * GT + (k4 + 1) * 128],
                            idb[:], ptrb, [kp_b, idb_b])
                self.acp(ktm[:], ptr[:], [ptrb], [ktm_b])
                waves = []
                for w in range(4):
                    ws = [w * 4 + hh for hh in range(4)]
                    tsls = [slice((h // 2) * GT + k4 * 128,
                                  (h // 2) * GT + (k4 + 1) * 128) for h in ws]
                    p0s = [(h % 2) * 64 for h in ws]
                    waves.append((ws, tsls, p0s))
                for w, (ws, tsls, p0s) in enumerate(waves):
                    atp, atpb = P["B%d" % w]
                    for hh in range(4):
                        self.mm(atp[:, hh * 128:(hh + 1) * 128],
                                [(kp[p0s[hh]:p0s[hh] + 64, tsls[hh]],
                                  qp[p0s[hh]:p0s[hh] + 64, tsls[hh]])],
                                atpb, [qp_b, kp_b], skip=True)
                    ab4, ab4b = atb[w]
                    self.tt("v", AL.mult, ab4[:], atp[:], utm[:],
                            [atpb, utm_b], [ab4b])
                for w, (ws, tsls, p0s) in enumerate(waves):
                    sp, spb = P["S%d" % (w // 2)]
                    for hh in range(4):
                        h = ws[hh]
                        self.mm(sp[p0s[hh]:p0s[hh] + 64,
                                   ((h // 2) % 4) * 65:((h // 2) % 4) * 65 + 65],
                                [(ktm[:, h * 64:(h + 1) * 64],
                                  vaug[:, vb + h * 65: vb + h * 65 + 65])],
                                spb, [ktm_b, vaug_b], start=(kk == 0),
                                stop=True, skip=True)
                kon, kon_b = self.c["kon"]
                for w, (ws, tsls, p0s) in enumerate(waves):
                    nd, ndb = P["B%d" % w]
                    ab4, ab4b = atb[w]
                    for hh in range(4):
                        h = ws[hh]
                        col = hh * 128
                        self.mm(nd[0:65, col:col + 128],
                                [(vaug[:, vb + h * 65: vb + h * 65 + 65],
                                  ab4[:, col:col + 128])],
                                ndb, [vaug_b, ab4b], start=True, stop=False,
                                skip=True)
                        self.mm(nd[0:65, col:col + 128],
                                [(S_sb[p0s[hh]:p0s[hh] + 64,
                                       (h // 2) * 65:(h // 2) * 65 + 65],
                                  qp[p0s[hh]:p0s[hh] + 64, tsls[hh]])],
                                ndb, [S_sb_b, qp_b], start=False, stop=True,
                                skip=True)
                    nsb, nsbb = ndsb[w]
                    self.acp(nsb[0:64, :], nd[0:64, :], [ndb], [nsbb])
                    self.act(nsb[64:65, :], nd[64:65, :], AF.Ln,
                             [ndb, kon_b], [nsbb], bias=kon[64:65, 3:4])
                    self.act(nsb[64:65, :], nsb[64:65, :], AF.Exp,
                             [nsbb, kon_b], [nsbb], scale=kon[64:65, 2:3])
                for w, (ws, tsls, p0s) in enumerate(waves):
                    nd, ndb = P["B%d" % w]
                    nsb, nsbb = ndsb[w]
                    self.mm(nd[0:64, :], [(onb[64:65, 0:64], nsb[64:65, :])],
                            ndb, [onb_b, nsbb])
                    for hh in range(4):
                        h = ws[hh]
                        col = hh * 128
                        self.tt("v", AL.mult,
                                att[p0s[hh]:p0s[hh] + 64, tsls[hh]],
                                nsb[0:64, col:col + 128], nd[0:64, col:col + 128],
                                [nsbb, ndb], [att_b])
                for si in range(2):
                    sp, spb = P["S%d" % si]
                    self.acp(S_sb[:, si * 260:(si + 1) * 260], sp[:, 0:260],
                             [spb], [S_sb_b])
            delg, delg_b = qp, qp_b
            for c in range(NCH):
                self.tt("g" if c % 2 else "v", AL.subtract,
                        delg[:, c * GT:(c + 1) * GT],
                        xgt[:, c * 520 + 1: c * 520 + 513],
                        xgt[:, c * 520: c * 520 + 512], [xgt_b], [delg_b])
            for co in range(NCH):
                po, pob = P["B%d" % (co % 2)]
                pairs = [(wo[:, ci * D + co * 128: ci * D + (co + 1) * 128],
                          att[:, ci * GT:(ci + 1) * GT]) for ci in range(NCH)]
                self.mm(po[:], pairs, pob, [wo_b, att_b])
                pg, pgb = P["B%d" % (2 + co % 2)]
                pairs = [(wg[:, ci * D + co * 128: ci * D + (co + 1) * 128],
                          delg[:, ci * GT:(ci + 1) * GT]) for ci in range(NCH)]
                self.mm(pg[:], pairs, pgb, [wg_b, delg_b])
                ga, gab = gat[co % 2]
                self.act(ga[:], pg[:], AF.Sigmoid, [pgb, bg_n_b], [gab],
                         bias=bg_n[:, co:co + 1])
                self.tt("v", AL.mult, ga[:], ga[:], po[:], [gab, pob], [gab])
                self.tt("g", AL.add, xpt[:, co * GT:(co + 1) * GT], ga[:],
                        xgt[:, co * 520 + 1: co * 520 + 513], [gab, xgt_b],
                        [xpt_b])
                S.dma("sy", self.d["xpp_sc"][co * 128:(co + 1) * 128, t0:t0 + GT],
                      xpt[:, co * GT:(co + 1) * GT], reads=[xpt_b],
                      writes=[self.db("xpp_sc")], st="xppw")
            pA, pAb = P["B0"]
            pB, pBb = P["B1"]
            for c in range(NCH):
                sqb, sqb_b = sqbs[c % 2]
                self.act(sqb[:], xpt[:, c * GT:(c + 1) * GT], AF.Square,
                         [xpt_b], [sqb_b])
                self.mm(pB[0:1, :], [(onb[:, 0:1], sqb[:])], pBb,
                        [onb_b, sqb_b], start=(c == 0), stop=(c == NCH - 1),
                        skip=True)
            pairs = [(onb[:, 0:1], xpt[:, c * GT:(c + 1) * GT])
                     for c in range(NCH)]
            self.mm(pA[0:1, :], pairs, pAb, [onb_b, xpt_b])
            self.ts("v", stt2[0:1, 0:512], pA[0:1, :], 1.0 / D, None, AL.mult,
                    AL.bypass, [pAb], [stt2_b])
            self.ts("v", stt2[0:1, 512:1024], pB[0:1, :], 1.0 / D, None,
                    AL.mult, AL.bypass, [pBb], [stt2_b])
            self.tt("v", AL.mult, sqbs[0][0][0:1, 0:512], stt2[0:1, 0:512],
                    stt2[0:1, 0:512], [stt2_b], [sqbs[0][1]])
            self.tt("v", AL.subtract, stt2[0:1, 512:1024], stt2[0:1, 512:1024],
                    sqbs[0][0][0:1, 0:512], [stt2_b, sqbs[0][1]], [stt2_b])
            kon, kon_b = self.c["kon"]
            self.act(stt2[0:1, 512:1024], stt2[0:1, 512:1024], AF.Ln,
                     [stt2_b, kon_b], [stt2_b], bias=kon[0:1, 0:1])
            self.act(stt2[0:1, 512:1024], stt2[0:1, 512:1024], AF.Exp,
                     [stt2_b, kon_b], [stt2_b], scale=kon[0:1, 1:2])
            self.cp("v", stbf[0:1, 0:512], stt2[0:1, 0:512], [stt2_b], [stbf_b])
            self.cp("v", stbf[0:1, 512:1024], stt2[0:1, 512:1024], [stt2_b],
                    [stbf_b])
            TA, TAb = P["T0"][0][:].bitcast(F32), P["T0"][1]
            TB, TBb = P["T1"][0][:].bitcast(F32), P["T1"][1]
            self.mm(TA[:, 0:512], [(onb[0:1, 0:128], stbf[0:1, 0:512])], TAb,
                    [onb_b, stbf_b])
            self.mm(TB[:, 0:512], [(onb[0:1, 0:128], stbf[0:1, 512:1024])], TBb,
                    [onb_b, stbf_b])
            for c in range(NCH):
                r0, r0b = rop[c % 2]
                self.tt("v", AL.subtract, r0[:], xpt[:, c * GT:(c + 1) * GT],
                        TA[:, 0:512], [xpt_b, TAb], [r0b])
                self.tt("v", AL.mult, r0[:], r0[:], TB[:, 0:512], [r0b, TBb], [r0b])
                self.ts("v", hgt[:, c * GT:(c + 1) * GT], r0[:],
                        lngc[:, c:c + 1], lnbc[:, c:c + 1], AL.mult, AL.add,
                        [r0b, lngc_b, lnbc_b], [hgt_b])
                S.dma("sy", self.d["h_sc"][c * 128:(c + 1) * 128, t0:t0 + GT],
                      hgt[:, c * GT:(c + 1) * GT], reads=[hgt_b],
                      writes=[self.db("h_sc")], st="hw")
            pend.append(lambda t0=t0: self._moe_gate(
                t0, hgt, hgt_b, wmg, wmg_b, stt2, stt2_b, gwt, gwt_b,
                gtmp, gtmp_b, idb, idb_b))
        for fn in pend:
            fn()
        pend.clear()

    def _moe_gate(self, t0, hgt, hgt_b, wmg, wmg_b, stt2, stt2_b,
                  gwt, gwt_b, gtmp, gtmp_b, idb, idb_b):
        S, P = self.S, self.P
        for ts4 in range(4):
            o = 16 * ts4
            pb, pbb = P["B%d" % (2 + ts4 % 2)]
            pairs = [(hgt[:, ci * GT + ts4 * 128: ci * GT + (ts4 + 1) * 128],
                      wmg[:, ci * 4:(ci + 1) * 4]) for ci in range(NCH)]
            self.mm(pb[:, 0:4], pairs, pbb, [hgt_b, wmg_b])
            self.red(stt2[:, o:o + 1], pb[:, 0:4], AL.max, [pbb], [stt2_b])
            self.ts("v", stt2[:, o + 4:o + 8], pb[:, 0:4], stt2[:, o:o + 1],
                    None, AL.subtract, AL.bypass, [pbb, stt2_b], [stt2_b])
            self.act(stt2[:, o + 8:o + 12], stt2[:, o + 4:o + 8], AF.Exp,
                     [stt2_b], [stt2_b])
            self.red(stt2[:, o + 1:o + 2], stt2[:, o + 8:o + 12], AL.add,
                     [stt2_b], [stt2_b])
            self.recip(stt2[:, o + 2:o + 3], stt2[:, o + 1:o + 2], [stt2_b],
                       [stt2_b])
            self.ts("v", gwt[:, 4 * (ts4 % 2): 4 * (ts4 % 2) + 4],
                    stt2[:, o + 8:o + 12], stt2[:, o + 2:o + 3], None,
                    AL.mult, AL.bypass, [stt2_b], [gwt_b])
            pt, ptb = P["T%d" % (ts4 % 2)]
            self.tr(pt[0:4, 0:128], gwt[:, 4 * (ts4 % 2): 4 * (ts4 % 2) + 4],
                    idb[:], ptb, [gwt_b, idb_b])
            self.cp("v", gtmp[:], pt[0:4, 0:128], [ptb], [gtmp_b])
            S.dma("sy", self.d["gw_sc"][0:4, t0 + ts4 * 128: t0 + (ts4 + 1) * 128],
                  gtmp[:], reads=[gtmp_b], writes=[self.db("gw_sc")], st="gww")

    def phase_m3(self, mark):
        S, sb, P, T = self.S, self.sb, self.P, self.T
        T2 = T // 2
        NT2 = T2 // GT
        onb, onb_b = self.c["onb"]
        b1c, b1c_b = self.c["b1c"]
        b2r, b2r_b = self.c["b2r"]
        sb.reset(mark)
        hh_, hh_b = sb.alloc("hhalf", [128, NCH * T2], BF16)
        acc, acc_b = sb.alloc("m3acc", [128, NCH * T2], F32)
        hidG, hidG_b = sb.alloc("hidG", [128, 8 * T2], BF16)
        w1b_, w1b_bb = sb.alloc("w1b", [128, 8 * D], BF16)
        w2b, w2b_b = sb.alloc("w2b", [128, 8 * D], BF16)
        gwb, gwb_b = sb.alloc("gwb", [128, T2], BF16)
        gw4, gw4_b = sb.alloc("gw4", [4, T2], BF16)
        gwr, gwr_b = sb.alloc("gwr", [1, 512], BF16)
        sil = [sb.alloc(f"sil{i}", [128, 512], F32) for i in range(2)]
        xts, xts_b = sb.alloc("xts", [128, 512], BF16)
        sts, sts_b = sb.alloc("sts", [128, 512], BF16)

        for hf in range(2):
            toff = hf * T2
            for c in range(NCH):
                S.dma("g", hh_[:, c * T2:(c + 1) * T2],
                      self.d["h_sc"][c * 128:(c + 1) * 128, toff:toff + T2],
                      reads=[self.db("h_sc")], writes=[hh_b], st="hr")
            S.dma("sy", gw4[:], self.d["gw_sc"][0:4, toff:toff + T2],
                  reads=[self.db("gw_sc")], writes=[gw4_b], st="gwr")
            for e in range(4):
                for tt in range(NT2):
                    S.dma("sy", gwr[:], self.d["gw_sc"][e, toff + tt * GT:
                                                        toff + (tt + 1) * GT]
                          .unsqueeze(0), reads=[self.db("gw_sc")], writes=[gwr_b], st="gwrow")
                    pb, pbb = P["B3"]
                    self.mm(pb[:], [(onb[0:1, 0:128], gwr[:])], pbb,
                            [onb_b, gwr_b])
                    self.cp("v", gwb[:, tt * GT:(tt + 1) * GT], pb[:], [pbb],
                            [gwb_b])
                T0f = P["T0"][0][:].bitcast(F32)
                T1f = P["T1"][0][:].bitcast(F32)
                psW = [(P["S0"][0][:], P["S0"][1]), (P["S1"][0][:], P["S1"][1]),
                       (T0f, P["T0"][1]), (T1f, P["T1"][1])]
                for jg in range(4):
                    wb_, wb_b = w1b_, w1b_bb
                    f0 = jg * 8
                    for ci in range(NCH):
                        S.dma("g", wb_[:, ci * D:(ci + 1) * D],
                              self.d["w1T"][e, ci * 128:(ci + 1) * 128,
                                            f0 * 128:(f0 + 8) * 128],
                              writes=[wb_b], st="w1")
                    for j in range(8):
                        for tt in range(NT2):
                            pb, pbb = P["B%d" % tt]
                            pairs = [(wb_[:, ci * D + j * 128: ci * D + (j + 1) * 128],
                                      hh_[:, ci * T2 + tt * GT: ci * T2 + (tt + 1) * GT])
                                     for ci in range(NCH)]
                            self.mm(pb[:], pairs, pbb, [wb_b, hh_b])
                            sl, sl_b = sil[tt % 2]
                            self.act(sl[:], pb[:], AF.Silu, [pbb, b1c_b],
                                     [sl_b], bias=b1c[:, e * 32 + f0 + j:
                                                      e * 32 + f0 + j + 1])
                            self.tt("v", AL.mult,
                                    hidG[:, j * T2 + tt * GT: j * T2 + (tt + 1) * GT],
                                    sl[:], gwb[:, tt * GT:(tt + 1) * GT],
                                    [sl_b, gwb_b], [hidG_b])
                    for j in range(8):
                        S.dma("g", w2b[:, j * D:(j + 1) * D],
                              self.d["w2T"][e, (f0 + j) * 128:(f0 + j + 1) * 128, :],
                              writes=[w2b_b], st="w2")
                    for co in range(NCH):
                        bias_here = (e == 0 and jg == 0)
                        for tt in range(NT2):
                            pairs = [(w2b[:, j * D + co * 128: j * D + (co + 1) * 128],
                                      hidG[:, j * T2 + tt * GT: j * T2 + (tt + 1) * GT])
                                     for j in range(8)]
                            rb = [w2b_b, hidG_b]
                            if bias_here:
                                pairs.append((b2r[0:4, co * 128:(co + 1) * 128],
                                              gw4[0:4, tt * GT:(tt + 1) * GT]))
                                rb += [b2r_b, gw4_b]
                            self.mm(psW[tt][0], pairs, psW[tt][1], rb)
                            asl = acc[:, co * T2 + tt * GT: co * T2 + (tt + 1) * GT]
                            if bias_here:
                                self.cp("v", asl, psW[tt][0], [psW[tt][1]],
                                        [acc_b])
                            else:
                                self.tt("v", AL.add, asl, asl, psW[tt][0],
                                        [acc_b, psW[tt][1]], [acc_b])
                            if e == 3 and jg == 3:
                                r = slice(co * 128, (co + 1) * 128)
                                tsl = slice(toff + tt * GT, toff + (tt + 1) * GT)
                                S.dma("g", xts[:], self.d["xpp_sc"][r, tsl],
                                      reads=[self.db("xpp_sc")], writes=[xts_b],
                                      st="xppr")
                                self.tt("v", AL.add, sts[:], asl, xts[:],
                                        [acc_b, xts_b], [sts_b])
                                S.dma("sy", self.d["s_sc"][r, tsl], sts[:],
                                      reads=[sts_b], writes=[self.db("s_sc")],
                                      st="sw")


    def phase_ffn(self, mark):
        S, sb, P, T = self.S, self.sb, self.P, self.T
        NT = T // GT
        onb, onb_b = self.c["onb"]
        idb, idb_b = self.c["idb"]
        idf, idf_b = self.c["idf"]
        bd1c, bd1c_b = self.c["bd1c"]
        bd2r, bd2r_b = self.c["bd2r"]
        bbc, bbc_b = self.c["bbc"]
        badr, badr_b = self.c["badr"]
        baur, baur_b = self.c["baur"]
        gbias, gbias_b = self.c["gbias"]
        selv, selv_b = self.c["selv"]
        bauc, bauc_b = self.c["bauc"]
        sb.reset(mark)
        xnb, xnb_b = sb.alloc("xnb", [128, NCH * T], BF16)
        wbase, wbase_b = sb.alloc("wbase", [128, NCH * D], BF16)
        wd1b = [sb.alloc(f"wd1b{i}", [128, NCH * 256], BF16) for i in range(2)]
        wd2b = [sb.alloc(f"wd2b{i}", [128, 2 * D], BF16) for i in range(2)]
        hid2, hid2_b = sb.alloc("hid2", [128, 2 * GT], BF16)
        jkf, jkf_b = sb.alloc("jkf", [128, GT], F32)
        wadeb, wadeb_b = sb.alloc("wadeb", [128, NCH * 64], BF16)
        waueb, waueb_b = sb.alloc("waueb", [64, D], BF16)
        af, af_b = sb.alloc("af", [64, T], BF16)
        gbrow, gbrow_b = sb.alloc("gbrow", [1, D], BF16)
        ohc, ohc_b = self.c["ohc"]
        sel_mark = sb.mark()
        lrx, lrx_b = sb.alloc("lrx", [128, NT * NCH], F32)
        lrr, lrr_b = sb.alloc("lrr", [128, NT * NCH], F32)
        LRX, LRX_b = sb.alloc("LRX", [128, 8], F32)
        LRR, LRR_b = sb.alloc("LRR", [128, 8], F32)
        LL, LL_b = sb.alloc("LL", [128, 8], F32)
        llt, llt_b = sb.alloc("llt", [8, 128], F32)
        lc8, lc8_b = sb.alloc("lc8", [8, 4], F32)
        lrw, lrw_b = sb.alloc("lrw", [1, 16], F32)
        ohr, ohr_b = sb.alloc("ohr", [1, 8], BF16)
        selb = [sb.alloc(f"selb{i}", [8, 4096], BF16) for i in range(2)]
        web, web_b = sb.alloc("web", [1, 4096], BF16)

        for c in range(NCH):
            S.dma("g", xnb[:, c * T:(c + 1) * T],
                  self.d["xn_sc"][c * 128:(c + 1) * 128, :],
                  reads=[self.db("xn_sc")], writes=[xnb_b], st="xnr")
            S.dma("g", wbase[:, c * D:(c + 1) * D],
                  self.d["Wbase"][c * 128:(c + 1) * 128, :], writes=[wbase_b],
                  st="wb")

        for e in range(8):
            wd1, wd1_b = wd1b[e % 2]
            wd2, wd2_b = wd2b[e % 2]
            for ci in range(NCH):
                S.dma("g", wd1[:, ci * 256:(ci + 1) * 256],
                      self.d["Wd1"][e, ci * 128:(ci + 1) * 128, :],
                      writes=[wd1_b], st="wd%d" % (e % 2))
            for f in range(2):
                S.dma("g", wd2[:, f * D:(f + 1) * D],
                      self.d["Wd2"][e, f * 128:(f + 1) * 128, :], writes=[wd2_b],
                      st="wd%d" % (e % 2))
            for tt in range(NT):
                tsl = slice(tt * GT, (tt + 1) * GT)
                for f in range(2):
                    pb, pbb = P["B%d" % f]
                    pairs = [(wd1[:, ci * 256 + f * 128: ci * 256 + (f + 1) * 128],
                              xnb[:, ci * T + tt * GT: ci * T + (tt + 1) * GT])
                             for ci in range(NCH)]
                    self.mm(pb[:], pairs, pbb, [wd1_b, xnb_b])
                    self.act(hid2[:, f * GT:(f + 1) * GT], pb[:], AF.Relu,
                             [pbb, bd1c_b], [hid2_b],
                             bias=bd1c[:, e * 2 + f: e * 2 + f + 1])
                for co in range(NCH):
                    pb, pbb = P["B%d" % (2 + co % 2)]
                    pairs = [(wd2[:, f * D + co * 128: f * D + (co + 1) * 128],
                              hid2[:, f * GT:(f + 1) * GT]) for f in range(2)]
                    pairs.append((bd2r[0:1, e * D + co * 128: e * D + (co + 1) * 128],
                                  onb[0:1, :]))
                    self.mm(pb[:], pairs, pbb, [wd2_b, hid2_b, bd2r_b, onb_b])
                    idx = tt * NCH + co
                    self.stt("v", jkf[:], pb[:], 1.0,
                             xnb[:, co * T + tt * GT: co * T + (tt + 1) * GT],
                             AL.mult, AL.mult, [pbb, xnb_b], [jkf_b, lrx_b],
                             accum=lrx[:, idx:idx + 1])
                    self.act(jkf[:], pb[:], AF.Square, [pbb], [jkf_b, lrr_b],
                             accum=lrr[:, idx:idx + 1])
            self.red(LRX[:, e:e + 1], lrx[:], AL.add, [lrx_b], [LRX_b])
            self.red(LRR[:, e:e + 1], lrr[:], AL.add, [lrr_b], [LRR_b])
        self.stt("v", LL[:], LRX[:], -2.0, LRR[:], AL.mult, AL.add,
                 [LRX_b, LRR_b], [LL_b])
        pf, pfb = P["B2"]
        self.tr(pf[0:8, 0:128], LL[:], idf[:], pfb, [LL_b, idf_b])
        self.cp("v", llt[:], pf[0:8, 0:128], [pfb], [llt_b])
        self.red(lc8[:, 0:1], llt[:], AL.add, [llt_b], [lc8_b])
        S.dma("sy", self.d["lrow_sc"], lc8[0:8, 0:1], reads=[lc8_b],
              writes=[self.db("lrow_sc")], st="selm")
        S.dma("sy", lrw[0:1, 0:8], self.d["lrow_sc"].unsqueeze(0),
              reads=[self.db("lrow_sc")], writes=[lrw_b], st="selm")
        self.red(lrw[0:1, 8:9], lrw[0:1, 0:8], AL.min, [lrw_b], [lrw_b])
        self.ts("v", ohr[:], lrw[0:1, 0:8], lrw[0:1, 8:9], None, AL.is_le,
                AL.bypass, [lrw_b], [ohr_b])
        S.dma("sy", self.d["oh_sc"], ohr[0:1, 0:8], reads=[ohr_b],
              writes=[self.db("oh_sc")], st="selm")
        S.dma("sy", ohc[0:8, 0:1], self.d["oh_sc"].unsqueeze(1),
              reads=[self.db("oh_sc")], writes=[ohc_b], st="selm")
        for (src, dst, dstb) in ((self.d["WadF"], self.d["wad_sc"],
                                  self.db("wad_sc")),
                                 (self.d["WauF"], self.d["wau_sc"],
                                  self.db("wau_sc"))):
            for ch in range(16):
                slb, slb_b = selb[ch % 2]
                S.dma("g", slb[:], src[:, ch * 4096:(ch + 1) * 4096],
                      writes=[slb_b], st="sel%d" % (ch % 2))
                for i in range(8):
                    pb, pbb = P["B%d" % (i % 4)]
                    self.mm(pb[0:1, :], [(ohc[0:8, 0:1],
                                          slb[0:8, i * 512:(i + 1) * 512])],
                            pbb, [ohc_b, slb_b])
                    self.cp("v", web[0:1, i * 512:(i + 1) * 512], pb[0:1, :],
                            [pbb], [web_b])
                S.dma("sy", dst[ch * 4096:(ch + 1) * 4096].unsqueeze(0),
                      web[:], reads=[web_b], writes=[dstb], st="selw")
        pb, pbb = P["B2"]
        self.mm(pb[0:64, 0:1], [(badr[0:8, :], ohc[0:8, 0:1])], pbb,
                [badr_b, ohc_b])
        self.cp("v", selv[0:64, 0:1], pb[0:64, 0:1], [pbb], [selv_b])
        for c in range(NCH):
            pb2, pb2b = P["B%d" % (3 - c % 2)]
            self.mm(pb2[:, 0:1], [(baur[0:8, c * 128:(c + 1) * 128],
                                   ohc[0:8, 0:1])], pb2b, [baur_b, ohc_b])
            self.cp("v", bauc[:, c:c + 1], pb2[:, 0:1], [pb2b], [bauc_b])
        self.tt("v", AL.add, gbias[:], bbc[:], bauc[:], [bbc_b, bauc_b],
                [gbias_b])
        pgb, pgbb = P["B3"]
        self.tr(pgb[0:8, 0:128], gbias[:, 0:8], idf[:], pgbb, [gbias_b, idf_b])
        gb8, gb8_b = sb.alloc("gb8", [8, 128], BF16)
        self.cp("v", gb8[:], pgb[0:8, 0:128], [pgbb], [gb8_b])
        S.dma("sy", self.da("gb_sc").rearrange("(c p) -> c p", p=128), gb8[:],
              reads=[gb8_b], writes=[self.db("gb_sc")], st="selw")
        S.dma("sy", gbrow[:], self.da("gb_sc").unsqueeze(0),
              reads=[self.db("gb_sc")], writes=[gbrow_b], st="selr")
        for c in range(NCH):
            S.dma("sy", wadeb[:, c * 64:(c + 1) * 64],
                  self.d["wad_sc"][c * 8192:(c + 1) * 8192]
                  .rearrange("(p r) -> p r", r=64),
                  reads=[self.db("wad_sc")], writes=[wadeb_b], st="selr")
        S.dma("sy", waueb[:], self.d["wau_sc"].rearrange("(p o) -> p o", o=D),
              reads=[self.db("wau_sc")], writes=[waueb_b], st="selr")
        S.barrier()
        sb.reset(sel_mark)
        sgs = [sb.alloc(f"sg{i}", [128, NCH * GT], BF16) for i in range(2)]
        stms = [sb.alloc(f"stm{i}", [128, 1024], BF16) for i in range(2)]
        gtms = [sb.alloc(f"gtm{i}", [128, 1024], BF16) for i in range(2)]
        prds = [sb.alloc(f"prd{i}", [128, 1024], F32) for i in range(2)]
        xtm2 = [sb.alloc(f"xtm2{i}", [128, D], F32) for i in range(4)]
        otm = [sb.alloc(f"otm{i}", [128, D], F32) for i in range(2)]
        for tt in range(NT):
            pb, pbb = P["B%d" % (tt % 2)]
            pairs = [(wadeb[:, ci * 64:(ci + 1) * 64],
                      xnb[:, ci * T + tt * GT: ci * T + (tt + 1) * GT])
                     for ci in range(NCH)]
            self.mm(pb[0:64, :], pairs, pbb, [wadeb_b, xnb_b])
            self.act(af[0:64, tt * GT:(tt + 1) * GT], pb[0:64, :], AF.Relu,
                     [pbb, selv_b], [af_b], bias=selv[0:64, 0:1])
        for tt in range(NT):
            t0 = tt * GT
            sg, sg_b = sgs[tt % 2]
            for c in range(NCH):
                S.dma("g", sg[:, c * GT:(c + 1) * GT],
                      self.d["s_sc"][c * 128:(c + 1) * 128, t0:t0 + GT],
                      reads=[self.db("s_sc")], writes=[sg_b], st="sr")
            for s4 in range(4):
                xt, xt_b = xtm2[s4]
                S.dma("g" if s4 % 2 == 0 else "sc", xt[:],
                      self.d["x"][t0 + s4 * 128: t0 + (s4 + 1) * 128, :],
                      writes=[xt_b], st="x2%d" % (s4 % 2))
            for s4 in range(4):
                ts0 = t0 + s4 * 128
                gtm, gtm_b = gtms[s4 % 2]
                for hf in range(2):
                    pb, pbb = P["B%d" % ((s4 * 2 + hf) % 4)]
                    pairs = [(xnb[:, ci * T + ts0: ci * T + ts0 + 128],
                              wbase[:, ci * D + hf * 512: ci * D + hf * 512 + 512])
                             for ci in range(NCH)]
                    pairs.append((af[0:64, ts0:ts0 + 128],
                                  waueb[0:64, hf * 512: hf * 512 + 512]))
                    pairs.append((onb[0:1, 0:128],
                                  gbrow[0:1, hf * 512: hf * 512 + 512]))
                    self.mm(pb[:], pairs, pbb,
                            [wbase_b, xnb_b, waueb_b, af_b, onb_b, gbrow_b])
                    self.act(gtm[:, hf * 512: hf * 512 + 512], pb[:], AF.Silu,
                             [pbb], [gtm_b])
                pt, ptb = P["T%d" % (s4 % 2)]
                stm, stm_b = stms[s4 % 2]
                prd, prd_b = prds[s4 % 2]
                for c in range(NCH):
                    self.tr(pt[:, c * 128:(c + 1) * 128],
                            sg[:, c * GT + s4 * 128: c * GT + (s4 + 1) * 128],
                            idb[:], ptb, [sg_b, idb_b])
                self.acp(stm[:], pt[:], [ptb], [stm_b])
                xt, xt_b = xtm2[s4]
                ot, ot_b = otm[s4 % 2]
                self.tt("v", AL.mult, prd[:], stm[:], gtm[:],
                        [stm_b, gtm_b], [prd_b])
                self.tt("v", AL.add, ot[:], xt[:], prd[:], [xt_b, prd_b],
                        [ot_b])
                S.dma("sy", self.d["out"][t0 + s4 * 128: t0 + (s4 + 1) * 128, :],
                      ot[:], reads=[ot_b], writes=[self.db("out")], st="out")


def host_prep(inputs, T=None):
    gi = {k: np.asarray(v) for k, v in inputs.items()}
    x = gi["L0_attn_x"]
    B = x.shape[0]
    Tfull = x.shape[1]
    T = T or Tfull
    f32 = np.float32
    bf = ml_dtypes.bfloat16

    com = {}
    com["wdwp"] = np.ascontiguousarray(
        gi["L0_attn_w_dw"].astype(f32).reshape(8, 128, 3).transpose(1, 2, 0)
        .reshape(128, 24))
    com["wpwT"] = np.ascontiguousarray(gi["L0_attn_w_pw"].T.astype(f32))
    com["wqkvT"] = np.ascontiguousarray(gi["L0_attn_w_qkv"].T.astype(f32))
    com["woT"] = np.ascontiguousarray(gi["L0_attn_w_o"].T.astype(f32))
    com["wgT"] = np.ascontiguousarray(gi["L0_attn_w_gate"].T.astype(f32))
    com["wmgT"] = np.ascontiguousarray(gi["L0_attn_w_moe_gate"].T.astype(f32))
    com["w1T"] = np.ascontiguousarray(gi["L0_attn_w1"].transpose(0, 2, 1).astype(f32))
    com["w2T"] = np.ascontiguousarray(gi["L0_attn_w2"].transpose(0, 2, 1).astype(f32))
    com["b1p"] = np.ascontiguousarray(
        gi["L0_attn_b1"].astype(f32).reshape(4, 32, 128).transpose(2, 0, 1)
        .reshape(128, 128))
    com["b2"] = gi["L0_attn_b2"].astype(f32)
    com["bd1p"] = np.ascontiguousarray(
        gi["L0_ffn_bd1"].astype(f32).reshape(8, 2, 128).transpose(2, 0, 1)
        .reshape(128, 16))
    com["bvecs"] = np.ascontiguousarray(np.concatenate(
        [np.asarray(gi[k]).astype(f32).reshape(8, 128).T for k in
         ("L0_attn_b_gate", "L0_attn_ln_g", "L0_attn_ln_b", "L0_ffn_b_base")],
        axis=1))
    com["Wbase"] = np.ascontiguousarray(gi["L0_ffn_W_base"].astype(f32))
    com["Wd1"] = np.ascontiguousarray(gi["L0_ffn_Wd1"].astype(f32))
    com["Wd2"] = np.ascontiguousarray(gi["L0_ffn_Wd2"].astype(f32))
    com["bd2"] = gi["L0_ffn_bd2"].astype(f32)
    com["WadF"] = np.ascontiguousarray(gi["L0_ffn_Wa_down"].reshape(8, -1).astype(f32))
    com["bad"] = gi["L0_ffn_ba_down"].astype(f32)
    com["WauF"] = np.ascontiguousarray(gi["L0_ffn_Wa_up"].reshape(8, -1).astype(f32))
    com["bau"] = gi["L0_ffn_ba_up"].astype(f32)

    com["identb"] = np.eye(128, dtype=bf)
    com["identf"] = np.eye(128, dtype=f32)
    com["utmask"] = np.ascontiguousarray(
        np.tile(np.triu(np.ones((128, 128), np.float32)), (1, 4))).astype(bf)
    com["onesb"] = np.ones((128, 512), bf)
    inv = (1.0 / 10000.0 ** (np.arange(0, DH, 2, dtype=np.float64) / DH))
    pos = np.arange(T, dtype=np.float64)
    invf = inv.astype(np.float32).astype(np.float64)
    posf = pos.astype(np.float32).astype(np.float64)
    th = (posf[None, :] * invf[:, None]).astype(np.float32).astype(np.float64)
    sn = np.sin(th).astype(np.float32)
    cs = np.cos(th).astype(np.float32)
    C64 = np.repeat(cs, 2, axis=0)
    S64 = np.empty((64, T), np.float32)
    S64[0::2] = -sn
    S64[1::2] = sn
    com["ropeC"] = np.concatenate([C64, C64], 0).astype(bf)
    com["ropeS"] = np.concatenate([S64, S64], 0).astype(bf)

    in_maps = []
    for b in range(B):
        m = dict(com)
        m["x"] = np.ascontiguousarray(x[b, :T].astype(f32))
        in_maps.append(m)
    return in_maps


_CACHED = {}


def kernel(**inputs):
    T = int(np.asarray(inputs["L0_attn_x"]).shape[1])
    in_maps = host_prep(inputs, T)
    key = ("k", T)
    if key not in _CACHED:
        kb = KB(T)
        _CACHED[key] = kb.build()
    nc = _CACHED[key]
    res = run_bass_kernel_spmd(nc, in_maps, core_ids=list(range(8)))
    out = np.stack([np.asarray(r["out"]) for r in res.results], axis=0)
    return out.astype(np.float32)


# version 34 (final)
# speedup vs baseline: 1.1572x; 1.1548x over previous
import numpy as np
import ml_dtypes
import concourse.bass as bass
import concourse.mybir as mybir
import concourse.bass_utils as _bass_utils
from concourse.bass_utils import run_bass_kernel_spmd

if not getattr(_bass_utils, "_ldw_opt_patched", False):
    _orig_walrus_args = _bass_utils.get_walrus_args

    def _walrus_args_ldw(arch, tmpdir, **kw):
        return [a.replace("--enable-ldw-opt=false", "--enable-ldw-opt=true")
                for a in _orig_walrus_args(arch, tmpdir, **kw)]

    _bass_utils.get_walrus_args = _walrus_args_ldw
    _bass_utils._ldw_opt_patched = True

F32 = mybir.dt.float32
BF16 = mybir.dt.bfloat16
AF = mybir.ActivationFunctionType
AL = mybir.AluOpType
AX = mybir.AxisListType

D = 1024
NCH = 8
NH = 16
DH = 64
GT = 512
CT = 128
SWAP_MASK = [i ^ 1 for i in range(32)]


def _dtsz(dt):
    return 2 if dt == BF16 else 4


class Buf:
    __slots__ = ("name", "w", "readers", "excl")

    def __init__(self, name, excl=False):
        self.name = name
        self.w = None
        self.readers = []
        self.excl = excl


class Sched:
    ENGS = ("t", "v", "sc", "g", "sy")

    def __init__(self, nc):
        self.nc = nc
        self.q = {e: [] for e in self.ENGS}
        self.sems = {}
        for k in self.ENGS:
            self.sems[k] = nc.alloc_semaphore("S_" + k)
        self.cnt = {k: 0 for k in self.sems}
        self.waited = {e: {} for e in self.ENGS}
        self.ninstr = 0
        self.dma_bumps = 0

    def _need(self, eng, tok):
        key, val = tok
        if key.startswith("d:"):
            cur = self.cnt[key]
            if cur > val:
                self.dma_bumps += 1
                val = cur
        w = self.waited[eng]
        if val <= w.get(key, 0):
            return
        w[key] = val
        sem = self.sems[key]
        self.q[eng].append(lambda E, s=sem, v=val: E.wait_ge(s, v))

    def _deps(self, eng, reads, writes):
        toks = set()
        for b in reads:
            if b.w is not None:
                toks.add(b.w)
            if b.excl and False:
                for tok in b.readers:
                    if tok[0] != eng:
                        toks.add(tok)
        for b in writes:
            if b.w is not None:
                toks.add(b.w)
            toks.update(b.readers)
        for tok in sorted(toks):
            self._need(eng, tok)

    def _finish(self, tok, reads, writes):
        for b in writes:
            b.w = tok
            b.readers = []
        for b in reads:
            if b.w is not tok:
                b.readers.append(tok)

    def emit(self, eng, fns, reads=(), writes=()):
        if not isinstance(fns, (list, tuple)):
            fns = [fns]
        self._deps(eng, reads, writes)
        self.cnt[eng] += 1
        tok = (eng, self.cnt[eng])
        sem = self.sems[eng]
        n = len(fns)
        self.ninstr += n
        for i, f in enumerate(fns):
            if i == n - 1:
                self.q[eng].append(lambda E, f=f, s=sem: f(E).then_inc(s, 1))
            else:
                self.q[eng].append(lambda E, f=f: f(E))
        self._finish(tok, reads, writes)
        return tok

    def emit_chains(self, eng, chains, rbufs):
        wbufs = [c[0] for c in chains]
        self._deps(eng, rbufs, wbufs)
        sem = self.sems[eng]
        nsteps = max(len(c[1]) for c in chains)
        toks = []
        for i in range(nsteps):
            for (buf, quads) in chains:
                if i >= len(quads):
                    continue
                o, l, r = quads[i]
                st = (i == 0)
                sp = (i == len(quads) - 1)
                self.ninstr += 1
                if sp:
                    self.cnt[eng] += 1
                    tok = (eng, self.cnt[eng])
                    toks.append((buf, tok))
                    self.q[eng].append(
                        lambda E, o=o, l=l, r=r, a=st, s=sem:
                        E.matmul(o, l, r, start=a, stop=True,
                                 skip_group_check=True).then_inc(s, 1))
                else:
                    self.q[eng].append(
                        lambda E, o=o, l=l, r=r, a=st:
                        E.matmul(o, l, r, start=a, stop=False,
                                 skip_group_check=True))
        last = toks[-1][1]
        for buf, tok in toks:
            buf.w = tok
            buf.readers = []
        for b in rbufs:
            b.readers.append(last)

    def dma(self, qn, out_ap, in_ap, reads=(), writes=(), st="m"):
        key = f"d:{qn}:{st}"
        if key not in self.sems:
            self.sems[key] = self.nc.alloc_semaphore("S" + key)
            self.cnt[key] = 0
        self._deps(qn, reads, writes)
        self.cnt[key] += 16
        tok = (key, self.cnt[key])
        sem = self.sems[key]
        self.ninstr += 1
        self.q[qn].append(
            lambda E, o=out_ap, i=in_ap, s=sem: E.dma_start(out=o, in_=i).then_inc(s, 16))
        self._finish(tok, reads, writes)
        return tok

    def barrier(self):
        snap = dict(self.cnt)
        for e in self.ENGS:
            for key, val in snap.items():
                if val > 0:
                    self._need(e, (key, val))

    def finalize(self):
        for key, val in self.cnt.items():
            if val > 0:
                self._need("sy", (key, val))

    def replay(self):
        with self.nc.Block() as blk:
            blk.tensor(lambda E: [f(E) for f in self.q["t"]])
            blk.vector(lambda E: [f(E) for f in self.q["v"]])
            blk.scalar(lambda E: [f(E) for f in self.q["sc"]])
            blk.gpsimd(lambda E: [f(E) for f in self.q["g"]])
            blk.sync(lambda E: [f(E) for f in self.q["sy"]])


class SB:
    def __init__(self, nc):
        self.nc = nc
        self.off = nc.sbuf_base + 4096
        self.cap = nc.sbuf_top

    def alloc(self, name, shape, dtype):
        nbytes = int(np.prod(shape[1:])) * _dtsz(dtype)
        self.off = (self.off + 63) & ~63
        assert self.off + nbytes <= self.cap, \
            f"SBUF overflow at {name}: {self.off}+{nbytes} > {self.cap}"
        t = self.nc.alloc_sbuf_tensor_at(name, list(shape), dtype, offset=self.off)
        self.off += nbytes
        return t, Buf(name)

    def mark(self):
        return self.off

    def reset(self, mark):
        self.off = mark


class KB:
    def __init__(self, T, taps=()):
        assert T % 1024 == 0
        self.T = T
        self.taps = tuple(taps)
        self.nc = bass.Bass("TRN2", target_bir_lowering=False)
        self.S = Sched(self.nc)
        self.sb = SB(self.nc)
        self.dram = {}
        self.NG = T // GT

    def din(self, name, shape, dtype=F32):
        ap = self.nc.dram_tensor(name, list(shape), dtype, kind="ExternalInput").ap()
        self.dram[name] = (ap, Buf(name))
        return ap

    def dout(self, name, shape, dtype=F32):
        ap = self.nc.dram_tensor(name, list(shape), dtype, kind="ExternalOutput").ap()
        self.dram[name] = (ap, Buf(name))
        return ap

    def dscr(self, name, shape, dtype=BF16):
        ap = self.nc.dram_tensor(name, list(shape), dtype, kind="Internal").ap()
        self.dram[name] = (ap, Buf(name))
        return ap

    def db(self, name):
        return self.dram[name][1]

    def da(self, name):
        return self.dram[name][0]

    def mm(self, psum_ap, pairs, psum_buf, rbufs, start=True, stop=True, skip=False):
        fns = []
        n = len(pairs)
        for i, (lt, rh) in enumerate(pairs):
            st = start and (i == 0)
            sp = stop and (i == n - 1)
            fns.append(lambda E, o=psum_ap, l=lt, r=rh, a=st, b=sp, k=skip:
                       E.matmul(o, l, r, start=a, stop=b, skip_group_check=k))
        return self.S.emit("t", fns, reads=rbufs, writes=[psum_buf])

    def mmx(self, quads, rbufs, wbufs):
        fns = [lambda E, o=o, l=l, r=r, a=a, b=b:
               E.matmul(o, l, r, start=a, stop=b, skip_group_check=True)
               for (o, l, r, a, b) in quads]
        return self.S.emit("t", fns, reads=rbufs, writes=wbufs)

    def acp(self, out_ap, in_ap, rbufs, wbufs):
        return self.act(out_ap, in_ap, AF.Copy, rbufs, wbufs)

    def tr(self, psum_ap, in_ap, ident_ap, psum_buf, rbufs):
        return self.S.emit(
            "t", lambda E, o=psum_ap, i=in_ap, d=ident_ap: E.transpose(o, i, d),
            reads=rbufs, writes=[psum_buf])

    def cp(self, eng, out_ap, in_ap, rbufs, wbufs):
        return self.S.emit(eng, lambda E, o=out_ap, i=in_ap: E.tensor_copy(o, i),
                           reads=rbufs, writes=wbufs)

    def tt(self, eng, op, out_ap, a_ap, b_ap, rbufs, wbufs):
        return self.S.emit(
            eng, lambda E, o=out_ap, a=a_ap, b=b_ap, p=op:
            E.tensor_tensor(o, a, b, op=p), reads=rbufs, writes=wbufs)

    def ts(self, eng, out_ap, in_ap, s1, s2, op0, op1, rbufs, wbufs):
        return self.S.emit(
            eng, lambda E, o=out_ap, i=in_ap, a=s1, b=s2, p=op0, q=op1:
            E.tensor_scalar(o, i, a, b, p, q), reads=rbufs, writes=wbufs)

    def stt(self, eng, out_ap, in0_ap, scalar, in1_ap, op0, op1, rbufs, wbufs,
            accum=None):
        return self.S.emit(
            eng, lambda E, o=out_ap, i=in0_ap, s=scalar, j=in1_ap, p=op0, q=op1,
            a=accum: E.scalar_tensor_tensor(o, i, s, j, p, q, accum_out=a),
            reads=rbufs, writes=wbufs)

    def act(self, out_ap, in_ap, func, rbufs, wbufs, bias=0.0, scale=1.0, accum=None):
        return self.S.emit(
            "sc", lambda E, o=out_ap, i=in_ap, f=func, b=bias, s=scale, a=accum:
            E.activation(out=o, in_=i, func=f, bias=b, scale=s, accum_out=a),
            reads=rbufs, writes=wbufs)

    def red(self, out_ap, in_ap, op, rbufs, wbufs):
        return self.S.emit(
            "v", lambda E, o=out_ap, i=in_ap, p=op: E.tensor_reduce(o, i, AX.X, p),
            reads=rbufs, writes=wbufs)

    def recip(self, out_ap, in_ap, rbufs, wbufs):
        return self.S.emit(
            "v", lambda E, o=out_ap, i=in_ap: E.reciprocal(o, i),
            reads=rbufs, writes=wbufs)

    def shuf(self, out_ap, in_ap, rbufs, wbufs):
        return self.S.emit(
            "v", lambda E, o=out_ap, i=in_ap: E.stream_shuffle(o, i, SWAP_MASK),
            reads=rbufs, writes=wbufs)

    def memset(self, eng, ap, val, wbufs):
        return self.S.emit(eng, lambda E, a=ap, v=val: E.memset(a, v), writes=wbufs)

    def ttr(self, out_ap, in0_ap, in1_ap, accum_ap, rbufs, wbufs):
        return self.S.emit(
            "v", lambda E, o=out_ap, a=in0_ap, b=in1_ap, c=accum_ap:
            E.tensor_tensor_reduce(o, a, b, 1.0, 0.0, AL.mult, AL.add, c),
            reads=rbufs, writes=wbufs)

    def build(self):
        nc, S, sb, T = self.nc, self.S, self.sb, self.T
        NG = self.NG
        T2 = T // 2
        NT2 = T2 // GT

        x_in = self.din("x", [T, D])
        wdwp_d = self.din("wdwp", [128, 24])
        bvecs_d = self.din("bvecs", [128, 32])
        b1p_d = self.din("b1p", [128, 128])
        bd1p_d = self.din("bd1p", [128, 16])
        wpwT_d = self.din("wpwT", [D, D])
        wqkvT_d = self.din("wqkvT", [D, 3 * D])
        woT_d = self.din("woT", [D, D])
        wgT_d = self.din("wgT", [D, D])
        wmgT_d = self.din("wmgT", [D, 4])
        w1T_d = self.din("w1T", [4, D, 4 * D])
        w2T_d = self.din("w2T", [4, 4 * D, D])
        b2_d = self.din("b2", [4, D])
        Wbase_d = self.din("Wbase", [D, D])
        Wd1_d = self.din("Wd1", [8, D, 256])
        Wd2_d = self.din("Wd2", [8, 256, D])
        bd2_d = self.din("bd2", [8, D])
        WadF_d = self.din("WadF", [8, D * 64])
        bad_d = self.din("bad", [8, 64])
        WauF_d = self.din("WauF", [8, 64 * D])
        bau_d = self.din("bau", [8, D])
        identb_d = self.din("identb", [128, 128], BF16)
        identf_d = self.din("identf", [128, 128], F32)
        utmask_d = self.din("utmask", [128, 512], BF16)
        ropeC_d = self.din("ropeC", [128, T], BF16)
        ropeS_d = self.din("ropeS", [128, T], BF16)
        ones_d = self.din("onesb", [128, 512], BF16)
        out_d = self.dout("out", [T, D])

        xn_sc = self.dscr("xn_sc", [D, T])
        xp_sc = self.dscr("xp_sc", [D, T])
        xpp_sc = self.dscr("xpp_sc", [D, T])
        h_sc = self.dscr("h_sc", [D, T])
        s_sc = self.dscr("s_sc", [D, T])
        gw_sc = self.dscr("gw_sc", [4, T])
        wad_sc = self.dscr("wad_sc", [D * 64])
        wau_sc = self.dscr("wau_sc", [64 * D])
        lrow_sc = self.dscr("lrow_sc", [8], F32)
        gb_sc = self.dscr("gb_sc", [D], BF16)
        oh_sc = self.dscr("oh_sc", [8], BF16)
        for nm in self.taps:
            self.dout("tap_" + nm, [D, T], BF16)

        P = {}
        for i in range(4):
            P[f"B{i}"] = (nc.alloc_psum_tensor(f"PB{i}", [128, 512], F32),
                          Buf(f"PB{i}", excl=True))
        P["S0"] = (nc.alloc_psum_tensor("PS0", [128, 512], F32), Buf("PS0", excl=True))
        P["S1"] = (nc.alloc_psum_tensor("PS1", [128, 512], F32), Buf("PS1", excl=True))
        P["T0"] = (nc.alloc_psum_tensor("PT0", [128, 1024], BF16), Buf("PT0", excl=True))
        P["T1"] = (nc.alloc_psum_tensor("PT1", [128, 1024], BF16), Buf("PT1", excl=True))
        self.P = P

        idb, idb_b = sb.alloc("idb", [128, 128], BF16)
        idf, idf_b = sb.alloc("idf", [128, 128], F32)
        utm, utm_b = sb.alloc("utm", [128, 512], BF16)
        onb, onb_b = sb.alloc("onb", [128, 512], BF16)
        wdwc, wdwc_b = sb.alloc("wdwc", [128, 3 * NCH], F32)
        bg_n, bg_n_b = sb.alloc("bg_n", [128, NCH], F32)
        lngc, lngc_b = sb.alloc("lngc", [128, NCH], F32)
        lnbc, lnbc_b = sb.alloc("lnbc", [128, NCH], F32)
        b1c, b1c_b = sb.alloc("b1c", [128, 4 * 32], F32)
        bd1c, bd1c_b = sb.alloc("bd1c", [128, 16], F32)
        bd2r, bd2r_b = sb.alloc("bd2r", [1, 8 * D], BF16)
        b2r, b2r_b = sb.alloc("b2r", [4, D], BF16)
        bbc, bbc_b = sb.alloc("bbc", [128, NCH], F32)
        badr, badr_b = sb.alloc("badr", [8, 64], BF16)
        baur, baur_b = sb.alloc("baur", [8, D], BF16)
        gbias, gbias_b = sb.alloc("gbias", [128, NCH], F32)
        selv, selv_b = sb.alloc("selv", [128, 2], F32)
        bauc, bauc_b = sb.alloc("bauc", [128, NCH], F32)
        S_sb, S_sb_b = sb.alloc("S_sb", [128, 8 * 65], BF16)
        ohc, ohc_b = sb.alloc("ohc", [8, 2], BF16)
        kon, kon_b = sb.alloc("kon", [128, 4], F32)
        qkv_mark = sb.mark()
        wqv, wqv_b = sb.alloc("wq", [128, NCH * D], BF16)
        wkv, wkv_b = sb.alloc("wk", [128, NCH * D], BF16)
        wvv, wvv_b = sb.alloc("wv", [128, NCH * D], BF16)
        perm_mark = sb.mark()
        self.memset("v", kon[:, 0:1], 1e-5, [kon_b])
        self.memset("v", kon[:, 1:2], -0.5, [kon_b])
        self.memset("v", kon[:, 2:3], -1.0, [kon_b])
        self.memset("v", kon[:, 3:4], 1e-6, [kon_b])

        S.dma("sy", idb[:], identb_d, writes=[idb_b], st="pre")
        S.dma("sy", idf[:], identf_d, writes=[idf_b], st="pre")
        S.dma("sy", utm[:], utmask_d, writes=[utm_b], st="pre")
        S.dma("sy", onb[:], ones_d, writes=[onb_b], st="pre")
        S.dma("g", b2r[:], b2_d, writes=[b2r_b], st="pre")
        S.dma("g", bd2r[:], bd2_d.rearrange("e d -> (e d)").unsqueeze(0),
              writes=[bd2r_b], st="pre")
        S.dma("g", badr[:], bad_d, writes=[badr_b], st="pre")
        S.dma("g", baur[:], bau_d, writes=[baur_b], st="pre")
        S.dma("sy", wdwc[:], wdwp_d, writes=[wdwc_b], st="pre")
        S.dma("sy", bg_n[:], bvecs_d[:, 0:8], writes=[bg_n_b], st="pre")
        S.dma("sy", lngc[:], bvecs_d[:, 8:16], writes=[lngc_b], st="pre")
        S.dma("sy", lnbc[:], bvecs_d[:, 16:24], writes=[lnbc_b], st="pre")
        S.dma("sy", bbc[:], bvecs_d[:, 24:32], writes=[bbc_b], st="pre")
        S.dma("sy", b1c[:], b1p_d, writes=[b1c_b], st="pre")
        S.dma("sy", bd1c[:], bd1p_d, writes=[bd1c_b], st="pre")


        self.c = dict(idb=(idb, idb_b), idf=(idf, idf_b), utm=(utm, utm_b),
                      onb=(onb, onb_b), wdwc=(wdwc, wdwc_b), bg_n=(bg_n, bg_n_b),
                      lngc=(lngc, lngc_b), lnbc=(lnbc, lnbc_b), b1c=(b1c, b1c_b),
                      bd1c=(bd1c, bd1c_b), bd2r=(bd2r, bd2r_b), b2r=(b2r, b2r_b),
                      bbc=(bbc, bbc_b), badr=(badr, badr_b), baur=(baur, baur_b),
                      gbias=(gbias, gbias_b), selv=(selv, selv_b),
                      bauc=(bauc, bauc_b), S_sb=(S_sb, S_sb_b), ohc=(ohc, ohc_b),
                      kon=(kon, kon_b), wq=(wqv, wqv_b), wk=(wkv, wkv_b),
                      wv=(wvv, wvv_b))
        self.d = dict(x=x_in, wpwT=wpwT_d, wqkvT=wqkvT_d, woT=woT_d, wgT=wgT_d,
                      wmgT=wmgT_d, w1T=w1T_d, w2T=w2T_d, Wbase=Wbase_d,
                      Wd1=Wd1_d, Wd2=Wd2_d, WadF=WadF_d, WauF=WauF_d,
                      ropeC=ropeC_d, ropeS=ropeS_d, out=out_d,
                      xn_sc=xn_sc, xp_sc=xp_sc, xpp_sc=xpp_sc, h_sc=h_sc,
                      s_sc=s_sc, gw_sc=gw_sc, wad_sc=wad_sc, wau_sc=wau_sc,
                      lrow_sc=lrow_sc, oh_sc=oh_sc)

        self.phase_p0m1(perm_mark)
        S.barrier()
        self.phase_m2(perm_mark)
        S.barrier()
        self.phase_m3(qkv_mark)
        S.barrier()
        self.phase_ffn(qkv_mark)
        for nm in self.taps:
            src = self.d[nm + "_sc"]
            S.dma("sy", self.da("tap_" + nm), src, reads=[self.db(nm + "_sc")],
                  writes=[self.db("tap_" + nm)], st="tap")
        S.finalize()
        S.replay()
        return nc

    def phase_p0m1(self, mark):
        S, sb, P, T = self.S, self.sb, self.P, self.T
        NG = self.NG
        idb, idb_b = self.c["idb"]
        wdwc, wdwc_b = self.c["wdwc"]
        sb.reset(mark)
        xtm = [sb.alloc(f"xtm{i}", [128, 4 * D], F32) for i in range(2)]
        xcv2 = [[sb.alloc(f"xcv{p}_{c}", [128, 514], BF16) for c in range(NCH)]
                for p in range(2)]
        xout, xout_b = sb.alloc("xout", [128, 2 * GT], BF16)
        wpw, wpw_b = sb.alloc("wpw", [128, NCH * D], BF16)
        wpw3 = [sb.alloc(f"wpw3_{k}", [128, NCH * D], BF16) for k in range(3)]
        stats = [sb.alloc(f"stat{i}", [128, 16], F32) for i in range(4)]
        xnt = [sb.alloc(f"xnt{i}", [128, D], BF16) for i in range(4)]

        for ci in range(NCH):
            S.dma("g", wpw[:, ci * D:(ci + 1) * D],
                  self.d["wpwT"][ci * 128:(ci + 1) * 128, :], writes=[wpw_b],
                  st="wpw")
        for k in range(3):
            for ci in range(NCH):
                self.ts("v", wpw3[k][0][:, ci * D:(ci + 1) * D],
                        wpw[:, ci * D:(ci + 1) * D],
                        wdwc[:, k * NCH + ci: k * NCH + ci + 1], None,
                        AL.mult, AL.bypass, [wpw_b, wdwc_b], [wpw3[k][1]])
        for c in range(NCH):
            self.memset("v", xcv2[0][c][0][:, 0:2], 0.0, [xcv2[0][c][1]])

        for g in range(NG):
            t0 = g * GT
            xt, xt_b = xtm[g % 2]
            S.dma("g", xt[:].rearrange("p (s d) -> p s d", d=D),
                  self.d["x"][t0:t0 + GT, :].rearrange("(s p) d -> p s d", p=128),
                  writes=[xt_b], st="x%d" % (g % 2))
            for wi2, nm in enumerate(("wq", "wk", "wv")):
                wt_, wt_b = self.c[nm]
                S.dma("g", wt_[:, g * D:(g + 1) * D],
                      self.d["wqkvT"][g * 128:(g + 1) * 128,
                                      wi2 * D:(wi2 + 1) * D],
                      writes=[wt_b], st="wqkv")
            xcv = xcv2[g % 2]
            if g > 0:
                prev = xcv2[(g - 1) % 2]
                for c in range(NCH):
                    self.cp("v", xcv[c][0][:, 0:2], prev[c][0][:, 512:514],
                            [prev[c][1]], [xcv[c][1]])
            kon, kon_b = self.c["kon"]
            for s4 in range(4):
                xv = xt[:, s4 * D:(s4 + 1) * D]
                stat, stat_b = stats[s4]
                self.S.emit("v", lambda E, o=stat[:, 0:6], i=xv[:, 0:512]:
                            E.bn_stats(o, i), reads=[xt_b], writes=[stat_b])
                self.S.emit("v", lambda E, o=stat[:, 6:12], i=xv[:, 512:1024]:
                            E.bn_stats(o, i), reads=[xt_b], writes=[stat_b])
                self.S.emit("v", lambda E, o=stat[:, 12:14], i=stat[:, 0:12]:
                            E.bn_aggr(o, i), reads=[stat_b], writes=[stat_b])
                self.act(stat[:, 14:15], stat[:, 13:14], AF.Ln, [stat_b, kon_b],
                         [stat_b], bias=kon[:, 0:1])
                self.act(stat[:, 14:15], stat[:, 14:15], AF.Exp, [stat_b, kon_b],
                         [stat_b], scale=kon[:, 1:2])
                xn, xn_b = xnt[s4]
                self.ts("v", xn[:], xv, stat[:, 12:13], stat[:, 14:15],
                        AL.subtract, AL.mult, [xt_b, stat_b], [xn_b])
            for s4 in range(4):
                xn, xn_b = xnt[s4]
                for c in range(NCH):
                    pt, ptb = P["T%d" % ((s4 * NCH + c) % 2)]
                    self.tr(pt[:, 0:128], xn[:, c * 128:(c + 1) * 128], idb[:],
                            ptb, [xn_b, idb_b])
                    dstv = xcv[c][0][:, 2 + s4 * 128: 2 + (s4 + 1) * 128]
                    if c % 2 == 0:
                        self.acp(dstv, pt[:, 0:128], [ptb], [xcv[c][1]])
                    else:
                        self.cp("v", dstv, pt[:, 0:128], [ptb], [xcv[c][1]])
            for c in range(NCH):
                S.dma("sy", self.d["xn_sc"][c * 128:(c + 1) * 128, t0:t0 + GT],
                      xcv[c][0][:, 2:514], reads=[xcv[c][1]],
                      writes=[self.db("xn_sc")], st="xnw")
            for co in range(NCH):
                pb, pbb = P["B%d" % (co % 2)]
                pairs = []
                for k, sh in ((2, 2), (1, 1), (0, 0)):
                    wv, wv_b = wpw3[k]
                    for ci in range(NCH):
                        pairs.append(
                            (wv[:, ci * D + co * 128: ci * D + (co + 1) * 128],
                             xcv[ci][0][:, sh: sh + 512]))
                self.mm(pb[:], pairs, pbb,
                        [wpw3[0][1], wpw3[1][1], wpw3[2][1]] +
                        [xcv[ci][1] for ci in range(NCH)])
                xo = xout[:, (co % 2) * GT:(co % 2 + 1) * GT]
                self.tt("v", AL.add, xo, pb[:], xcv[co][0][:, 2:514],
                        [pbb, xcv[co][1]], [xout_b])
                S.dma("sy", self.d["xp_sc"][co * 128:(co + 1) * 128, t0:t0 + GT],
                      xo, reads=[xout_b], writes=[self.db("xp_sc")], st="xpw")

    def phase_m2(self, mark):
        S, sb, P, T = self.S, self.sb, self.P, self.T
        NG = self.NG
        idb, idb_b = self.c["idb"]
        utm, utm_b = self.c["utm"]
        onb, onb_b = self.c["onb"]
        bg_n, bg_n_b = self.c["bg_n"]
        lngc, lngc_b = self.c["lngc"]
        lnbc, lnbc_b = self.c["lnbc"]
        S_sb, S_sb_b = self.c["S_sb"]
        sb.reset(mark)
        wq, wq_b = self.c["wq"]
        wk, wk_b = self.c["wk"]
        wv, wv_b = self.c["wv"]
        wo, wo_b = sb.alloc("wo", [128, NCH * D], BF16)
        wg, wg_b = sb.alloc("wg", [128, NCH * D], BF16)
        wmg, wmg_b = sb.alloc("wmg", [128, NCH * 4], BF16)
        rcg = [sb.alloc(f"rcg{i}", [128, GT], BF16) for i in range(2)]
        rsg = [sb.alloc(f"rsg{i}", [128, GT], BF16) for i in range(2)]
        xg = [sb.alloc(f"xg{i}", [128, NCH * 520], BF16) for i in range(2)]
        qp, qp_b = sb.alloc("qp", [128, NCH * GT], BF16)
        kp, kp_b = sb.alloc("kp", [128, NCH * GT], BF16)
        vaug, vaug_b = sb.alloc("vaug", [128, 4 * 1048], BF16)
        ktm, ktm_b = sb.alloc("ktm", [128, 1024], BF16)
        atb = [sb.alloc(f"atb{i}", [128, 512], BF16) for i in range(4)]
        ndsb = [sb.alloc(f"ndsb{i}", [128, 512], BF16) for i in range(4)]
        att, att_b = sb.alloc("attg", [128, NCH * GT], BF16)
        xpt, xpt_b = sb.alloc("xppg", [128, NCH * GT], BF16)
        hgt, hgt_b = att, att_b
        rop = [sb.alloc(f"rop{i}", [128, GT], F32) for i in range(6)]
        gat = [sb.alloc(f"gat{i}", [128, GT], BF16) for i in range(2)]
        stt2, stt2_b = sb.alloc("stt2", [128, 1024], F32)
        stbf, stbf_b = sb.alloc("stbf", [128, 1024], BF16)
        sqbs = [sb.alloc(f"sqb{i}", [128, GT], BF16) for i in range(2)]
        gwt, gwt_b = sb.alloc("gwt", [128, 8], BF16)
        gtmp, gtmp_b = sb.alloc("gtmp", [4, 128], BF16)

        for ci in range(NCH):
            S.dma("g", wmg[:, ci * 4:(ci + 1) * 4],
                  self.d["wmgT"][ci * 128:(ci + 1) * 128, :],
                  writes=[wmg_b], st="wm2")

        for k4 in range(4):
            for hh in range(NH):
                o = k4 * 1048 + hh * 65 + 64
                self.memset("v", vaug[:, o:o + 1], 1.0, [vaug_b])
        self.memset("v", S_sb[:], 0.0, [S_sb_b])

        pend = []
        for g in range(NG):
            t0 = g * GT
            xgt, xgt_b = xg[g % 2]
            rc, rc_b = rcg[g % 2]
            rs, rs_b = rsg[g % 2]
            S.dma("g", rc[:], self.d["ropeC"][:, t0:t0 + GT], writes=[rc_b],
                  st="rope%d" % (g % 2))
            S.dma("g", rs[:], self.d["ropeS"][:, t0:t0 + GT], writes=[rs_b],
                  st="rope%d" % (g % 2))
            for c in range(NCH):
                base = c * 520
                r = slice(c * 128, (c + 1) * 128)
                if g == 0:
                    S.dma("g", xgt[:, base + 1: base + 513],
                          self.d["xp_sc"][r, 0:GT], reads=[self.db("xp_sc")],
                          writes=[xgt_b], st="xg0")
                    self.cp("v", xgt[:, base:base + 1],
                            xgt[:, base + 1:base + 2], [xgt_b], [xgt_b])
                else:
                    S.dma("g", xgt[:, base: base + 513],
                          self.d["xp_sc"][r, t0 - 1:t0 + GT],
                          reads=[self.db("xp_sc")], writes=[xgt_b],
                          st="xg%d" % (g % 2))
            if g == 0:
                for ci in range(NCH):
                    cd = slice(ci * D, (ci + 1) * D)
                    r = slice(ci * 128, (ci + 1) * 128)
                    S.dma("g", wo[:, cd], self.d["woT"][r, :], writes=[wo_b],
                          st="wm2")
                    S.dma("g", wg[:, cd], self.d["wgT"][r, :], writes=[wg_b],
                          st="wm2")

            for wi, (wb_, wbuf, dst, dstb) in enumerate(
                    ((wq, wq_b, qp, qp_b), (wk, wk_b, kp, kp_b))):
                for co in range(NCH):
                    t6 = (wi * NCH + co) % 2
                    c0, c0b = rop[3 * t6]
                    r0, r0b = rop[3 * t6 + 1]
                    r1, r1b = rop[3 * t6 + 2]
                    pb, pbb = P["B%d" % (co % 4)]
                    pairs = [(wb_[:, ci * D + co * 128: ci * D + (co + 1) * 128],
                              xgt[:, ci * 520 + 1: ci * 520 + 513])
                             for ci in range(NCH)]
                    self.mm(pb[:], pairs, pbb, [wbuf, xgt_b])
                    self.cp("v", c0[:], pb[:], [pbb], [c0b])
                    Cs = rc[:]
                    Ss = rs[:]
                    self.shuf(r0[:], c0[:], [c0b], [r0b])
                    self.tt("g", AL.mult, r1[:], c0[:], Cs, [c0b, rc_b], [r1b])
                    self.tt("g", AL.mult, r0[:], r0[:], Ss, [r0b, rs_b], [r0b])
                    self.tt("v", AL.add, r1[:], r1[:], r0[:], [r0b, r1b], [r1b])
                    self.act(r0[:], r1[:], AF.Exp, [r1b], [r0b])
                    self.S.emit("v", lambda E, o=r0[:], i=r0[:]:
                                E.tensor_scalar_min(o, i, 1.0),
                                reads=[r0b], writes=[r0b])
                    self.stt("v", dst[:, co * GT:(co + 1) * GT], r1[:], 1.0,
                             r0[:], AL.add, AL.max, [r0b, r1b], [dstb])

            for fn in pend:
                fn()
            pend.clear()
            for k4 in range(4):
                for hf in range(2):
                    pb, pbb = P["B%d" % (3 - hf)]
                    pairs = [(xgt[:, ci * 520 + 1 + k4 * 128:
                                  ci * 520 + 1 + (k4 + 1) * 128],
                              wv[:, ci * D + hf * 512: ci * D + (hf + 1) * 512])
                             for ci in range(NCH)]
                    self.mm(pb[:], pairs, pbb, [wv_b, xgt_b])
                    ov = vaug[:, k4 * 1048 + hf * 8 * 65: k4 * 1048 + hf * 8 * 65 + 8 * 65] \
                        .rearrange("p (h y) -> p h y", y=65)[:, :, 0:64]
                    iv = pb[:].rearrange("p (h y) -> p h y", y=64)
                    self.acp(ov, iv, [pbb], [vaug_b])

            for k4 in range(4):
                kk = g * 4 + k4
                vb = k4 * 1048
                ptr, ptrb = P["T%d" % (kk % 2)]
                for c in range(NCH):
                    self.tr(ptr[:, c * 128:(c + 1) * 128],
                            kp[:, c * GT + k4 * 128: c * GT + (k4 + 1) * 128],
                            idb[:], ptrb, [kp_b, idb_b])
                self.acp(ktm[:], ptr[:], [ptrb], [ktm_b])
                waves = []
                for w in range(4):
                    ws = [w * 4 + hh for hh in range(4)]
                    tsls = [slice((h // 2) * GT + k4 * 128,
                                  (h // 2) * GT + (k4 + 1) * 128) for h in ws]
                    p0s = [(h % 2) * 64 for h in ws]
                    waves.append((ws, tsls, p0s))
                for w, (ws, tsls, p0s) in enumerate(waves):
                    atp, atpb = P["B%d" % w]
                    for hh in range(4):
                        self.mm(atp[:, hh * 128:(hh + 1) * 128],
                                [(kp[p0s[hh]:p0s[hh] + 64, tsls[hh]],
                                  qp[p0s[hh]:p0s[hh] + 64, tsls[hh]])],
                                atpb, [qp_b, kp_b], skip=True)
                    ab4, ab4b = atb[w]
                    self.tt("v", AL.mult, ab4[:], atp[:], utm[:],
                            [atpb, utm_b], [ab4b])
                for w, (ws, tsls, p0s) in enumerate(waves):
                    sp, spb = P["S%d" % (w // 2)]
                    for hh in range(4):
                        h = ws[hh]
                        self.mm(sp[p0s[hh]:p0s[hh] + 64,
                                   ((h // 2) % 4) * 65:((h // 2) % 4) * 65 + 65],
                                [(ktm[:, h * 64:(h + 1) * 64],
                                  vaug[:, vb + h * 65: vb + h * 65 + 65])],
                                spb, [ktm_b, vaug_b], start=(kk == 0),
                                stop=True, skip=True)
                kon, kon_b = self.c["kon"]
                for w, (ws, tsls, p0s) in enumerate(waves):
                    nd, ndb = P["B%d" % w]
                    ab4, ab4b = atb[w]
                    for hh in range(4):
                        h = ws[hh]
                        col = hh * 128
                        self.mm(nd[0:65, col:col + 128],
                                [(vaug[:, vb + h * 65: vb + h * 65 + 65],
                                  ab4[:, col:col + 128])],
                                ndb, [vaug_b, ab4b], start=True, stop=False,
                                skip=True)
                        self.mm(nd[0:65, col:col + 128],
                                [(S_sb[p0s[hh]:p0s[hh] + 64,
                                       (h // 2) * 65:(h // 2) * 65 + 65],
                                  qp[p0s[hh]:p0s[hh] + 64, tsls[hh]])],
                                ndb, [S_sb_b, qp_b], start=False, stop=True,
                                skip=True)
                    nsb, nsbb = ndsb[w]
                    self.acp(nsb[0:64, :], nd[0:64, :], [ndb], [nsbb])
                    self.act(nsb[64:65, :], nd[64:65, :], AF.Ln,
                             [ndb, kon_b], [nsbb], bias=kon[64:65, 3:4])
                    self.act(nsb[64:65, :], nsb[64:65, :], AF.Exp,
                             [nsbb, kon_b], [nsbb], scale=kon[64:65, 2:3])
                for w, (ws, tsls, p0s) in enumerate(waves):
                    nd, ndb = P["B%d" % w]
                    nsb, nsbb = ndsb[w]
                    self.mm(nd[0:64, :], [(onb[64:65, 0:64], nsb[64:65, :])],
                            ndb, [onb_b, nsbb])
                    for hh in range(4):
                        h = ws[hh]
                        col = hh * 128
                        self.tt("v", AL.mult,
                                att[p0s[hh]:p0s[hh] + 64, tsls[hh]],
                                nsb[0:64, col:col + 128], nd[0:64, col:col + 128],
                                [nsbb, ndb], [att_b])
                for si in range(2):
                    sp, spb = P["S%d" % si]
                    self.acp(S_sb[:, si * 260:(si + 1) * 260], sp[:, 0:260],
                             [spb], [S_sb_b])
            delg, delg_b = qp, qp_b
            for c in range(NCH):
                self.tt("g" if c % 2 else "v", AL.subtract,
                        delg[:, c * GT:(c + 1) * GT],
                        xgt[:, c * 520 + 1: c * 520 + 513],
                        xgt[:, c * 520: c * 520 + 512], [xgt_b], [delg_b])
            for co in range(NCH):
                po, pob = P["B%d" % (co % 2)]
                pairs = [(wo[:, ci * D + co * 128: ci * D + (co + 1) * 128],
                          att[:, ci * GT:(ci + 1) * GT]) for ci in range(NCH)]
                self.mm(po[:], pairs, pob, [wo_b, att_b])
                pg, pgb = P["B%d" % (2 + co % 2)]
                pairs = [(wg[:, ci * D + co * 128: ci * D + (co + 1) * 128],
                          delg[:, ci * GT:(ci + 1) * GT]) for ci in range(NCH)]
                self.mm(pg[:], pairs, pgb, [wg_b, delg_b])
                ga, gab = gat[co % 2]
                self.act(ga[:], pg[:], AF.Sigmoid, [pgb, bg_n_b], [gab],
                         bias=bg_n[:, co:co + 1])
                self.tt("v", AL.mult, ga[:], ga[:], po[:], [gab, pob], [gab])
                self.tt("g", AL.add, xpt[:, co * GT:(co + 1) * GT], ga[:],
                        xgt[:, co * 520 + 1: co * 520 + 513], [gab, xgt_b],
                        [xpt_b])
                S.dma("sy", self.d["xpp_sc"][co * 128:(co + 1) * 128, t0:t0 + GT],
                      xpt[:, co * GT:(co + 1) * GT], reads=[xpt_b],
                      writes=[self.db("xpp_sc")], st="xppw")
            pA, pAb = P["B0"]
            pB, pBb = P["B1"]
            for c in range(NCH):
                sqb, sqb_b = sqbs[c % 2]
                self.act(sqb[:], xpt[:, c * GT:(c + 1) * GT], AF.Square,
                         [xpt_b], [sqb_b])
                self.mm(pB[0:1, :], [(onb[:, 0:1], sqb[:])], pBb,
                        [onb_b, sqb_b], start=(c == 0), stop=(c == NCH - 1),
                        skip=True)
            pairs = [(onb[:, 0:1], xpt[:, c * GT:(c + 1) * GT])
                     for c in range(NCH)]
            self.mm(pA[0:1, :], pairs, pAb, [onb_b, xpt_b])
            self.ts("v", stt2[0:1, 0:512], pA[0:1, :], 1.0 / D, None, AL.mult,
                    AL.bypass, [pAb], [stt2_b])
            self.ts("v", stt2[0:1, 512:1024], pB[0:1, :], 1.0 / D, None,
                    AL.mult, AL.bypass, [pBb], [stt2_b])
            self.tt("v", AL.mult, sqbs[0][0][0:1, 0:512], stt2[0:1, 0:512],
                    stt2[0:1, 0:512], [stt2_b], [sqbs[0][1]])
            self.tt("v", AL.subtract, stt2[0:1, 512:1024], stt2[0:1, 512:1024],
                    sqbs[0][0][0:1, 0:512], [stt2_b, sqbs[0][1]], [stt2_b])
            kon, kon_b = self.c["kon"]
            self.act(stt2[0:1, 512:1024], stt2[0:1, 512:1024], AF.Ln,
                     [stt2_b, kon_b], [stt2_b], bias=kon[0:1, 0:1])
            self.act(stt2[0:1, 512:1024], stt2[0:1, 512:1024], AF.Exp,
                     [stt2_b, kon_b], [stt2_b], scale=kon[0:1, 1:2])
            self.cp("v", stbf[0:1, 0:512], stt2[0:1, 0:512], [stt2_b], [stbf_b])
            self.cp("v", stbf[0:1, 512:1024], stt2[0:1, 512:1024], [stt2_b],
                    [stbf_b])
            TA, TAb = P["T0"][0][:].bitcast(F32), P["T0"][1]
            TB, TBb = P["T1"][0][:].bitcast(F32), P["T1"][1]
            self.mm(TA[:, 0:512], [(onb[0:1, 0:128], stbf[0:1, 0:512])], TAb,
                    [onb_b, stbf_b])
            self.mm(TB[:, 0:512], [(onb[0:1, 0:128], stbf[0:1, 512:1024])], TBb,
                    [onb_b, stbf_b])
            for c in range(NCH):
                r0, r0b = rop[c % 2]
                self.tt("v", AL.subtract, r0[:], xpt[:, c * GT:(c + 1) * GT],
                        TA[:, 0:512], [xpt_b, TAb], [r0b])
                self.tt("v", AL.mult, r0[:], r0[:], TB[:, 0:512], [r0b, TBb], [r0b])
                self.ts("v", hgt[:, c * GT:(c + 1) * GT], r0[:],
                        lngc[:, c:c + 1], lnbc[:, c:c + 1], AL.mult, AL.add,
                        [r0b, lngc_b, lnbc_b], [hgt_b])
                S.dma("sy", self.d["h_sc"][c * 128:(c + 1) * 128, t0:t0 + GT],
                      hgt[:, c * GT:(c + 1) * GT], reads=[hgt_b],
                      writes=[self.db("h_sc")], st="hw")
            pend.append(lambda t0=t0: self._moe_gate(
                t0, hgt, hgt_b, wmg, wmg_b, stt2, stt2_b, gwt, gwt_b,
                gtmp, gtmp_b, idb, idb_b))
        for fn in pend:
            fn()
        pend.clear()

    def _moe_gate(self, t0, hgt, hgt_b, wmg, wmg_b, stt2, stt2_b,
                  gwt, gwt_b, gtmp, gtmp_b, idb, idb_b):
        S, P = self.S, self.P
        for ts4 in range(4):
            o = 16 * ts4
            pb, pbb = P["B%d" % (2 + ts4 % 2)]
            pairs = [(hgt[:, ci * GT + ts4 * 128: ci * GT + (ts4 + 1) * 128],
                      wmg[:, ci * 4:(ci + 1) * 4]) for ci in range(NCH)]
            self.mm(pb[:, 0:4], pairs, pbb, [hgt_b, wmg_b])
            self.red(stt2[:, o:o + 1], pb[:, 0:4], AL.max, [pbb], [stt2_b])
            self.ts("v", stt2[:, o + 4:o + 8], pb[:, 0:4], stt2[:, o:o + 1],
                    None, AL.subtract, AL.bypass, [pbb, stt2_b], [stt2_b])
            self.act(stt2[:, o + 8:o + 12], stt2[:, o + 4:o + 8], AF.Exp,
                     [stt2_b], [stt2_b])
            self.red(stt2[:, o + 1:o + 2], stt2[:, o + 8:o + 12], AL.add,
                     [stt2_b], [stt2_b])
            self.recip(stt2[:, o + 2:o + 3], stt2[:, o + 1:o + 2], [stt2_b],
                       [stt2_b])
            self.ts("v", gwt[:, 4 * (ts4 % 2): 4 * (ts4 % 2) + 4],
                    stt2[:, o + 8:o + 12], stt2[:, o + 2:o + 3], None,
                    AL.mult, AL.bypass, [stt2_b], [gwt_b])
            pt, ptb = P["T%d" % (ts4 % 2)]
            self.tr(pt[0:4, 0:128], gwt[:, 4 * (ts4 % 2): 4 * (ts4 % 2) + 4],
                    idb[:], ptb, [gwt_b, idb_b])
            self.cp("v", gtmp[:], pt[0:4, 0:128], [ptb], [gtmp_b])
            S.dma("sy", self.d["gw_sc"][0:4, t0 + ts4 * 128: t0 + (ts4 + 1) * 128],
                  gtmp[:], reads=[gtmp_b], writes=[self.db("gw_sc")], st="gww")

    def phase_m3(self, mark):
        S, sb, P, T = self.S, self.sb, self.P, self.T
        T2 = T // 2
        NT2 = T2 // GT
        onb, onb_b = self.c["onb"]
        b1c, b1c_b = self.c["b1c"]
        b2r, b2r_b = self.c["b2r"]
        sb.reset(mark)
        hh_, hh_b = sb.alloc("hhalf", [128, NCH * T2], BF16)
        acc, acc_b = sb.alloc("m3acc", [128, NCH * T2], F32)
        hidG, hidG_b = sb.alloc("hidG", [128, 8 * T2], BF16)
        w1b_, w1b_bb = sb.alloc("w1b", [128, 8 * D], BF16)
        w2b, w2b_b = sb.alloc("w2b", [128, 8 * D], BF16)
        gwb, gwb_b = sb.alloc("gwb", [128, T2], BF16)
        gw4, gw4_b = sb.alloc("gw4", [4, T2], BF16)
        gwr, gwr_b = sb.alloc("gwr", [1, 512], BF16)
        sil = [sb.alloc(f"sil{i}", [128, 512], F32) for i in range(2)]
        xts, xts_b = sb.alloc("xts", [128, 512], BF16)
        sts, sts_b = sb.alloc("sts", [128, 512], BF16)

        for hf in range(2):
            toff = hf * T2
            for c in range(NCH):
                S.dma("g", hh_[:, c * T2:(c + 1) * T2],
                      self.d["h_sc"][c * 128:(c + 1) * 128, toff:toff + T2],
                      reads=[self.db("h_sc")], writes=[hh_b], st="hr")
            S.dma("sy", gw4[:], self.d["gw_sc"][0:4, toff:toff + T2],
                  reads=[self.db("gw_sc")], writes=[gw4_b], st="gwr")
            for e in range(4):
                for tt in range(NT2):
                    S.dma("sy", gwr[:], self.d["gw_sc"][e, toff + tt * GT:
                                                        toff + (tt + 1) * GT]
                          .unsqueeze(0), reads=[self.db("gw_sc")], writes=[gwr_b], st="gwrow")
                    pb, pbb = P["B3"]
                    self.mm(pb[:], [(onb[0:1, 0:128], gwr[:])], pbb,
                            [onb_b, gwr_b])
                    self.cp("v", gwb[:, tt * GT:(tt + 1) * GT], pb[:], [pbb],
                            [gwb_b])
                T0f = P["T0"][0][:].bitcast(F32)
                T1f = P["T1"][0][:].bitcast(F32)
                psW = [(P["S0"][0][:], P["S0"][1]), (P["S1"][0][:], P["S1"][1]),
                       (T0f, P["T0"][1]), (T1f, P["T1"][1])]
                for jg in range(4):
                    wb_, wb_b = w1b_, w1b_bb
                    f0 = jg * 8
                    for ci in range(NCH):
                        S.dma("g", wb_[:, ci * D:(ci + 1) * D],
                              self.d["w1T"][e, ci * 128:(ci + 1) * 128,
                                            f0 * 128:(f0 + 8) * 128],
                              writes=[wb_b], st="w1")
                    for j in range(8):
                        for tt in range(NT2):
                            pb, pbb = P["B%d" % tt]
                            pairs = [(wb_[:, ci * D + j * 128: ci * D + (j + 1) * 128],
                                      hh_[:, ci * T2 + tt * GT: ci * T2 + (tt + 1) * GT])
                                     for ci in range(NCH)]
                            self.mm(pb[:], pairs, pbb, [wb_b, hh_b])
                            sl, sl_b = sil[tt % 2]
                            self.act(sl[:], pb[:], AF.Silu, [pbb, b1c_b],
                                     [sl_b], bias=b1c[:, e * 32 + f0 + j:
                                                      e * 32 + f0 + j + 1])
                            self.tt("v", AL.mult,
                                    hidG[:, j * T2 + tt * GT: j * T2 + (tt + 1) * GT],
                                    sl[:], gwb[:, tt * GT:(tt + 1) * GT],
                                    [sl_b, gwb_b], [hidG_b])
                    for j in range(8):
                        S.dma("g", w2b[:, j * D:(j + 1) * D],
                              self.d["w2T"][e, (f0 + j) * 128:(f0 + j + 1) * 128, :],
                              writes=[w2b_b], st="w2")
                    for co in range(NCH):
                        bias_here = (e == 0 and jg == 0)
                        for tt in range(NT2):
                            pairs = [(w2b[:, j * D + co * 128: j * D + (co + 1) * 128],
                                      hidG[:, j * T2 + tt * GT: j * T2 + (tt + 1) * GT])
                                     for j in range(8)]
                            rb = [w2b_b, hidG_b]
                            if bias_here:
                                pairs.append((b2r[0:4, co * 128:(co + 1) * 128],
                                              gw4[0:4, tt * GT:(tt + 1) * GT]))
                                rb += [b2r_b, gw4_b]
                            self.mm(psW[tt][0], pairs, psW[tt][1], rb)
                            asl = acc[:, co * T2 + tt * GT: co * T2 + (tt + 1) * GT]
                            if bias_here:
                                self.cp("v", asl, psW[tt][0], [psW[tt][1]],
                                        [acc_b])
                            else:
                                self.tt("v", AL.add, asl, asl, psW[tt][0],
                                        [acc_b, psW[tt][1]], [acc_b])
                            if e == 3 and jg == 3:
                                r = slice(co * 128, (co + 1) * 128)
                                tsl = slice(toff + tt * GT, toff + (tt + 1) * GT)
                                S.dma("g", xts[:], self.d["xpp_sc"][r, tsl],
                                      reads=[self.db("xpp_sc")], writes=[xts_b],
                                      st="xppr")
                                self.tt("v", AL.add, sts[:], asl, xts[:],
                                        [acc_b, xts_b], [sts_b])
                                S.dma("sy", self.d["s_sc"][r, tsl], sts[:],
                                      reads=[sts_b], writes=[self.db("s_sc")],
                                      st="sw")


    def phase_ffn(self, mark):
        S, sb, P, T = self.S, self.sb, self.P, self.T
        NT = T // GT
        onb, onb_b = self.c["onb"]
        idb, idb_b = self.c["idb"]
        idf, idf_b = self.c["idf"]
        bd1c, bd1c_b = self.c["bd1c"]
        bd2r, bd2r_b = self.c["bd2r"]
        bbc, bbc_b = self.c["bbc"]
        badr, badr_b = self.c["badr"]
        baur, baur_b = self.c["baur"]
        gbias, gbias_b = self.c["gbias"]
        selv, selv_b = self.c["selv"]
        bauc, bauc_b = self.c["bauc"]
        sb.reset(mark)
        xnb, xnb_b = sb.alloc("xnb", [128, NCH * T], BF16)
        wbase, wbase_b = sb.alloc("wbase", [128, NCH * D], BF16)
        wd1b = [sb.alloc(f"wd1b{i}", [128, NCH * 256], BF16) for i in range(2)]
        wd2b = [sb.alloc(f"wd2b{i}", [128, 2 * D], BF16) for i in range(2)]
        hid2, hid2_b = sb.alloc("hid2", [128, 2 * GT], BF16)
        jkfs = [sb.alloc(f"jkf{i}", [128, GT], F32) for i in range(2)]
        wadeb, wadeb_b = sb.alloc("wadeb", [128, NCH * 64], BF16)
        waueb, waueb_b = sb.alloc("waueb", [64, D], BF16)
        af, af_b = sb.alloc("af", [64, T], BF16)
        gbrow, gbrow_b = sb.alloc("gbrow", [1, D], BF16)
        ohc, ohc_b = self.c["ohc"]
        sel_mark = sb.mark()
        lrx, lrx_b = sb.alloc("lrx", [128, NT * NCH], F32)
        lrr, lrr_b = sb.alloc("lrr", [128, NT * NCH], F32)
        LRX, LRX_b = sb.alloc("LRX", [128, 8], F32)
        LRR, LRR_b = sb.alloc("LRR", [128, 8], F32)
        LL, LL_b = sb.alloc("LL", [128, 8], F32)
        llt, llt_b = sb.alloc("llt", [8, 128], F32)
        lc8, lc8_b = sb.alloc("lc8", [8, 4], F32)
        lrw, lrw_b = sb.alloc("lrw", [1, 16], F32)
        ohr, ohr_b = sb.alloc("ohr", [1, 8], BF16)
        selb = [sb.alloc(f"selb{i}", [8, 4096], BF16) for i in range(2)]
        web, web_b = sb.alloc("web", [1, 4096], BF16)

        for c in range(NCH):
            S.dma("g", xnb[:, c * T:(c + 1) * T],
                  self.d["xn_sc"][c * 128:(c + 1) * 128, :],
                  reads=[self.db("xn_sc")], writes=[xnb_b], st="xnr")
            S.dma("g", wbase[:, c * D:(c + 1) * D],
                  self.d["Wbase"][c * 128:(c + 1) * 128, :], writes=[wbase_b],
                  st="wb")

        for e in range(8):
            wd1, wd1_b = wd1b[e % 2]
            wd2, wd2_b = wd2b[e % 2]
            for ci in range(NCH):
                S.dma("g", wd1[:, ci * 256:(ci + 1) * 256],
                      self.d["Wd1"][e, ci * 128:(ci + 1) * 128, :],
                      writes=[wd1_b], st="wd%d" % (e % 2))
            for f in range(2):
                S.dma("g", wd2[:, f * D:(f + 1) * D],
                      self.d["Wd2"][e, f * 128:(f + 1) * 128, :], writes=[wd2_b],
                      st="wd%d" % (e % 2))
            for tt in range(NT):
                tsl = slice(tt * GT, (tt + 1) * GT)
                for f in range(2):
                    pb, pbb = P["B%d" % f]
                    pairs = [(wd1[:, ci * 256 + f * 128: ci * 256 + (f + 1) * 128],
                              xnb[:, ci * T + tt * GT: ci * T + (tt + 1) * GT])
                             for ci in range(NCH)]
                    self.mm(pb[:], pairs, pbb, [wd1_b, xnb_b])
                    self.act(hid2[:, f * GT:(f + 1) * GT], pb[:], AF.Relu,
                             [pbb, bd1c_b], [hid2_b],
                             bias=bd1c[:, e * 2 + f: e * 2 + f + 1])
                for co in range(NCH):
                    pb, pbb = P["B%d" % (2 + co % 2)]
                    pairs = [(wd2[:, f * D + co * 128: f * D + (co + 1) * 128],
                              hid2[:, f * GT:(f + 1) * GT]) for f in range(2)]
                    pairs.append((bd2r[0:1, e * D + co * 128: e * D + (co + 1) * 128],
                                  onb[0:1, :]))
                    self.mm(pb[:], pairs, pbb, [wd2_b, hid2_b, bd2r_b, onb_b])
                    idx = tt * NCH + co
                    jkf, jkf_b = jkfs[co % 2]
                    self.stt("v", jkf[:], pb[:], 1.0,
                             xnb[:, co * T + tt * GT: co * T + (tt + 1) * GT],
                             AL.mult, AL.mult, [pbb, xnb_b], [jkf_b, lrx_b],
                             accum=lrx[:, idx:idx + 1])
                    self.act(jkf[:], pb[:], AF.Square, [pbb], [jkf_b, lrr_b],
                             accum=lrr[:, idx:idx + 1])
            self.red(LRX[:, e:e + 1], lrx[:], AL.add, [lrx_b], [LRX_b])
            self.red(LRR[:, e:e + 1], lrr[:], AL.add, [lrr_b], [LRR_b])
        self.stt("v", LL[:], LRX[:], -2.0, LRR[:], AL.mult, AL.add,
                 [LRX_b, LRR_b], [LL_b])
        pf, pfb = P["B2"]
        self.tr(pf[0:8, 0:128], LL[:], idf[:], pfb, [LL_b, idf_b])
        self.cp("v", llt[:], pf[0:8, 0:128], [pfb], [llt_b])
        self.red(lc8[:, 0:1], llt[:], AL.add, [llt_b], [lc8_b])
        S.dma("sy", self.d["lrow_sc"], lc8[0:8, 0:1], reads=[lc8_b],
              writes=[self.db("lrow_sc")], st="selm")
        S.dma("sy", lrw[0:1, 0:8], self.d["lrow_sc"].unsqueeze(0),
              reads=[self.db("lrow_sc")], writes=[lrw_b], st="selm")
        self.red(lrw[0:1, 8:9], lrw[0:1, 0:8], AL.min, [lrw_b], [lrw_b])
        self.ts("v", ohr[:], lrw[0:1, 0:8], lrw[0:1, 8:9], None, AL.is_le,
                AL.bypass, [lrw_b], [ohr_b])
        S.dma("sy", self.d["oh_sc"], ohr[0:1, 0:8], reads=[ohr_b],
              writes=[self.db("oh_sc")], st="selm")
        S.dma("sy", ohc[0:8, 0:1], self.d["oh_sc"].unsqueeze(1),
              reads=[self.db("oh_sc")], writes=[ohc_b], st="selm")
        for (src, dst, dstb) in ((self.d["WadF"], self.d["wad_sc"],
                                  self.db("wad_sc")),
                                 (self.d["WauF"], self.d["wau_sc"],
                                  self.db("wau_sc"))):
            for ch in range(16):
                slb, slb_b = selb[ch % 2]
                S.dma("g", slb[:], src[:, ch * 4096:(ch + 1) * 4096],
                      writes=[slb_b], st="sel%d" % (ch % 2))
                for i in range(8):
                    pb, pbb = P["B%d" % (i % 4)]
                    self.mm(pb[0:1, :], [(ohc[0:8, 0:1],
                                          slb[0:8, i * 512:(i + 1) * 512])],
                            pbb, [ohc_b, slb_b])
                    self.cp("v", web[0:1, i * 512:(i + 1) * 512], pb[0:1, :],
                            [pbb], [web_b])
                S.dma("sy", dst[ch * 4096:(ch + 1) * 4096].unsqueeze(0),
                      web[:], reads=[web_b], writes=[dstb], st="selw")
        pb, pbb = P["B2"]
        self.mm(pb[0:64, 0:1], [(badr[0:8, :], ohc[0:8, 0:1])], pbb,
                [badr_b, ohc_b])
        self.cp("v", selv[0:64, 0:1], pb[0:64, 0:1], [pbb], [selv_b])
        for c in range(NCH):
            pb2, pb2b = P["B%d" % (3 - c % 2)]
            self.mm(pb2[:, 0:1], [(baur[0:8, c * 128:(c + 1) * 128],
                                   ohc[0:8, 0:1])], pb2b, [baur_b, ohc_b])
            self.cp("v", bauc[:, c:c + 1], pb2[:, 0:1], [pb2b], [bauc_b])
        self.tt("v", AL.add, gbias[:], bbc[:], bauc[:], [bbc_b, bauc_b],
                [gbias_b])
        pgb, pgbb = P["B3"]
        self.tr(pgb[0:8, 0:128], gbias[:, 0:8], idf[:], pgbb, [gbias_b, idf_b])
        gb8, gb8_b = sb.alloc("gb8", [8, 128], BF16)
        self.cp("v", gb8[:], pgb[0:8, 0:128], [pgbb], [gb8_b])
        S.dma("sy", self.da("gb_sc").rearrange("(c p) -> c p", p=128), gb8[:],
              reads=[gb8_b], writes=[self.db("gb_sc")], st="selw")
        S.dma("sy", gbrow[:], self.da("gb_sc").unsqueeze(0),
              reads=[self.db("gb_sc")], writes=[gbrow_b], st="selr")
        for c in range(NCH):
            S.dma("sy", wadeb[:, c * 64:(c + 1) * 64],
                  self.d["wad_sc"][c * 8192:(c + 1) * 8192]
                  .rearrange("(p r) -> p r", r=64),
                  reads=[self.db("wad_sc")], writes=[wadeb_b], st="selr")
        S.dma("sy", waueb[:], self.d["wau_sc"].rearrange("(p o) -> p o", o=D),
              reads=[self.db("wau_sc")], writes=[waueb_b], st="selr")
        S.barrier()
        sb.reset(sel_mark)
        sgs = [sb.alloc(f"sg{i}", [128, NCH * GT], BF16) for i in range(2)]
        stms = [sb.alloc(f"stm{i}", [128, 1024], BF16) for i in range(2)]
        gtms = [sb.alloc(f"gtm{i}", [128, 1024], BF16) for i in range(2)]
        prds = [sb.alloc(f"prd{i}", [128, 1024], F32) for i in range(2)]
        xtm2 = [sb.alloc(f"xtm2{i}", [128, D], F32) for i in range(4)]
        otm = [sb.alloc(f"otm{i}", [128, D], F32) for i in range(2)]
        for tt in range(NT):
            pb, pbb = P["B%d" % (tt % 2)]
            pairs = [(wadeb[:, ci * 64:(ci + 1) * 64],
                      xnb[:, ci * T + tt * GT: ci * T + (tt + 1) * GT])
                     for ci in range(NCH)]
            self.mm(pb[0:64, :], pairs, pbb, [wadeb_b, xnb_b])
            self.act(af[0:64, tt * GT:(tt + 1) * GT], pb[0:64, :], AF.Relu,
                     [pbb, selv_b], [af_b], bias=selv[0:64, 0:1])
        for tt in range(NT):
            t0 = tt * GT
            sg, sg_b = sgs[tt % 2]
            for c in range(NCH):
                S.dma("g", sg[:, c * GT:(c + 1) * GT],
                      self.d["s_sc"][c * 128:(c + 1) * 128, t0:t0 + GT],
                      reads=[self.db("s_sc")], writes=[sg_b], st="sr")
            for s4 in range(4):
                xt, xt_b = xtm2[s4]
                S.dma("g" if s4 % 2 == 0 else "sc", xt[:],
                      self.d["x"][t0 + s4 * 128: t0 + (s4 + 1) * 128, :],
                      writes=[xt_b], st="x2%d" % (s4 % 2))
            for s4 in range(4):
                ts0 = t0 + s4 * 128
                gtm, gtm_b = gtms[s4 % 2]
                for hf in range(2):
                    pb, pbb = P["B%d" % ((s4 * 2 + hf) % 4)]
                    pairs = [(xnb[:, ci * T + ts0: ci * T + ts0 + 128],
                              wbase[:, ci * D + hf * 512: ci * D + hf * 512 + 512])
                             for ci in range(NCH)]
                    pairs.append((af[0:64, ts0:ts0 + 128],
                                  waueb[0:64, hf * 512: hf * 512 + 512]))
                    pairs.append((onb[0:1, 0:128],
                                  gbrow[0:1, hf * 512: hf * 512 + 512]))
                    self.mm(pb[:], pairs, pbb,
                            [wbase_b, xnb_b, waueb_b, af_b, onb_b, gbrow_b])
                    self.act(gtm[:, hf * 512: hf * 512 + 512], pb[:], AF.Silu,
                             [pbb], [gtm_b])
                pt, ptb = P["T%d" % (s4 % 2)]
                stm, stm_b = stms[s4 % 2]
                prd, prd_b = prds[s4 % 2]
                for c in range(NCH):
                    self.tr(pt[:, c * 128:(c + 1) * 128],
                            sg[:, c * GT + s4 * 128: c * GT + (s4 + 1) * 128],
                            idb[:], ptb, [sg_b, idb_b])
                self.acp(stm[:], pt[:], [ptb], [stm_b])
                xt, xt_b = xtm2[s4]
                ot, ot_b = otm[s4 % 2]
                self.tt("v", AL.mult, prd[:], stm[:], gtm[:],
                        [stm_b, gtm_b], [prd_b])
                self.tt("v", AL.add, ot[:], xt[:], prd[:], [xt_b, prd_b],
                        [ot_b])
                S.dma("sy", self.d["out"][t0 + s4 * 128: t0 + (s4 + 1) * 128, :],
                      ot[:], reads=[ot_b], writes=[self.db("out")], st="out")


def host_prep(inputs, T=None):
    gi = {k: np.asarray(v) for k, v in inputs.items()}
    x = gi["L0_attn_x"]
    B = x.shape[0]
    Tfull = x.shape[1]
    T = T or Tfull
    f32 = np.float32
    bf = ml_dtypes.bfloat16

    com = {}
    com["wdwp"] = np.ascontiguousarray(
        gi["L0_attn_w_dw"].astype(f32).reshape(8, 128, 3).transpose(1, 2, 0)
        .reshape(128, 24))
    com["wpwT"] = np.ascontiguousarray(gi["L0_attn_w_pw"].T.astype(f32))
    com["wqkvT"] = np.ascontiguousarray(gi["L0_attn_w_qkv"].T.astype(f32))
    com["woT"] = np.ascontiguousarray(gi["L0_attn_w_o"].T.astype(f32))
    com["wgT"] = np.ascontiguousarray(gi["L0_attn_w_gate"].T.astype(f32))
    com["wmgT"] = np.ascontiguousarray(gi["L0_attn_w_moe_gate"].T.astype(f32))
    com["w1T"] = np.ascontiguousarray(gi["L0_attn_w1"].transpose(0, 2, 1).astype(f32))
    com["w2T"] = np.ascontiguousarray(gi["L0_attn_w2"].transpose(0, 2, 1).astype(f32))
    com["b1p"] = np.ascontiguousarray(
        gi["L0_attn_b1"].astype(f32).reshape(4, 32, 128).transpose(2, 0, 1)
        .reshape(128, 128))
    com["b2"] = gi["L0_attn_b2"].astype(f32)
    com["bd1p"] = np.ascontiguousarray(
        gi["L0_ffn_bd1"].astype(f32).reshape(8, 2, 128).transpose(2, 0, 1)
        .reshape(128, 16))
    com["bvecs"] = np.ascontiguousarray(np.concatenate(
        [np.asarray(gi[k]).astype(f32).reshape(8, 128).T for k in
         ("L0_attn_b_gate", "L0_attn_ln_g", "L0_attn_ln_b", "L0_ffn_b_base")],
        axis=1))
    com["Wbase"] = np.ascontiguousarray(gi["L0_ffn_W_base"].astype(f32))
    com["Wd1"] = np.ascontiguousarray(gi["L0_ffn_Wd1"].astype(f32))
    com["Wd2"] = np.ascontiguousarray(gi["L0_ffn_Wd2"].astype(f32))
    com["bd2"] = gi["L0_ffn_bd2"].astype(f32)
    com["WadF"] = np.ascontiguousarray(gi["L0_ffn_Wa_down"].reshape(8, -1).astype(f32))
    com["bad"] = gi["L0_ffn_ba_down"].astype(f32)
    com["WauF"] = np.ascontiguousarray(gi["L0_ffn_Wa_up"].reshape(8, -1).astype(f32))
    com["bau"] = gi["L0_ffn_ba_up"].astype(f32)

    com["identb"] = np.eye(128, dtype=bf)
    com["identf"] = np.eye(128, dtype=f32)
    com["utmask"] = np.ascontiguousarray(
        np.tile(np.triu(np.ones((128, 128), np.float32)), (1, 4))).astype(bf)
    com["onesb"] = np.ones((128, 512), bf)
    inv = (1.0 / 10000.0 ** (np.arange(0, DH, 2, dtype=np.float64) / DH))
    pos = np.arange(T, dtype=np.float64)
    invf = inv.astype(np.float32).astype(np.float64)
    posf = pos.astype(np.float32).astype(np.float64)
    th = (posf[None, :] * invf[:, None]).astype(np.float32).astype(np.float64)
    sn = np.sin(th).astype(np.float32)
    cs = np.cos(th).astype(np.float32)
    C64 = np.repeat(cs, 2, axis=0)
    S64 = np.empty((64, T), np.float32)
    S64[0::2] = -sn
    S64[1::2] = sn
    com["ropeC"] = np.concatenate([C64, C64], 0).astype(bf)
    com["ropeS"] = np.concatenate([S64, S64], 0).astype(bf)

    in_maps = []
    for b in range(B):
        m = dict(com)
        m["x"] = np.ascontiguousarray(x[b, :T].astype(f32))
        in_maps.append(m)
    return in_maps


_CACHED = {}


def kernel(**inputs):
    T = int(np.asarray(inputs["L0_attn_x"]).shape[1])
    in_maps = host_prep(inputs, T)
    key = ("k", T)
    if key not in _CACHED:
        kb = KB(T)
        _CACHED[key] = kb.build()
    nc = _CACHED[key]
    res = run_bass_kernel_spmd(nc, in_maps, core_ids=list(range(8)))
    out = np.stack([np.asarray(r["out"]) for r in res.results], axis=0)
    return out.astype(np.float32)
